# version 15
# baseline (speedup 1.0000x reference)
from contextlib import ExitStack
import numpy as np
import ml_dtypes
import concourse.bass as bass
import concourse.mybir as mybir
from concourse.bass_utils import run_bass_kernel_spmd

F32 = mybir.dt.float32
BF16 = mybir.dt.bfloat16
AF = mybir.ActivationFunctionType
ALU = mybir.AluOpType
AX = mybir.AxisListType

NCORES = 8
B, T, D = 16, 2048, 1024
NSEQ = B // NCORES
NTOK = NSEQ * T
NTT = NTOK // 128
TPS = T // 128
NH, NKV, DH, WINDOW = 16, 4, 64, 512
FF = 4 * D
KX, VX = 128, 2048
NEG = -1e30
BF = ml_dtypes.bfloat16


def host_constants():
    c = {}
    theta = 1.0 / 10000.0 ** (np.arange(0, DH, 2, dtype=np.float64) / DH)
    pos = np.arange(T, dtype=np.float64)
    freqs = np.outer(pos, theta)
    cos = np.concatenate([np.cos(freqs), np.cos(freqs)], -1).astype(np.float32)
    sin = np.concatenate([np.sin(freqs), np.sin(freqs)], -1).astype(np.float32)
    sinA = np.concatenate([-sin[:, 0:32], sin[:, 32:64]], -1).astype(np.float32)
    c["cos16"] = np.ascontiguousarray(np.tile(cos, (1, NH)))
    c["sinA16"] = np.ascontiguousarray(np.tile(sinA, (1, NH)))
    c["ident_bf"] = np.eye(128, dtype=BF)
    c["ones_bf"] = np.ones((1, 128), dtype=BF)
    c["ones_f32"] = np.ones((1, 128), dtype=np.float32)
    c["onescol_f32"] = np.ones((128, 1), dtype=np.float32)
    p = np.arange(128)[:, None]
    cq = np.arange(128)[None, :]
    negstep_gt = np.where(p > cq, np.float32(NEG), np.float32(0)).astype(BF)
    negstep_le = np.where(p <= cq, np.float32(NEG), np.float32(0)).astype(BF)
    c["negstep_gt4"] = np.ascontiguousarray(np.tile(negstep_gt, (1, 4)))
    c["negstep_le4"] = np.ascontiguousarray(np.tile(negstep_le, (1, 4)))
    c["l1_negstep"] = np.where(cq > p, np.float32(NEG), np.float32(0)).astype(np.float32)
    LAM = 10000.0
    inv_lams = 1.0 / LAM ** (np.arange(0, D, 2, dtype=np.float64) / D)
    pre = np.arange(T, dtype=np.float64)[:, None] * inv_lams[None, :]
    cat = np.concatenate([np.sin(pre), np.cos(pre)], -1)
    c["catrevT"] = np.ascontiguousarray(cat[::-1].T.astype(np.float32))
    return c


CONST_SPECS = {
    "cos16": ([T, D], F32), "sinA16": ([T, D], F32),
    "ident_bf": ([128, 128], BF16), "ones_bf": ([1, 128], BF16),
    "ones_f32": ([1, 128], F32), "onescol_f32": ([128, 1], F32),
    "negstep_gt4": ([128, 512], BF16), "negstep_le4": ([128, 512], BF16),
    "l1_negstep": ([128, 128], F32),
    "catrevT": ([D, T], F32),
}

WEIGHT_SPECS = {
    "l0_WqT": ([D, D], F32),
    "l0_WkvT": ([D, 512], F32),
    "l0_WprojTh": ([64, 16 * D], F32),
    "l0_W1T": ([D, FF], F32),
    "l0_b1c": ([128, FF // 128], F32),
    "l0_W2T": ([FF, D], F32),
    "l0_b2": ([1, D], F32),
    "l1_gc": ([128, 8], F32),
    "l1_Wq": ([D, KX], F32),
    "l1_Wk": ([D, KX], F32),
    "l1_Wv": ([D, VX], F32),
    "l1_Wg": ([D, VX], F32),
    "l1_Wr": ([D, KX], F32),
    "l1_xlu": ([KX, 1], F32),
    "l1_xlv": ([KX, 1], F32),
    "l1_Wo": ([VX, D], F32),
    "l1f_W1": ([D, FF], F32),
    "l1f_b1c": ([128, FF // 128], F32),
    "l1f_W2": ([FF, D], F32),
    "l1f_b2": ([1, D], F32),
}


def shard_weights(i):
    w = {}
    w["l0_WqT"] = i["L0_attn_Wq"].T
    w["l0_WkvT"] = i["L0_attn_Wkv"].T
    w["l0_WprojTh"] = i["L0_attn_Wproj"].T.reshape(16, 64, D).transpose(1, 0, 2).reshape(64, 16 * D)
    w["l0_W1T"] = i["L0_ffn_W1"].T
    w["l0_b1c"] = i["L0_ffn_b1"].reshape(FF // 128, 128).T
    w["l0_W2T"] = i["L0_ffn_W2"].T
    w["l0_b2"] = i["L0_ffn_b2"].reshape(1, D)
    w["l1_gc"] = i["L1_attn_g_ln"].reshape(8, 128).T
    w["l1_Wq"] = i["L1_attn_W_q"]
    w["l1_Wk"] = i["L1_attn_W_k"]
    w["l1_Wv"] = i["L1_attn_W_v"]
    w["l1_Wg"] = i["L1_attn_W_g"]
    w["l1_Wr"] = i["L1_attn_W_r"]
    w["l1_xlu"] = i["L1_attn_xl_u"].reshape(KX, 1)
    w["l1_xlv"] = i["L1_attn_xl_v"].reshape(KX, 1)
    w["l1_Wo"] = i["L1_attn_W_o"]
    w["l1f_W1"] = i["L1_ffn_W1"]
    w["l1f_b1c"] = i["L1_ffn_b1"].reshape(FF // 128, 128).T
    w["l1f_W2"] = i["L1_ffn_W2"]
    w["l1f_b2"] = i["L1_ffn_b2"].reshape(1, D)
    return {k: np.ascontiguousarray(np.asarray(v, dtype=np.float32)) for k, v in w.items()}


ENGS = ("sync", "scalar", "vector", "tensor", "gpsimd")


class Ctr:
    def __init__(self, sem):
        self.sem = sem
        self.val = 0


class Plan:

    def __init__(self, chain=None):
        self.q = {e: [] for e in ENGS}
        self.chain = chain or {}

    def on(self, eng, fn):
        self.q[eng].append(fn)

    def wait(self, eng, ctr, val=None):
        v = int(ctr.val if val is None else val)
        if v > 0:
            self.q[eng].append(lambda e, s=ctr.sem, v=v: e.wait_ge(s, v))

    def _prewait(self, eng):
        c = self.chain.get(eng)
        if c is not None and c.val > 0:
            self.q[eng].append(lambda e, s=c.sem, v=c.val: e.wait_ge(s, v))

    def inc(self, eng, ctr, fn, n=1):
        self._prewait(eng)
        ctr.val += n
        tgt = ctr.val

        def go(e, fn=fn, s=ctr.sem, n=n):
            inst = fn(e)
            inst.then_inc(s, n)
        self.q[eng].append(go)
        return tgt

    def run(self, eng, fn):
        c = self.chain.get(eng)
        if c is None:
            self.q[eng].append(fn)
            return None
        return self.inc(eng, c, fn)

    def realize(self, nc):
        with nc.Block() as blk:
            for eng in ENGS:
                ops = self.q[eng]
                if not ops:
                    continue
                getattr(blk, eng)(lambda e, ops=ops: [f(e) for f in ops])


def emit_ln_tile(p, ctx, x_ap, out_ap, scr, scr2, stat, var_scale, var_bias,
                 war_act=0, war_pe=0):
    s_dve, s_act, s_pe = ctx["s_dve"], ctx["s_act"], ctx["s_pe"]
    v = "vector"
    s = "scalar"
    t0 = p.inc(v, s_dve, lambda e: e.reduce_sum(stat[:, 0:1], x_ap, axis=AX.X))
    p.wait(v, s_dve, t0)
    t1 = p.inc(v, s_dve, lambda e: e.tensor_scalar(stat[:, 1:2], stat[:, 0:1], 1.0 / D, None, ALU.mult))
    p.wait(v, s_act, war_act)
    p.wait(v, s_dve, t1)
    t_sub = p.inc(v, s_dve, lambda e: e.tensor_scalar(scr[:, :], x_ap, stat[:, 1:2], None, ALU.subtract))
    p.wait(s, s_dve, t_sub)
    t_sq = p.inc(s, s_act, lambda e: e.activation(scr2[:, :], scr[:, :], AF.Square,
                                                 accum_out=stat[:, 2:3]))
    p.wait(v, s_act, t_sq)
    t2 = p.inc(v, s_dve, lambda e: e.tensor_scalar(stat[:, 3:4], stat[:, 2:3], var_scale, var_bias,
                                                   ALU.mult, ALU.add))
    p.wait(v, s_dve, t2)
    t_rc = p.inc(v, s_dve, lambda e: e.reciprocal(stat[:, 4:5], stat[:, 3:4]))
    p.wait(s, s_dve, t_rc)
    t_sq2 = p.inc(s, s_act, lambda e: e.activation(stat[:, 5:6], stat[:, 4:5], AF.Sqrt))
    p.wait(v, s_act, t_sq2)
    p.wait(v, s_pe, war_pe)
    t_out = p.inc(v, s_dve, lambda e: e.tensor_scalar(out_ap, scr[:, :], stat[:, 5:6], None, ALU.mult))
    return t_out, t_sq


def x_slice_dram(x_ap, sl, ntiles=2):
    r0 = 128 * ntiles * sl
    return x_ap[r0:r0 + 128 * ntiles, :].rearrange("(a p) d -> p a d", p=128)


def sb_3d(sb_ap, ntiles=2):
    return sb_ap.rearrange("p (a d) -> p a d", a=ntiles)


def emit_ffn_phase(nc, ctx, *, x_src, x_dst, W1_sb, W2_sb, b1_sb, b2row_sb,
                   act_func, name, taps=None, nsl=None):
    s_x, s_pe, s_dve, s_act = ctx["s_x"], ctx["s_pe"], ctx["s_dve"], ctx["s_act"]
    s_l = ctx["s_l"]
    s_st = ctx["s_st"]
    ones_bf = ctx["ones_bf"]
    ident = ctx["ident_bf"]
    NSL = NTT // 2 if nsl is None else nsl

    with (
        nc.sbuf_tensor(f"{name}_x0", [128, 2 * D], F32) as xb0,
        nc.sbuf_tensor(f"{name}_x1", [128, 2 * D], F32) as xb1,
        nc.sbuf_tensor(f"{name}_scr", [128, D], F32) as scr,
        nc.sbuf_tensor(f"{name}_scr2", [128, D], F32) as scr2,
        nc.sbuf_tensor(f"{name}_stat", [128, 8], F32) as stat,
        nc.sbuf_tensor(f"{name}_lnbf0", [128, D], BF16) as lnbf0,
        nc.sbuf_tensor(f"{name}_lnbf1", [128, D], BF16) as lnbf1,
        nc.sbuf_tensor(f"{name}_lnfm0", [128, 8 * 256], BF16) as lnfm0,
        nc.sbuf_tensor(f"{name}_lnfm1", [128, 8 * 256], BF16) as lnfm1,
        nc.sbuf_tensor(f"{name}_h", [128, 32 * 256], BF16) as h_sb,
        nc.sbuf_tensor(f"{name}_out", [128, 2 * 2 * D], F32) as out_f32,
        nc.psum_tensor(f"{name}_pst", [128, 1024], BF16) as ps_t,
        nc.psum_tensor(f"{name}_ph0", [128, 256], F32) as ph0,
        nc.psum_tensor(f"{name}_ph1", [128, 256], F32) as ph1,
        nc.psum_tensor(f"{name}_po0", [128, 1024], F32) as po0,
        nc.psum_tensor(f"{name}_po1", [128, 1024], F32) as po1,
    ):
        p = Plan(chain={"vector": ctx["s_dve"], "scalar": ctx["s_act"], "gpsimd": ctx["s_pool"]})
        ph = [ph0, ph1]
        po = [po0, po1]
        xb = [xb0, xb1]
        lnbf = [lnbf0, lnbf1]
        lnfm = [lnfm0, lnfm1]

        x_loaded = [0] * NSL
        out_evac = [[0, 0] for _ in range(NSL)]
        stored = [0] * NSL
        h_evac = {}
        lnfm_ready = [0] * NSL
        o_used = [[0, 0] for _ in range(NSL)]
        pst_free = 0
        ln_war_act = 0
        w_tr = [0, 0]
        w_mm1 = [0, 0]
        t_ln_sl = [0] * (NSL + 1)

        def emit_load(sl):
            if sl >= NSL:
                return
            if sl >= 2:
                p.wait("sync", s_dve, out_evac[sl - 2][1])
                p.wait("sync", s_l[sl % 2], x_loaded[sl - 2])
            x_loaded[sl] = p.inc("sync", s_l[sl % 2],
                                 lambda e, sl=sl: e.dma_start(sb_3d(xb[sl % 2].ap()), x_slice_dram(x_src, sl)), 16)
        emit_load(0)
        emit_load(1)

        def stage1_ln(sl):
            nonlocal ln_war_act
            b = sl % 2
            for tt in range(2):
                p.wait("vector", s_l[b], x_loaded[sl])
                t_ln, t_sq = emit_ln_tile(p, ctx, xb[b][:, D * tt:D * (tt + 1)],
                                          lnbf[b][:, :] if tt == 1 else lnbf[b][:, :],
                                          scr, scr2, stat, 1.0 / D, 1e-5,
                                          war_act=ln_war_act, war_pe=w_tr[b])
                ln_war_act = t_sq
                yield t_ln, tt

        def stage1(sl):
            nonlocal pst_free
            b = sl % 2
            for t_ln, tt in stage1_ln(sl):
                p.wait("tensor", s_dve, t_ln)
                p.wait("tensor", s_dve, pst_free)
                for c in range(8):
                    fn = (lambda e, c=c, b=b: e.transpose(ps_t[:, 128 * c:128 * (c + 1)],
                                                          lnbf[b][:, 128 * c:128 * (c + 1)], ident[:, :]))
                    if c < 7:
                        p.run("tensor", fn)
                    else:
                        t_tr = p.inc("tensor", s_pe, fn)
                w_tr[b] = t_tr
                p.wait("vector", s_pe, t_tr)
                p.wait("vector", s_pe, w_mm1[b])
                dst = lnfm[b].ap().rearrange("p (c a) -> p c a", c=8, a=256)[:, :, 128 * tt:128 * (tt + 1)]
                t_ev = p.inc("vector", s_dve,
                             lambda e, dst=dst: e.tensor_copy(dst, ps_t.ap().rearrange("p (c a) -> p c a", c=8, a=128)))
                pst_free = t_ev
            lnfm_ready[sl] = pst_free
            t_ln_sl[sl] = pst_free

        def stage2(sl):
            b = sl % 2
            p.wait("tensor", s_dve, lnfm_ready[sl])
            for fc in range(32):
                if (sl, fc - 2) in h_evac:
                    p.wait("tensor", s_act, h_evac[(sl, fc - 2)])
                elif sl > 0:
                    p.wait("tensor", s_act, h_evac[(sl - 1, 30 + fc)])
                for dc in range(8):
                    fn = (lambda e, fc=fc, dc=dc, b=b:
                          e.matmul(ph[fc % 2][:, :],
                                   W1_sb[:, FF * dc + 128 * fc:FF * dc + 128 * (fc + 1)],
                                   lnfm[b][:, 256 * dc:256 * (dc + 1)],
                                   start=(dc == 0), stop=(dc == 7)))
                    if dc < 7:
                        p.run("tensor", fn)
                    else:
                        t_mm = p.inc("tensor", s_pe, fn)
                p.wait("scalar", s_pe, t_mm)
                h_evac[(sl, fc)] = p.inc("scalar", s_act,
                                         lambda e, fc=fc: e.activation(h_sb[:, 256 * fc:256 * (fc + 1)],
                                                                       ph[fc % 2][:, :], act_func,
                                                                       bias=b1_sb[:, fc:fc + 1]))
            w_mm1[b] = t_mm
            for tt in range(2):
                p.wait("tensor", s_act, h_evac[(sl, 31)])
                if sl > 0:
                    p.wait("tensor", s_dve, o_used[sl - 1][tt])
                for half in range(2):
                    fn0 = (lambda e, tt=tt, half=half: e.matmul(po[tt][:, 512 * half:512 * (half + 1)],
                                                                ones_bf[0:1, :],
                                                                b2row_sb[0:1, 512 * half:512 * (half + 1)],
                                                                start=True, stop=False))
                    p.run("tensor", fn0)
                for fc in range(32):
                    for half in range(2):
                        fn = (lambda e, fc=fc, tt=tt, half=half:
                              e.matmul(po[tt][:, 512 * half:512 * (half + 1)],
                                       h_sb[:, 256 * fc + 128 * tt:256 * fc + 128 * (tt + 1)],
                                       W2_sb[:, D * fc + 512 * half:D * fc + 512 * (half + 1)],
                                       start=False, stop=(fc == 31)))
                        if fc == 31 and half == 1:
                            t_mm2 = p.inc("tensor", s_pe, fn)
                        else:
                            p.run("tensor", fn)
                p.wait("vector", s_pe, t_mm2)
                if sl >= 2:
                    p.wait("vector", s_st[sl % 2], stored[sl - 2])
                half_off = 2 * D * (sl % 2)
                out_evac[sl][tt] = o_used[sl][tt] = p.inc(
                    "vector", s_dve,
                    lambda e, tt=tt, sl=sl, half_off=half_off:
                    e.tensor_tensor(out_f32[:, half_off + D * tt:half_off + D * (tt + 1)],
                                    po[tt][:, :], xb[sl % 2][:, D * tt:D * (tt + 1)], ALU.add))
            p.wait("sync", s_dve, out_evac[sl][1])
            if sl >= 2:
                p.wait("sync", s_st[sl % 2], stored[sl - 2])
            stored[sl] = p.inc("sync", s_st[sl % 2],
                               lambda e, sl=sl, half_off=2 * D * (sl % 2):
                               e.dma_start(x_slice_dram(x_dst, sl),
                                           sb_3d(out_f32[:, half_off:half_off + 2 * D])), 16)
            emit_load(sl + 2)
            if taps is not None and sl == 0:
                p.wait("gpsimd", s_dve, out_evac[sl][1])
                p.inc("gpsimd", ctx["s_w"], lambda e, b=b: e.dma_start(taps["lnfm"][:, :], lnfm[b][:, :]), 16)
                t_tap = p.inc("gpsimd", ctx["s_w"], lambda e: e.dma_start(taps["h"][:, :], h_sb[:, :]), 16)
                p.wait("gpsimd", ctx["s_w"], t_tap)

        for i in range(NSL + 1):
            if i < NSL:
                stage1(i)
            if i >= 1:
                stage2(i - 1)
        p.wait("sync", s_st[(NSL - 1) % 2], stored[NSL - 1])
        p.wait("sync", s_st[NSL % 2], stored[NSL - 2])
        p.realize(nc)


def emit_ffn_weight_load_plain(nc, ctx, *, W1_dram, W2_dram, b1_dram, b2_dram,
                               W1_sb, W2_sb, b1_sb, b2row_sb, name):
    s_w = ctx["s_w"]
    p = Plan(chain={"vector": ctx["s_dve"], "scalar": ctx["s_act"], "gpsimd": ctx["s_pool"]})
    t = 0
    for dc in range(8):
        t = p.inc("gpsimd", s_w, lambda e, dc=dc: e.dma_start(
            W1_sb[:, FF * dc:FF * (dc + 1)], W1_dram[128 * dc:128 * (dc + 1), :]), 16)
    for fc in range(32):
        t = p.inc("gpsimd", s_w, lambda e, fc=fc: e.dma_start(
            W2_sb[:, D * fc:D * (fc + 1)], W2_dram[128 * fc:128 * (fc + 1), :]), 16)
    t = p.inc("gpsimd", s_w, lambda e: e.dma_start(b1_sb[:, :], b1_dram[:, :]), 16)
    t = p.inc("gpsimd", s_w, lambda e: e.dma_start(b2row_sb[:, :], b2_dram[:, :]), 16)
    p.wait("gpsimd", s_w, t)
    p.realize(nc)


def emit_ffn_weight_ternarize(nc, ctx, *, W1_dram, W2_dram, b1_dram, b2_dram,
                              W1_sb, W2_sb, b1_sb, b2row_sb, name, taps=None):
    s_x, s_w, s_pe, s_dve, s_act, s_pool = (ctx["s_x"], ctx["s_w"], ctx["s_pe"],
                                            ctx["s_dve"], ctx["s_act"], ctx["s_pool"])
    s_l = ctx["s_l"]
    ones_f32 = ctx["ones_f32"]
    onescol = ctx["onescol_f32"]

    with (
        nc.sbuf_tensor(f"{name}_m0", [128, FF], F32) as m0,
        nc.sbuf_tensor(f"{name}_m1", [128, FF], F32) as m1,
        nc.sbuf_tensor(f"{name}_pa", [128, 48], F32) as partials,
        nc.sbuf_tensor(f"{name}_sa", [128, FF], F32) as sa,
        nc.sbuf_tensor(f"{name}_sbb", [128, FF], F32) as sb_,
        nc.sbuf_tensor(f"{name}_thr", [128, 4], F32) as thr,
        nc.sbuf_tensor(f"{name}_tot", [1, 2], F32) as tot,
        nc.psum_tensor(f"{name}_ps", [128, 8], F32) as ps,
    ):
        p = Plan(chain={"vector": ctx["s_dve"], "scalar": ctx["s_act"], "gpsimd": ctx["s_pool"]})
        mb = [m0, m1]
        loads = []
        red_done = []
        nch = 8 + 32
        for ci in range(nch):
            if ci >= 2:
                p.wait("sync", s_dve, red_done[ci - 2])
                p.wait("sync", s_l[ci % 2], loads[ci - 2])
            src = (W1_dram[128 * ci:128 * (ci + 1), :] if ci < 8 else
                   W2_dram[128 * (ci - 8):128 * (ci - 7), :])
            width = FF if ci < 8 else D
            loads.append(p.inc("sync", s_l[ci % 2],
                               lambda e, src=src, ci=ci, width=width: e.dma_start(mb[ci % 2][:, 0:width], src), 16))
            p.wait("vector", s_l[ci % 2], loads[ci])
            red_done.append(p.inc("vector", s_dve,
                                  lambda e, ci=ci, width=width: e.reduce_sum(
                                      partials[:, ci:ci + 1], mb[ci % 2][:, 0:width], axis=AX.X,
                                      apply_absolute_value=True)))
        p.wait("vector", s_dve, red_done[nch - 1])
        p.run("vector", lambda e: e.reduce_sum(partials[:, 40:41], partials[:, 0:8], axis=AX.X))
        t_p = p.inc("vector", s_dve, lambda e: e.reduce_sum(partials[:, 41:42], partials[:, 8:40], axis=AX.X))
        p.wait("vector", s_dve, t_p)
        p.wait("tensor", s_dve, t_p)
        p.run("tensor", lambda e: e.matmul(ps[0:1, 0:1], partials[:, 40:41], onescol[:, 0:1],
                                           start=True, stop=True))
        t_mm = p.inc("tensor", s_pe, lambda e: e.matmul(ps[0:1, 1:2], partials[:, 41:42], onescol[:, 0:1],
                                                        start=True, stop=True))
        p.wait("scalar", s_pe, t_mm)
        p.run("scalar", lambda e: e.activation(tot[0:1, 0:1], ps[0:1, 0:1], AF.Copy,
                                               scale=float(0.5 / (D * FF))))
        t_a = p.inc("scalar", s_act, lambda e: e.activation(tot[0:1, 1:2], ps[0:1, 1:2], AF.Copy,
                                                            scale=float(0.5 / (D * FF))))
        p.wait("vector", s_act, t_a)
        t_mx = p.inc("vector", s_dve, lambda e: e.tensor_scalar(tot[0:1, 0:2], tot[0:1, 0:2], 0.5e-8, None,
                                                                ALU.max))
        p.wait("tensor", s_dve, t_mx)
        p.run("tensor", lambda e: e.matmul(ps[:, 2:3], ones_f32[0:1, :], tot[0:1, 0:1],
                                           start=True, stop=True))
        t_bc = p.inc("tensor", s_pe, lambda e: e.matmul(ps[:, 3:4], ones_f32[0:1, :], tot[0:1, 1:2],
                                                        start=True, stop=True))
        p.wait("vector", s_pe, t_bc)
        p.run("vector", lambda e: e.tensor_copy(thr[:, 0:1], ps[:, 2:3]))
        p.run("vector", lambda e: e.tensor_scalar(thr[:, 1:2], ps[:, 2:3], -1.0, None, ALU.mult))
        p.run("vector", lambda e: e.tensor_copy(thr[:, 2:3], ps[:, 3:4]))
        t_th = p.inc("vector", s_dve, lambda e: e.tensor_scalar(thr[:, 3:4], ps[:, 3:4], -1.0, None, ALU.mult))

        p.wait("sync", s_dve, t_th)
        p.wait("gpsimd", s_dve, t_th)
        tern_done_v = [0] * nch
        tern_done_p = [0] * nch
        loads2 = []
        for ci in range(nch):
            if ci >= 2:
                p.wait("sync", s_dve, tern_done_v[ci - 2])
                p.wait("sync", s_l[ci % 2], loads2[ci - 2])
            src = (W1_dram[128 * ci:128 * (ci + 1), :] if ci < 8 else
                   W2_dram[128 * (ci - 8):128 * (ci - 7), :])
            width = FF if ci < 8 else D
            dst = (W1_sb[:, FF * ci:FF * (ci + 1)] if ci < 8 else
                   W2_sb[:, D * (ci - 8):D * (ci - 7)])
            tcol = 0 if ci < 8 else 2
            loads2.append(p.inc("sync", s_l[ci % 2],
                                lambda e, src=src, ci=ci, width=width: e.dma_start(mb[ci % 2][:, 0:width], src), 16))
            p.wait("vector", s_l[ci % 2], loads2[ci])
            p.wait("gpsimd", s_l[ci % 2], loads2[ci])
            t_a2 = p.inc("vector", s_dve,
                         lambda e, ci=ci, width=width, tcol=tcol: e.tensor_scalar(
                             sa[:, 0:width], mb[ci % 2][:, 0:width], thr[:, tcol:tcol + 1], None, ALU.is_gt))
            tern_done_p[ci] = p.inc("vector", s_dve,
                                    lambda e, ci=ci, width=width, tcol=tcol: e.tensor_scalar(
                                        sb_[:, 0:width], mb[ci % 2][:, 0:width], thr[:, tcol + 1:tcol + 2],
                                        None, ALU.is_lt))
            tern_done_v[ci] = p.inc("vector", s_dve,
                                    lambda e, ci=ci, width=width, dst=dst: e.tensor_tensor(
                                        dst, sa[:, 0:width], sb_[:, 0:width], ALU.subtract))
        if taps is not None:
            p.wait("gpsimd", s_dve, t_th)
            p.inc("gpsimd", s_w, lambda e: e.dma_start(taps["thr"][:, :], thr[:, :]), 16)
            p.inc("gpsimd", s_w, lambda e: e.dma_start(taps["partials"][:, :], partials[:, :]), 16)
        t_b = p.inc("gpsimd", s_w, lambda e: e.dma_start(b1_sb[:, :], b1_dram[:, :]), 16)
        t_b = p.inc("gpsimd", s_w, lambda e: e.dma_start(b2row_sb[:, :], b2_dram[:, :]), 16)
        p.wait("gpsimd", s_w, t_b)
        p.wait("vector", s_dve, tern_done_v[nch - 1])
        p.realize(nc)


def build_nc(phases=("l0a", "l0f", "l1a", "l1f"), debug_mid=False):
    nc = bass.Bass("TRN2", target_bir_lowering=False)
    x_in = nc.dram_tensor("x", [NTOK, D], F32, kind="ExternalInput").ap()
    y_out = nc.dram_tensor("y", [NTOK, D], F32, kind="ExternalOutput").ap()
    consts = {k: nc.dram_tensor(k, sh, dt, kind="ExternalInput").ap()
              for k, (sh, dt) in CONST_SPECS.items()}
    weights = {k: nc.dram_tensor(k, sh, dt, kind="ExternalInput").ap()
               for k, (sh, dt) in WEIGHT_SPECS.items()}
    n_mid = max(len(phases) - 1, 0)
    kind_mid = "ExternalOutput" if debug_mid else "Internal"
    mids = [nc.dram_tensor(f"xmid{j}", [NTOK, D], F32, kind=kind_mid).ap()
            for j in range(n_mid)]
    chain = [x_in] + mids + [y_out]

    with (
        nc.semaphore("s_x") as _sx,
        nc.semaphore("s_w") as _sw,
        nc.semaphore("s_s2") as _ss2,
        nc.semaphore("s_pe") as _spe,
        nc.semaphore("s_dve") as _sdve,
        nc.semaphore("s_act") as _sact,
        nc.semaphore("s_pool") as _spool,
        nc.semaphore("s_l0") as _sl0,
        nc.semaphore("s_l1") as _sl1,
        nc.semaphore("s_st0") as _sst0,
        nc.semaphore("s_st1") as _sst1,
        nc.sbuf_tensor("c_ident", [128, 128], BF16) as ident_sb,
        nc.sbuf_tensor("c_ones_bf", [1, 128], BF16) as ones_bf_sb,
        nc.sbuf_tensor("c_ones_f32", [1, 128], F32) as ones_f32_sb,
        nc.sbuf_tensor("c_onescol", [128, 1], F32) as onescol_sb,
    ):
        ctx = {
            "s_x": Ctr(_sx), "s_w": Ctr(_sw), "s_s2": Ctr(_ss2), "s_pe": Ctr(_spe),
            "s_dve": Ctr(_sdve), "s_act": Ctr(_sact), "s_pool": Ctr(_spool),
            "s_l": [Ctr(_sl0), Ctr(_sl1)], "s_st": [Ctr(_sst0), Ctr(_sst1)],
            "ident_bf": ident_sb, "ones_bf": ones_bf_sb, "ones_f32": ones_f32_sb,
            "onescol_f32": onescol_sb, "consts": consts, "weights": weights,
        }
        p = Plan(chain={"vector": ctx["s_dve"], "scalar": ctx["s_act"], "gpsimd": ctx["s_pool"]})
        t = p.inc("sync", ctx["s_x"], lambda e: e.dma_start(ident_sb[:, :], consts["ident_bf"][:, :]), 16)
        p.wait("sync", ctx["s_x"], t)
        t = p.inc("sync", ctx["s_x"], lambda e: e.dma_start(ones_bf_sb[:, :], consts["ones_bf"][:, :]), 16)
        p.wait("sync", ctx["s_x"], t)
        t = p.inc("sync", ctx["s_x"], lambda e: e.dma_start(ones_f32_sb[:, :], consts["ones_f32"][:, :]), 16)
        p.wait("sync", ctx["s_x"], t)
        t = p.inc("sync", ctx["s_x"], lambda e: e.dma_start(onescol_sb[:, :], consts["onescol_f32"][:, :]), 16)
        p.wait("sync", ctx["s_x"], t)
        p.realize(nc)

        for pi, ph in enumerate(phases):
            x_src, x_dst = chain[pi], chain[pi + 1]
            _scope = nc.named_scope(f"ph_{ph}")
            _scope.__enter__()
            if ph == "l0f":
                with (
                    nc.sbuf_tensor("l0f_W1t", [128, 8 * FF], BF16) as W1_sb,
                    nc.sbuf_tensor("l0f_W2t", [128, 32 * D], BF16) as W2_sb,
                    nc.sbuf_tensor("l0f_b1sb", [128, 32], F32) as b1_sb,
                    nc.sbuf_tensor("l0f_b2sb", [1, D], BF16) as b2row_sb,
                ):
                    emit_ffn_weight_ternarize(nc, ctx, W1_dram=weights["l0_W1T"],
                                              W2_dram=weights["l0_W2T"],
                                              b1_dram=weights["l0_b1c"], b2_dram=weights["l0_b2"],
                                              W1_sb=W1_sb, W2_sb=W2_sb, b1_sb=b1_sb,
                                              b2row_sb=b2row_sb, name="l0t")
                    emit_ffn_phase(nc, ctx, x_src=x_src, x_dst=x_dst,
                                   W1_sb=W1_sb, W2_sb=W2_sb, b1_sb=b1_sb, b2row_sb=b2row_sb,
                                   act_func=AF.Gelu, name="l0f")
            elif ph == "l1f":
                with (
                    nc.sbuf_tensor("l1f_W1sb", [128, 8 * FF], BF16) as W1_sb,
                    nc.sbuf_tensor("l1f_W2sb", [128, 32 * D], BF16) as W2_sb,
                    nc.sbuf_tensor("l1f_b1sb", [128, 32], F32) as b1_sb,
                    nc.sbuf_tensor("l1f_b2sb", [1, D], BF16) as b2row_sb,
                ):
                    emit_ffn_weight_load_plain(nc, ctx, W1_dram=weights["l1f_W1"],
                                               W2_dram=weights["l1f_W2"],
                                               b1_dram=weights["l1f_b1c"], b2_dram=weights["l1f_b2"],
                                               W1_sb=W1_sb, W2_sb=W2_sb, b1_sb=b1_sb,
                                               b2row_sb=b2row_sb, name="l1f")
                    emit_ffn_phase(nc, ctx, x_src=x_src, x_dst=x_dst,
                                   W1_sb=W1_sb, W2_sb=W2_sb, b1_sb=b1_sb, b2row_sb=b2row_sb,
                                   act_func=AF.Relu, name="l1f")
            elif ph == "l0a":
                emit_l0_attn(nc, ctx, x_src=x_src, x_dst=x_dst)
            elif ph == "l1a":
                emit_l1_attn(nc, ctx, x_src=x_src, x_dst=x_dst)
            else:
                raise ValueError(ph)
            _scope.__exit__(None, None, None)
    return nc


def emit_l0_attn(nc, ctx, *, x_src, x_dst):
    s_x, s_w, s_s2, s_pe, s_dve, s_act, s_pool = (ctx["s_x"], ctx["s_w"], ctx["s_s2"],
                                                  ctx["s_pe"], ctx["s_dve"], ctx["s_act"],
                                                  ctx["s_pool"])
    s_l = ctx["s_l"]
    s_st = ctx["s_st"]
    w = ctx["weights"]
    consts = ctx["consts"]
    ident = ctx["ident_bf"]
    ones_f32 = ctx["ones_f32"]
    EPS = 1e-6

    es_ = ExitStack()

    def _sb(name, shape, dt, **kw):
        return es_.enter_context(nc.sbuf_tensor(name, shape, dt, **kw))

    WqT_sb = _sb("l0a_WqT", [128, 8 * 1024], BF16)
    WkvT_sb = _sb("l0a_WkvT", [128, 8 * 512], BF16)
    WpT_sb = _sb("l0a_WpT", [64, 16 * 1024], BF16)
    mgt_sb = _sb("l0a_mgt", [128, 512], BF16)
    mle_sb = _sb("l0a_mle", [128, 512], BF16)
    stat = _sb("l0a_stat", [128, 8], F32)
    xt = _sb("l0a_xt", [128, D], F32)
    xout = _sb("l0a_xout", [128, D], F32)
    q_hfm = _sb("l0a_qh", [128, 8 * 2048], BF16)
    k_fm = _sb("l0a_kf", [128, 2 * 2048], BF16)
    v_aug = _sb("l0a_va", [128, TPS * 260], BF16)
    if True:
        p = Plan(chain={"vector": ctx["s_dve"], "scalar": ctx["s_act"], "gpsimd": ctx["s_pool"]})
        t_w = p.inc("gpsimd", s_w, lambda e: e.dma_start(
            WqT_sb.ap().rearrange("p (c f) -> p c f", c=8),
            w["l0_WqT"].rearrange("(c p) f -> p c f", p=128)), 16)
        t_w = p.inc("gpsimd", s_w, lambda e: e.dma_start(
            WkvT_sb.ap().rearrange("p (c f) -> p c f", c=8),
            w["l0_WkvT"].rearrange("(c p) f -> p c f", p=128)), 16)
        t_w = p.inc("gpsimd", s_w, lambda e: e.dma_start(
            WpT_sb[:, :], w["l0_WprojTh"][:, :]), 16)
        t_w = p.inc("gpsimd", s_w, lambda e: e.dma_start(mgt_sb[:, :], consts["negstep_gt4"][:, :]), 16)
        t_w = p.inc("gpsimd", s_w, lambda e: e.dma_start(mle_sb[:, :], consts["negstep_le4"][:, :]), 16)
        p.wait("gpsimd", s_w, t_w)
        p.realize(nc)

        for sq in range(NSEQ):
            tile0 = TPS * sq
            _scA = nc.named_scope(f"l0a_A{sq}"); _scA.__enter__()
            esA = ExitStack()
            def _sbA(name, shape, dt):
                return esA.enter_context(nc.sbuf_tensor(name + str(sq), shape, dt, side="right"))
            scr = [_sbA("l0a_scrA", [128, D], F32), _sbA("l0a_scrB", [128, D], F32)]
            scr2 = [_sbA("l0a_scr2A", [128, D], F32), _sbA("l0a_scr2B", [128, D], F32)]
            ln_bf = [_sbA("l0a_lnbfA", [128, D], BF16), _sbA("l0a_lnbfB", [128, D], BF16)]
            lnfm = [_sbA("l0a_lnfmA", [128, D], BF16), _sbA("l0a_lnfmB", [128, D], BF16)]
            qsb = [_sbA("l0a_qsbA", [128, D], F32), _sbA("l0a_qsbB", [128, D], F32)]
            q_rope = [_sbA("l0a_qrA", [128, D], BF16), _sbA("l0a_qrB", [128, D], BF16)]
            k_rope = [_sbA("l0a_krA", [128, 256], BF16), _sbA("l0a_krB", [128, 256], BF16)]
            qn = [_sbA("l0a_qnA", [128, 64], F32), _sbA("l0a_qnB", [128, 64], F32)]
            xtb = [_sbA("l0a_xtA", [128, D], F32), _sbA("l0a_xtB", [128, D], F32)]
            tq = _sbA("l0a_tq", [128, D], F32)
            tq2 = _sbA("l0a_tq2", [128, D], F32)
            ct0 = _sbA("l0a_ct0", [128, D], F32)
            ct1 = _sbA("l0a_ct1", [128, D], F32)
            st0 = _sbA("l0a_st0", [128, D], F32)
            st1 = _sbA("l0a_st1", [128, D], F32)
            pst = esA.enter_context(nc.psum_tensor(f"l0a_pst{sq}", [128, 1024], BF16))
            psq = esA.enter_context(nc.psum_tensor(f"l0a_psq{sq}", [128, 1024], F32))
            pskv = esA.enter_context(nc.psum_tensor(f"l0a_pskv{sq}", [128, 512], F32))
            psT2a = esA.enter_context(nc.psum_tensor(f"l0a_pTa{sq}", [128, 128], BF16))
            psT2b = esA.enter_context(nc.psum_tensor(f"l0a_pTb{sq}", [128, 128], BF16))
            if True:
                p = Plan(chain={"vector": ctx["s_dve"], "scalar": ctx["s_act"], "gpsimd": ctx["s_pool"]})
                ctb = [ct0, ct1]
                stb = [st0, st1]
                psT2 = [psT2a, psT2b]
                t_ones = p.inc("vector", s_dve, lambda e: e.memset(
                    v_aug.ap().rearrange("p (j g c) -> p j g c", j=TPS, g=4)[:, :, :, 64:65], 1.0))
                w_sq = [0, 0]
                w_pool_scr2 = [0, 0]
                w_tr = [0, 0]
                w_qk = [0, 0]
                w_qrope = [0, 0]
                w_packq = [0, 0]
                w_packk = [0, 0]
                w_ln = [0, 0]
                pack_ev = {}
                bank_last = {0: 0, 1: 0}
                t_vev_prev = 0
                t_sclk_prev = 0
                pst_free = 0
                x_ld_prev = 0
                cs_ld = [0] * TPS
                t_krope_par = [0, 0]
                for tt in range(TPS):
                    b = tt % 2
                    gt = tile0 + tt
                    if tt >= 2:
                        p.wait("scalar", s_pool, t_krope_par[b])
                        p.wait("scalar", s_s2, cs_ld[tt - 1])
                    cs_ld[tt] = p.inc("scalar", s_s2, lambda e, tt=tt: e.dma_start(
                        ctb[tt % 2][:, :], consts["cos16"][128 * tt:128 * (tt + 1), :]), 16)
                    cs_ld[tt] = p.inc("scalar", s_s2, lambda e, tt=tt: e.dma_start(
                        stb[tt % 2][:, :], consts["sinA16"][128 * tt:128 * (tt + 1), :]), 16)
                    p.wait("sync", s_dve, w_ln[b])
                    p.wait("sync", s_l[1], x_ld_prev)
                    x_ld_prev = x_ld = p.inc("sync", s_l[1], lambda e, gt=gt, b=b: e.dma_start(
                        xtb[b][:, :], x_src[128 * gt:128 * (gt + 1), :]), 16)
                    p.wait("vector", s_l[1], x_ld)
                    p.wait("scalar", s_pool, w_pool_scr2[b])
                    t_ln, t_sq_ = emit_ln_tile(p, ctx, xtb[b][:, :], ln_bf[b][:, :],
                                               scr[b], scr2[b], stat, 1.0 / D, 1e-5,
                                               war_act=w_sq[b], war_pe=w_tr[b])
                    w_ln[b] = t_ln
                    w_sq[b] = t_sq_
                    p.wait("tensor", s_dve, t_ln)
                    p.wait("tensor", s_dve, pst_free)
                    for c in range(8):
                        fn = (lambda e, c=c, b=b: e.transpose(pst[:, 128 * c:128 * (c + 1)],
                                                              ln_bf[b][:, 128 * c:128 * (c + 1)], ident[:, :]))
                        if c < 7:
                            p.run("tensor", fn)
                        else:
                            t_tr = p.inc("tensor", s_pe, fn)
                    w_tr[b] = t_tr
                    p.wait("vector", s_pe, t_tr)
                    p.wait("vector", s_pe, w_qk[b])
                    t_lnfm = p.inc("vector", s_dve, lambda e, b=b: e.tensor_copy(lnfm[b][:, :], pst[:, :]))
                    pst_free = t_lnfm
                    p.wait("tensor", s_dve, t_lnfm)
                    p.wait("tensor", s_act, t_sclk_prev)
                    p.wait("tensor", s_dve, t_vev_prev)
                    for nh in range(2):
                        for dc in range(8):
                            p.run("tensor", lambda e, nh=nh, dc=dc, b=b: e.matmul(
                                psq[:, 512 * nh:512 * (nh + 1)],
                                lnfm[b][:, 128 * dc:128 * (dc + 1)],
                                WqT_sb[:, 1024 * dc + 512 * nh:1024 * dc + 512 * (nh + 1)],
                                start=(dc == 0), stop=(dc == 7)))
                    for dc in range(8):
                        fn = (lambda e, dc=dc, b=b: e.matmul(
                            pskv[:, :], lnfm[b][:, 128 * dc:128 * (dc + 1)],
                            WkvT_sb[:, 512 * dc:512 * (dc + 1)],
                            start=(dc == 0), stop=(dc == 7)))
                        if dc == 7:
                            t_qk = p.inc("tensor", s_pe, fn)
                        else:
                            p.run("tensor", fn)
                    w_qk[b] = t_qk
                    p.wait("scalar", s_pe, t_qk)
                    p.run("scalar", lambda e, b=b: e.activation(scr2[b][:, :], psq[:, :], AF.Square))
                    t_sqk = p.inc("scalar", s_act, lambda e, b=b: e.activation(
                        scr[b][:, 0:256], pskv[:, 0:256], AF.Square))
                    p.wait("vector", s_act, t_sqk)
                    p.run("vector", lambda e, b=b: e.reduce_sum(
                        qn[b][:, 0:16], scr2[b].ap().rearrange("p (h a) -> p h a", h=16), axis=AX.X))
                    t_r2 = p.inc("vector", s_dve, lambda e, b=b: e.reduce_sum(
                        qn[b][:, 16:20], scr[b][:, 0:256].rearrange("p (h a) -> p h a", h=4), axis=AX.X))
                    p.wait("scalar", s_dve, t_r2)
                    t_sqr = p.inc("scalar", s_act, lambda e, b=b: e.activation(
                        qn[b][:, 24:44], qn[b][:, 0:20], AF.Sqrt))
                    p.wait("vector", s_act, t_sqr)
                    t_eps = p.inc("vector", s_dve, lambda e, b=b: e.tensor_scalar(
                        qn[b][:, 24:44], qn[b][:, 24:44], EPS, None, ALU.add))
                    t_rcp = p.inc("vector", s_dve, lambda e, b=b: e.reciprocal(qn[b][:, 44:64], qn[b][:, 24:44]))
                    t_scq = p.inc("vector", s_dve, lambda e, b=b: e.tensor_scalar(
                        qn[b][:, 44:60], qn[b][:, 44:60], 0.125, None, ALU.mult))
                    p.wait("scalar", s_dve, t_scq)
                    p.wait("scalar", s_pool, w_qrope[b])
                    for h in range(16):
                        fn = (lambda e, h=h, b=b: e.activation(
                            qsb[b][:, 64 * h:64 * (h + 1)], psq[:, 64 * h:64 * (h + 1)],
                            AF.Identity, scale=qn[b][:, 44 + h:45 + h]))
                        if h == 15:
                            t_scl = p.inc("scalar", s_act, fn)
                        else:
                            p.run("scalar", fn)
                    for h in range(4):
                        fn = (lambda e, h=h, b=b: e.activation(
                            scr2[b][:, 64 * h:64 * (h + 1)], pskv[:, 64 * h:64 * (h + 1)],
                            AF.Identity, scale=qn[b][:, 60 + h:61 + h]))
                        if h == 3:
                            t_sclk = p.inc("scalar", s_act, fn)
                        else:
                            p.run("scalar", fn)
                    t_sclk_prev = t_sclk
                    p.wait("vector", s_act, t_sclk)
                    p.wait("vector", s_dve, t_ones)
                    t_vev = p.inc("vector", s_dve, lambda e, tt=tt: e.tensor_copy(
                        v_aug.ap().rearrange("p (j g c) -> p j g c", j=TPS, g=4)[:, tt, :, 0:64],
                        pskv[:, 256:512].rearrange("p (g c) -> p g c", g=4)))
                    t_vev_prev = t_vev
                    p.wait("gpsimd", s_act, t_scl)
                    p.wait("gpsimd", s_s2, cs_ld[tt])
                    p.run("gpsimd", lambda e, tt=tt, b=b: e.tensor_tensor(
                        tq[:, :], qsb[b][:, :], ctb[tt % 2][:, :], ALU.mult))
                    p.run("gpsimd", lambda e, tt=tt, b=b: e.tensor_tensor(
                        tq2.ap().rearrange("p (h a) -> p h a", h=16)[:, :, 0:32],
                        qsb[b].ap().rearrange("p (h a) -> p h a", h=16)[:, :, 32:64],
                        stb[tt % 2].ap().rearrange("p (h a) -> p h a", h=16)[:, :, 0:32], ALU.mult))
                    p.run("gpsimd", lambda e, tt=tt, b=b: e.tensor_tensor(
                        tq2.ap().rearrange("p (h a) -> p h a", h=16)[:, :, 32:64],
                        qsb[b].ap().rearrange("p (h a) -> p h a", h=16)[:, :, 0:32],
                        stb[tt % 2].ap().rearrange("p (h a) -> p h a", h=16)[:, :, 32:64], ALU.mult))
                    p.wait("gpsimd", s_pe, w_packq[b])
                    t_qrope = p.inc("gpsimd", s_pool, lambda e, b=b: e.tensor_tensor(
                        q_rope[b][:, :], tq[:, :], tq2[:, :], ALU.add))
                    w_qrope[b] = t_qrope
                    p.wait("gpsimd", s_act, t_sclk)
                    p.run("gpsimd", lambda e, tt=tt, b=b: e.tensor_tensor(
                        tq[:, 0:256], scr2[b][:, 0:256], ctb[tt % 2][:, 0:256], ALU.mult))
                    p.run("gpsimd", lambda e, tt=tt, b=b: e.tensor_tensor(
                        tq2.ap().rearrange("p (h a) -> p h a", h=16)[:, 0:4, 0:32],
                        scr2[b].ap().rearrange("p (h a) -> p h a", h=16)[:, 0:4, 32:64],
                        stb[tt % 2].ap().rearrange("p (h a) -> p h a", h=16)[:, 0:4, 0:32], ALU.mult))
                    p.run("gpsimd", lambda e, tt=tt, b=b: e.tensor_tensor(
                        tq2.ap().rearrange("p (h a) -> p h a", h=16)[:, 0:4, 32:64],
                        scr2[b].ap().rearrange("p (h a) -> p h a", h=16)[:, 0:4, 0:32],
                        stb[tt % 2].ap().rearrange("p (h a) -> p h a", h=16)[:, 0:4, 32:64], ALU.mult))
                    p.wait("gpsimd", s_pe, w_packk[b])
                    t_krope = p.inc("gpsimd", s_pool, lambda e, b=b: e.tensor_tensor(
                        k_rope[b][:, :], tq[:, 0:256], tq2[:, 0:256], ALU.add))
                    w_pool_scr2[b] = t_krope
                    t_krope_par[b] = t_krope
                    p.wait("tensor", s_pool, t_qrope)
                    pi = 0
                    for cpair in range(2):
                        for s5 in range(4):
                            h1 = 8 * cpair + s5
                            h2 = 8 * cpair + 4 + s5
                            g = 4 * cpair + s5
                            bk = pi % 2
                            p.wait("tensor", s_dve, bank_last[bk])
                            p.run("tensor", lambda e, h1=h1, bk=bk, b=b: e.transpose(
                                psT2[bk][0:64, :], q_rope[b][:, 64 * h1:64 * (h1 + 1)], ident[:, :]))
                            t_pT = p.inc("tensor", s_pe, lambda e, h2=h2, bk=bk, b=b: e.transpose(
                                psT2[bk][64:128, :], q_rope[b][:, 64 * h2:64 * (h2 + 1)], ident[:, :]))
                            p.wait("vector", s_pe, t_pT)
                            bank_last[bk] = pack_ev[(tt, pi)] = p.inc(
                                "vector", s_dve,
                                lambda e, g=g, tt=tt, bk=bk: e.tensor_copy(
                                    q_hfm[:, 2048 * g + 128 * tt:2048 * g + 128 * (tt + 1)],
                                    psT2[bk][:, :]))
                            pi += 1
                    w_packq[b] = t_pT
                    p.wait("tensor", s_pool, t_krope)
                    for cpair in range(2):
                        bk = pi % 2
                        p.wait("tensor", s_dve, bank_last[bk])
                        p.run("tensor", lambda e, cpair=cpair, bk=bk, b=b: e.transpose(
                            psT2[bk][0:64, :], k_rope[b][:, 128 * cpair:128 * cpair + 64], ident[:, :]))
                        t_pT = p.inc("tensor", s_pe, lambda e, cpair=cpair, bk=bk, b=b: e.transpose(
                            psT2[bk][64:128, :], k_rope[b][:, 128 * cpair + 64:128 * (cpair + 1)], ident[:, :]))
                        p.wait("vector", s_pe, t_pT)
                        bank_last[bk] = pack_ev[(tt, pi)] = p.inc(
                            "vector", s_dve,
                            lambda e, cpair=cpair, tt=tt, bk=bk: e.tensor_copy(
                                k_fm[:, 2048 * cpair + 128 * tt:2048 * cpair + 128 * (tt + 1)],
                                psT2[bk][:, :]))
                        pi += 1
                    w_packk[b] = t_pT
                p.wait("vector", s_dve, pack_ev[(TPS - 1, 9)])
                p.realize(nc)
            esA.close()
            _scA.__exit__(None, None, None)

            if globals().get("L0_SKIP_B"):
                continue
            _scB = nc.named_scope(f"l0a_B{sq}"); _scB.__enter__()
            esB = ExitStack()
            expT = esB.enter_context(nc.sbuf_tensor(f"l0b_exp{sq}", [128, 2048], BF16, side="right"))
            y_bf = esB.enter_context(nc.sbuf_tensor(f"l0b_y{sq}", [64, 2048], BF16, side="right"))
            lnrow = esB.enter_context(nc.sbuf_tensor(f"l0b_lnr{sq}", [1, 2048], F32, side="right"))
            rsrow = esB.enter_context(nc.sbuf_tensor(f"l0b_rsr{sq}", [1, 2048], F32, side="right"))
            psc = esB.enter_context(nc.psum_tensor(f"l0b_psc{sq}", [128, 2048], F32))
            psY = esB.enter_context(nc.psum_tensor(f"l0b_psY{sq}", [65, 2048], F32))
            if True:
                p = Plan(chain={"vector": ctx["s_dve"], "scalar": ctx["s_act"], "gpsimd": ctx["s_pool"]})
                out_evac = [0] * TPS
                stored = [0] * TPS
                psc_free_dve = 0
                psc_free_act = 0
                psY_free = 0
                exp_free_pe = 0
                xq_ld_l = [0] * (TPS + 1)
                xq_ld_l[0] = p.inc("sync", s_l[1], lambda e: e.dma_start(
                    xt[:, :], x_src[128 * tile0:128 * (tile0 + 1), :]), 16)
                for qt in range(TPS):
                    jt0 = max(0, qt - 4)
                    njt = qt - jt0 + 1
                    xq_ld = xq_ld_l[qt]
                    for ji, jt in enumerate(range(jt0, qt + 1)):
                        p.wait("tensor", s_dve, psc_free_dve)
                        p.wait("tensor", s_act, psc_free_act)
                        masked = (jt == qt) or (qt >= 4 and jt == jt0)
                        mask_sb = mgt_sb if jt == qt else mle_sb
                        for kvh in range(4):
                            rh = 64 * (kvh % 2)
                            qv_view = q_hfm.ap().rearrange("p (g t) -> p g t", g=8)[
                                rh:rh + 64, 4 * (kvh // 2):4 * (kvh // 2) + 4,
                                128 * qt:128 * (qt + 1)]
                            fn = (lambda e, kvh=kvh, rh=rh, jt=jt, qv_view=qv_view, masked=masked: e.matmul(
                                psc[:, 512 * kvh:512 * (kvh + 1)],
                                k_fm[rh:rh + 64, 2048 * (kvh // 2) + 128 * jt:2048 * (kvh // 2) + 128 * (jt + 1)],
                                qv_view, start=True, stop=not masked))
                            fn2 = (lambda e, kvh=kvh, mask_sb=mask_sb: e.matmul(
                                psc[:, 512 * kvh:512 * (kvh + 1)],
                                ident[:, :], mask_sb[:, :], start=False, stop=True))
                            last = (kvh == 3)
                            if masked:
                                p.run("tensor", fn)
                                if last:
                                    t_sc = p.inc("tensor", s_pe, fn2)
                                else:
                                    p.run("tensor", fn2)
                            else:
                                if last:
                                    t_sc = p.inc("tensor", s_pe, fn)
                                else:
                                    p.run("tensor", fn)
                        p.wait("scalar", s_pe, t_sc)
                        p.wait("scalar", s_pe, exp_free_pe)
                        t_exp = p.inc("scalar", s_act, lambda e: e.activation(
                            expT[:, :], psc[:, :], AF.Exp))
                        psc_free_act = t_exp
                        p.wait("tensor", s_act, t_exp)
                        if ji == 0:
                            p.wait("tensor", s_dve, psY_free)
                        for kvh in range(4):
                            fn = (lambda e, kvh=kvh, jt=jt, ji=ji, qt=qt: e.matmul(
                                psY[:, 512 * kvh:512 * (kvh + 1)],
                                v_aug[:, 260 * jt + 65 * kvh:260 * jt + 65 * (kvh + 1)],
                                expT[:, 512 * kvh:512 * (kvh + 1)],
                                start=(ji == 0), stop=(jt == qt)))
                            if kvh == 3:
                                t_av = p.inc("tensor", s_pe, fn)
                            else:
                                p.run("tensor", fn)
                        exp_free_pe = t_av
                    p.wait("scalar", s_pe, t_av)
                    t_ln2 = p.inc("scalar", s_act, lambda e: e.activation(
                        lnrow[0:1, :], psY[64:65, :], AF.Ln))
                    p.wait("scalar", s_act, t_ln2)
                    t_rs = p.inc("scalar", s_act, lambda e: e.activation(
                        rsrow[0:1, :], lnrow[0:1, :], AF.Exp, scale=-1.0))
                    p.wait("tensor", s_act, t_rs)
                    p.wait("tensor", s_dve, psc_free_dve)
                    p.wait("tensor", s_act, psc_free_act)
                    for kvh in range(4):
                        fn = (lambda e, kvh=kvh: e.matmul(
                            psc[0:64, 512 * kvh:512 * (kvh + 1)], ones_f32[0:1, 0:64],
                            rsrow[0:1, 512 * kvh:512 * (kvh + 1)], start=True, stop=True))
                        if kvh == 3:
                            t_bc = p.inc("tensor", s_pe, fn)
                        else:
                            p.run("tensor", fn)
                    p.wait("vector", s_pe, t_bc)
                    t_yc = p.inc("vector", s_dve, lambda e: e.tensor_copy(y_bf[:, :], psY[0:64, :]))
                    p.wait("vector", s_dve, t_yc)
                    t_yn = p.inc("vector", s_dve, lambda e: e.tensor_tensor(
                        y_bf[:, :], y_bf[:, :], psc[0:64, 0:2048], ALU.mult))
                    psY_free = t_yn
                    p.wait("tensor", s_dve, t_yn)
                    for h in range(16):
                        for hf in range(2):
                            kvh, s5 = h // 4, h % 4
                            fn = (lambda e, h=h, hf=hf, kvh=kvh, s5=s5: e.matmul(
                                psc[:, 1024 + 512 * hf:1024 + 512 * (hf + 1)],
                                y_bf[0:64, 512 * kvh + 128 * s5:512 * kvh + 128 * (s5 + 1)],
                                WpT_sb[0:64, 1024 * h + 512 * hf:1024 * h + 512 * (hf + 1)],
                                start=(h == 0), stop=(h == 15)))
                            if h == 15 and hf == 1:
                                t_pj = p.inc("tensor", s_pe, fn)
                            else:
                                p.run("tensor", fn)
                    p.wait("vector", s_pe, t_pj)
                    p.wait("vector", s_l[1], xq_ld)
                    if qt >= 1:
                        p.wait("vector", s_st[1], stored[qt - 1])
                    out_evac[qt] = psc_free_dve = p.inc("vector", s_dve, lambda e: e.tensor_tensor(
                        xout[:, :], psc[:, 1024:2048], xt[:, :], ALU.add))
                    p.wait("sync", s_dve, out_evac[qt])
                    if qt >= 1:
                        p.wait("sync", s_st[1], stored[qt - 1])
                    stored[qt] = p.inc("sync", s_st[1], lambda e, qt=qt: e.dma_start(
                        x_dst[128 * (tile0 + qt):128 * (tile0 + qt + 1), :], xout[:, :]), 16)
                    if qt + 1 < TPS:
                        p.wait("sync", s_l[1], xq_ld_l[qt])
                        xq_ld_l[qt + 1] = p.inc("sync", s_l[1], lambda e, qt=qt: e.dma_start(
                            xt[:, :], x_src[128 * (tile0 + qt + 1):128 * (tile0 + qt + 2), :]), 16)
                p.wait("sync", s_st[1], stored[TPS - 1])
                p.realize(nc)
            esB.close()
            _scB.__exit__(None, None, None)
    es_.close()


def emit_l1_attn(nc, ctx, *, x_src, x_dst):
    s_x, s_w, s_s2, s_pe, s_dve, s_act, s_pool = (ctx["s_x"], ctx["s_w"], ctx["s_s2"],
                                                  ctx["s_pe"], ctx["s_dve"], ctx["s_act"],
                                                  ctx["s_pool"])
    s_l = ctx["s_l"]
    s_st = ctx["s_st"]
    w = ctx["weights"]
    consts = ctx["consts"]
    ident = ctx["ident_bf"]
    TAU = float(KX) ** 0.5

    s2buf = nc.dram_tensor("l1a_s2buf", [2, 128, 2176], F32, kind="Internal").ap()
    rms_dram = nc.dram_tensor("l1a_rmsd", [8, 128, T], BF16, kind="Internal").ap()

    es_ = ExitStack()

    def _sb(name, shape, dt, **kw):
        return es_.enter_context(nc.sbuf_tensor(name, shape, dt, **kw))

    Wq_sb = _sb("l1a_Wq", [128, 8 * 128], BF16)
    Wk_sb = _sb("l1a_Wk", [128, 8 * 128], BF16)
    g_sb = _sb("l1a_gc", [128, 8], F32)
    xlu_sb = _sb("l1a_xlu", [KX, 1], F32)
    xlv_sb = _sb("l1a_xlv", [KX, 1], F32)
    rrel_sb = _sb("l1a_rrel", [128, T], BF16)
    negstep_sb = _sb("l1a_neg", [128, 128], F32)
    wbuf = _sb("l1a_wbuf", [128, 16 * 1024], BF16)
    v_sb = _sb("l1a_v", [128, TPS * VX], BF16)
    sg_sb = _sb("l1a_sg", [128, 16 * T], BF16)
    qu_sb = _sb("l1a_qu", [128, T], BF16)
    qv_sb = _sb("l1a_qv", [128, T], BF16)
    k_sb = _sb("l1a_k", [128, T], BF16)
    stat = _sb("l1a_stat", [128, 8], F32)
    xt = _sb("l1a_xt", [128, D], F32)
    xout = _sb("l1a_xout", [128, D], F32)
    if True:
        with (
            nc.sbuf_tensor("l1a_m0", [128, 256], F32) as m0,
            nc.sbuf_tensor("l1a_cat", [128, 2048], BF16) as cat0,
            nc.sbuf_tensor("l1a_catf", [128, 2048], F32) as catf,
            nc.sbuf_tensor("l1a_Wr", [128, 8 * 128], BF16) as Wr_sb,
            nc.sbuf_tensor("l1a_zero", [128, 128], F32) as zero_sb,
            nc.psum_tensor("l1a_psr", [128, 2048], F32) as psr,
        ):
            p = Plan(chain={"vector": ctx["s_dve"], "scalar": ctx["s_act"], "gpsimd": ctx["s_pool"]})
            t_g = p.inc("sync", s_x, lambda e: e.dma_start(g_sb[:, :], w["l1_gc"][:, :]), 16)
            p.wait("sync", s_x, t_g)
            t_xu = p.inc("sync", s_x, lambda e: e.dma_start(xlu_sb[:, :], w["l1_xlu"][:, :]), 16)
            p.wait("sync", s_x, t_xu)
            t_xv = p.inc("sync", s_x, lambda e: e.dma_start(xlv_sb[:, :], w["l1_xlv"][:, :]), 16)
            p.wait("sync", s_x, t_xv)
            t_ng = p.inc("sync", s_x, lambda e: e.dma_start(negstep_sb[:, :], consts["l1_negstep"][:, :]), 16)
            p.wait("sync", s_x, t_ng)
            t_wr = p.inc("gpsimd", s_w, lambda e: e.dma_start(
                Wr_sb.ap().rearrange("p (c k) -> p c k", c=8),
                w["l1_Wr"].rearrange("(c p) k -> p c k", p=128)), 16)
            t_k = 0
            t_m = 0
            t_m2 = 0
            for dc in range(8):
                p.wait("sync", s_dve, t_k)
                p.wait("sync", s_l[0], t_m)
                p.wait("sync", s_l[1], t_m2)
                t_m = p.inc("sync", s_l[0], lambda e, dc=dc: e.dma_start(
                    m0[:, 0:KX], w["l1_Wq"][128 * dc:128 * (dc + 1), :]), 16)
                t_m2 = p.inc("sync", s_l[1], lambda e, dc=dc: e.dma_start(
                    m0[:, KX:2 * KX], w["l1_Wk"][128 * dc:128 * (dc + 1), :]), 16)
                p.wait("vector", s_l[0], t_m)
                p.wait("vector", s_x, t_g)
                t_q = p.inc("vector", s_dve, lambda e, dc=dc: e.tensor_scalar(
                    Wq_sb[:, 128 * dc:128 * (dc + 1)], m0[:, 0:KX],
                    g_sb[:, dc:dc + 1], 1.0 / TAU, ALU.mult, ALU.mult))
                p.wait("vector", s_l[1], t_m2)
                t_k = p.inc("vector", s_dve, lambda e, dc=dc: e.tensor_scalar(
                    Wk_sb[:, 128 * dc:128 * (dc + 1)], m0[:, KX:2 * KX],
                    g_sb[:, dc:dc + 1], None, ALU.mult))
            rr_t = [0] * 8
            t_cat = 0
            t_cc = 0
            for dc in range(8):
                if dc == 0:
                    p.wait("gpsimd", s_w, t_wr)
                if dc >= 1:
                    p.wait("gpsimd", s_dve, t_cc)
                    p.wait("gpsimd", s_w, t_cat)
                t_cat = p.inc("gpsimd", s_w, lambda e, dc=dc: e.dma_start(
                    catf[:, :], consts["catrevT"][128 * dc:128 * (dc + 1), :]), 16)
                p.wait("vector", s_w, t_cat)
                if dc >= 1:
                    p.wait("vector", s_pe, rr_t[dc - 1])
                t_cc = p.inc("vector", s_dve, lambda e: e.tensor_copy(cat0[:, :], catf[:, :]))
                p.wait("tensor", s_dve, t_cc)
                p.wait("tensor", s_w, t_wr)
                for es in range(4):
                    fn = (lambda e, dc=dc, es=es: e.matmul(
                        psr[:, 512 * es:512 * (es + 1)],
                        Wr_sb[:, 128 * dc:128 * (dc + 1)],
                        cat0[:, 512 * es:512 * (es + 1)],
                        start=(dc == 0), stop=(dc == 7)))
                    if es == 3:
                        rr_t[dc] = p.inc("tensor", s_pe, fn)
                    else:
                        p.run("tensor", fn)
            p.wait("vector", s_pe, rr_t[7])
            t_rrev = p.inc("vector", s_dve, lambda e: e.tensor_copy(rrel_sb[:, :], psr[:, :]))
            t_z = p.inc("vector", s_dve, lambda e: e.memset(zero_sb[:, :], 0.0))
            p.wait("sync", s_dve, t_z)
            p.wait("sync", s_x, s_x.val)
            for b in range(2):
                t_zp = p.inc("sync", s_x, lambda e, b=b: e.dma_start(
                    s2buf[b, :, 2048:2176], zero_sb[:, 0:128]), 16)
                p.wait("sync", s_x, t_zp)
            p.wait("sync", s_dve, t_rrev)
            p.realize(nc)

        for sq in range(NSEQ):
            tile0 = TPS * sq
            _sc1 = nc.named_scope(f"l1a_1a{sq}"); _sc1.__enter__()
            with (
                nc.sbuf_tensor(f"l1a_rmst{sq}", [128, 8 * 128], BF16, side="right") as rmstile,
                nc.sbuf_tensor(f"l1a_wm{sq}", [128, 1024], F32, side="right") as wm,
                nc.sbuf_tensor(f"l1a_scr{sq}", [128, D], F32, side="right") as scr,
                nc.sbuf_tensor(f"l1a_scr2{sq}", [128, D], F32, side="right") as scr2,
                nc.sbuf_tensor(f"l1a_lnbf{sq}", [128, D], BF16, side="right") as ln_bf,
                nc.psum_tensor(f"l1a_pst{sq}", [128, 1024], BF16) as pst,
                nc.psum_tensor(f"l1a_psv{sq}", [128, 2048], F32) as psv,
                nc.psum_tensor(f"l1a_psq{sq}", [128, 128], F32) as psq,
                nc.psum_tensor(f"l1a_psk{sq}", [128, 128], F32) as psk,
            ):
                p = Plan(chain={"vector": ctx["s_dve"], "scalar": ctx["s_act"], "gpsimd": ctx["s_pool"]})
                wv_t = [0] * 16
                t_m = 0
                for hc in range(16):
                    dc, hh = hc // 2, hc % 2
                    if hc >= 1:
                        p.wait("sync", s_dve, wv_t[hc - 1])
                        p.wait("sync", s_l[0], t_m)
                    t_m = p.inc("sync", s_l[0], lambda e, dc=dc, hh=hh: e.dma_start(
                        wm[:, :], w["l1_Wv"][128 * dc:128 * (dc + 1), 1024 * hh:1024 * (hh + 1)]), 16)
                    p.wait("vector", s_l[0], t_m)
                    wv_t[hc] = p.inc("vector", s_dve, lambda e, dc=dc, hh=hh: e.tensor_scalar(
                        wbuf[:, 2048 * dc + 1024 * hh:2048 * dc + 1024 * (hh + 1)], wm[:, :],
                        g_sb[:, dc:dc + 1], None, ALU.mult))
                ln_war_act = 0
                ln_war_pe = 0
                pst_free = 0
                rms_st = [0] * TPS
                v_done = [0] * TPS
                quk_done = [0] * TPS
                t_ln_prev = 0
                x_ld = 0
                for tt in range(TPS):
                    p.wait("sync", s_dve, t_ln_prev)
                    p.wait("sync", s_l[1], x_ld)
                    x_ld = p.inc("sync", s_l[1], lambda e, tt=tt: e.dma_start(
                        xt[:, :], x_src[128 * (tile0 + tt):128 * (tile0 + tt + 1), :]), 16)
                    p.wait("vector", s_l[1], x_ld)
                    t_ln, t_sq_ = emit_ln_tile(p, ctx, xt[:, :], ln_bf[:, :],
                                               scr, scr2, stat,
                                               (1.0 + 1e-6) / D, 1e-11,
                                               war_act=ln_war_act, war_pe=ln_war_pe)
                    ln_war_act = t_sq_
                    t_ln_prev = t_ln
                    p.wait("tensor", s_dve, t_ln)
                    p.wait("tensor", s_dve, pst_free)
                    for c in range(8):
                        fn = (lambda e, c=c: e.transpose(pst[:, 128 * c:128 * (c + 1)],
                                                         ln_bf[:, 128 * c:128 * (c + 1)], ident[:, :]))
                        if c < 7:
                            p.run("tensor", fn)
                        else:
                            t_tr = p.inc("tensor", s_pe, fn)
                    ln_war_pe = t_tr
                    p.wait("vector", s_pe, t_tr)
                    if tt >= 1:
                        p.wait("vector", s_st[0], rms_st[tt - 1])
                    t_ev = p.inc("vector", s_dve, lambda e: e.tensor_copy(
                        rmstile.ap().rearrange("p (c t) -> p c t", c=8),
                        pst.ap().rearrange("p (c a) -> p c a", c=8)))
                    pst_free = t_ev
                    p.wait("sync", s_dve, t_ev)
                    if tt >= 1:
                        p.wait("sync", s_st[0], rms_st[tt - 1])
                    rms_st[tt] = p.inc("sync", s_st[0], lambda e, tt=tt: e.dma_start(
                        rms_dram.rearrange("c p t -> p c t")[:, :, 128 * tt:128 * (tt + 1)],
                        rmstile.ap().rearrange("p (c t) -> p c t", c=8)), 16)
                    p.wait("tensor", s_dve, t_ev)
                    if tt == 0:
                        p.wait("tensor", s_dve, wv_t[15])
                    if tt >= 1:
                        p.wait("tensor", s_act, v_done[tt - 1])
                        p.wait("tensor", s_act, quk_done[tt - 1])
                    for ns in range(4):
                        for dc in range(8):
                            fn = (lambda e, ns=ns, dc=dc: e.matmul(
                                psv[:, 512 * ns:512 * (ns + 1)],
                                rmstile[:, 128 * dc:128 * (dc + 1)],
                                wbuf[:, 2048 * dc + 512 * ns:2048 * dc + 512 * (ns + 1)],
                                start=(dc == 0), stop=(dc == 7)))
                            if ns == 3 and dc == 7:
                                t_v = p.inc("tensor", s_pe, fn)
                            else:
                                p.run("tensor", fn)
                    for dc in range(8):
                        p.run("tensor", lambda e, dc=dc: e.matmul(
                            psq[:, :], Wq_sb[:, 128 * dc:128 * (dc + 1)],
                            rmstile[:, 128 * dc:128 * (dc + 1)],
                            start=(dc == 0), stop=(dc == 7)))
                    for dc in range(8):
                        fn = (lambda e, dc=dc: e.matmul(
                            psk[:, :], Wk_sb[:, 128 * dc:128 * (dc + 1)],
                            rmstile[:, 128 * dc:128 * (dc + 1)],
                            start=(dc == 0), stop=(dc == 7)))
                        if dc == 7:
                            t_qk = p.inc("tensor", s_pe, fn)
                        else:
                            p.run("tensor", fn)
                    p.wait("scalar", s_pe, t_v)
                    v_done[tt] = p.inc("scalar", s_act, lambda e, tt=tt: e.activation(
                        v_sb[:, VX * tt:VX * (tt + 1)], psv[:, :], AF.Copy))
                    p.wait("scalar", s_pe, t_qk)
                    p.run("scalar", lambda e, tt=tt: e.activation(
                        qu_sb[:, 128 * tt:128 * (tt + 1)], psq[:, :], AF.Identity,
                        bias=xlu_sb[:, 0:1]))
                    p.run("scalar", lambda e, tt=tt: e.activation(
                        qv_sb[:, 128 * tt:128 * (tt + 1)], psq[:, :], AF.Identity,
                        bias=xlv_sb[:, 0:1]))
                    quk_done[tt] = p.inc("scalar", s_act, lambda e, tt=tt: e.activation(
                        k_sb[:, 128 * tt:128 * (tt + 1)], psk[:, :], AF.Copy))
                p.wait("scalar", s_act, quk_done[TPS - 1])
                p.realize(nc)
            rms_st_last = rms_st[TPS - 1]

            _sc1.__exit__(None, None, None)
            _sc2 = nc.named_scope(f"l1a_1b{sq}"); _sc2.__enter__()
            with (
                nc.sbuf_tensor(f"l1a_rsl0{sq}", [128, 8 * 512], BF16, side="right") as rs0,
                nc.sbuf_tensor(f"l1a_wm2{sq}", [128, 1024], F32, side="right") as wm2,
                nc.psum_tensor(f"l1a_psg{sq}", [128, 1024], F32) as psg,
            ):
                p = Plan(chain={"vector": ctx["s_dve"], "scalar": ctx["s_act"], "gpsimd": ctx["s_pool"]})
                rsl = [rs0, rs0]
                wg_t = [0] * 16
                t_m = 0
                for hc in range(16):
                    dc, hh = hc // 2, hc % 2
                    if hc >= 1:
                        p.wait("sync", s_dve, wg_t[hc - 1])
                        p.wait("sync", s_l[0], t_m)
                    t_m = p.inc("sync", s_l[0], lambda e, dc=dc, hh=hh: e.dma_start(
                        wm2[:, :], w["l1_Wg"][128 * dc:128 * (dc + 1), 1024 * hh:1024 * (hh + 1)]), 16)
                    p.wait("vector", s_l[0], t_m)
                    wg_t[hc] = p.inc("vector", s_dve, lambda e, dc=dc, hh=hh: e.tensor_scalar(
                        wbuf[:, 2048 * dc + 1024 * hh:2048 * dc + 1024 * (hh + 1)], wm2[:, :],
                        g_sb[:, dc:dc + 1], None, ALU.mult))
                sg_evac = {}
                rs_ld = [0] * 4
                p.wait("sync", s_st[0], rms_st_last)
                for ts4 in range(4):
                    if ts4 >= 1:
                        p.wait("sync", s_pe, t_sg)
                        p.wait("sync", s_l[1], rs_ld[ts4 - 1])
                    rs_ld[ts4] = p.inc("sync", s_l[1], lambda e, ts4=ts4: e.dma_start(
                        rsl[ts4 % 2].ap().rearrange("p (c t) -> p c t", c=8),
                        rms_dram.rearrange("c p t -> p c t")[:, :, 512 * ts4:512 * (ts4 + 1)]), 16)
                    for vc in range(16):
                        gi = ts4 * 16 + vc
                        bsel = gi % 2
                        if gi >= 2:
                            p.wait("tensor", s_act, sg_evac[gi - 2])
                        p.wait("tensor", s_dve, wg_t[15])
                        p.wait("tensor", s_l[1], rs_ld[ts4])
                        for dc in range(8):
                            fn = (lambda e, vc=vc, dc=dc, bsel=bsel: e.matmul(
                                psg[:, 512 * bsel:512 * (bsel + 1)],
                                wbuf[:, 2048 * dc + 128 * vc:2048 * dc + 128 * (vc + 1)],
                                rsl[0][:, 512 * dc:512 * (dc + 1)] if True else None,
                                start=(dc == 0), stop=(dc == 7)))
                            fn = (lambda e, vc=vc, dc=dc, bsel=bsel, ts4=ts4: e.matmul(
                                psg[:, 512 * bsel:512 * (bsel + 1)],
                                wbuf[:, 2048 * dc + 128 * vc:2048 * dc + 128 * (vc + 1)],
                                rsl[ts4 % 2][:, 512 * dc:512 * (dc + 1)],
                                start=(dc == 0), stop=(dc == 7)))
                            if dc == 7:
                                t_sg = p.inc("tensor", s_pe, fn)
                            else:
                                p.run("tensor", fn)
                        p.wait("scalar", s_pe, t_sg)
                        sg_evac[gi] = p.inc("scalar", s_act, lambda e, vc=vc, ts4=ts4, bsel=bsel: e.activation(
                            sg_sb[:, T * vc + 512 * ts4:T * vc + 512 * (ts4 + 1)],
                            psg[:, 512 * bsel:512 * (bsel + 1)], AF.Silu))
                p.wait("scalar", s_act, sg_evac[63])
                p.realize(nc)

            _sc2.__exit__(None, None, None)
            p = Plan(chain={"vector": ctx["s_dve"], "scalar": ctx["s_act"], "gpsimd": ctx["s_pool"]})
            wo_t = 0
            for vc in range(16):
                wo_t = p.inc("gpsimd", s_w, lambda e, vc=vc: e.dma_start(
                    wbuf[:, 1024 * vc:1024 * (vc + 1)], w["l1_Wo"][128 * vc:128 * (vc + 1), :]), 16)
            p.wait("gpsimd", s_w, wo_t)
            p.realize(nc)

            _sc3 = nc.named_scope(f"l1a_2{sq}"); _sc3.__enter__()
            with (
                nc.sbuf_tensor(f"l1a_s2w{sq}", [128, 512], F32, side="right") as s2w_sb,
                nc.sbuf_tensor(f"l1a_sc{sq}", [128, 2048], F32, side="right") as scores_sb,
                nc.sbuf_tensor(f"l1a_at{sq}", [128, 2048], BF16, side="right") as attn_sb,
                nc.sbuf_tensor(f"l1a_atT{sq}", [128, 16 * 128], BF16, side="right") as attnT_sb,
                nc.psum_tensor(f"l1a_psA{sq}", [128, 1024], F32) as psA,
                nc.psum_tensor(f"l1a_psW{sq}", [128, 2048], F32) as psW,
                nc.psum_tensor(f"l1a_psT0{sq}", [128, 512], BF16) as psT0,
                nc.psum_tensor(f"l1a_psT1{sq}", [128, 512], BF16) as psT1,
            ):
                wvg_sb = attn_sb
                p = Plan(chain={"vector": ctx["s_dve"], "scalar": ctx["s_act"], "gpsimd": ctx["s_pool"]})
                s2_read_done = [0, 0]
                out_evac = [0] * TPS
                stored = [0] * TPS
                psA_free_dve = 0
                psA_free_act = 0
                psW_free_dve = 0
                exp_prev = 0
                bd_prev = 0
                s2w_prev = 0
                t_lastT = 0
                xq_ld_l = [0] * (TPS + 1)
                xq_ld_l[0] = p.inc("sync", s_l[1], lambda e: e.dma_start(
                    xt[:, :], x_src[128 * tile0:128 * (tile0 + 1), :]), 16)
                for qt in range(TPS):
                    W = 128 * (qt + 1)
                    nhalf = (W + 1023) // 1024
                    bsel = qt % 2
                    xq_ld = xq_ld_l[qt]
                    t_s2w = 0
                    if qt >= 2:
                        p.wait("scalar", s_s2, s2_read_done[bsel])
                    for s5 in range(qt + 1):
                        pass
                    nsl5 = (W + 511) // 512
                    for s5 in range(nsl5):
                        e0 = 2048 - W + 512 * s5
                        sw = min(512, W - 512 * s5)
                        p.wait("tensor", s_dve, psA_free_dve)
                        p.wait("tensor", s_act, psA_free_act)
                        t_s2 = p.inc("tensor", s_pe, lambda e, qt=qt, e0=e0, sw=sw: e.matmul(
                            psA[:, 0:sw],
                            qv_sb[:, 128 * qt:128 * (qt + 1)],
                            rrel_sb[:, e0:e0 + sw],
                            start=True, stop=True))
                        p.wait("scalar", s_pe, t_s2)
                        p.wait("scalar", s_s2, s2w_prev)
                        t_s2e = p.inc("scalar", s_act, lambda e, sw=sw: e.activation(
                            s2w_sb[:, 0:sw], psA[:, 0:sw], AF.Copy))
                        psA_free_act = t_s2e
                        p.wait("scalar", s_act, t_s2e)
                        t_s2w = s2w_prev = p.inc("scalar", s_s2, lambda e, bsel=bsel, e0=e0, sw=sw: e.dma_start(
                            s2buf[bsel, :, e0:e0 + sw], s2w_sb[:, 0:sw]), 16)
                        
                    p.wait("scalar", s_s2, t_s2w)
                    p.wait("scalar", s_dve, bd_prev)
                    p.wait("scalar", s_act, exp_prev)
                    rd_ap = bass.AP(s2buf.tensor, bsel * 128 * 2176 + 2047 - 128 * qt,
                                    [[2175, 128], [1, W]])
                    t_s2r = p.inc("scalar", s_s2, lambda e, rd_ap=rd_ap, W=W: e.dma_start(
                        scores_sb[:, 0:W], rd_ap), 16)
                    s2_read_done[bsel] = t_s2r
                    for hf in range(nhalf):
                        c0 = 1024 * hf
                        cw = min(1024, W - c0)
                        ns = (cw + 511) // 512
                        p.wait("tensor", s_dve, psA_free_dve)
                        p.wait("tensor", s_act, psA_free_act)
                        for s5 in range(ns):
                            sw = min(512, cw - 512 * s5)
                            fn = (lambda e, qt=qt, c0=c0, s5=s5, sw=sw: e.matmul(
                                psA[:, 512 * s5:512 * s5 + sw],
                                qu_sb[:, 128 * qt:128 * (qt + 1)],
                                k_sb[:, c0 + 512 * s5:c0 + 512 * s5 + sw],
                                start=True, stop=True))
                            if s5 == ns - 1:
                                t_ac = p.inc("tensor", s_pe, fn)
                            else:
                                p.run("tensor", fn)
                        p.wait("vector", s_pe, t_ac)
                        p.wait("vector", s_s2, t_s2r)
                        psA_free_dve = p.inc("vector", s_dve, lambda e, c0=c0, cw=cw: e.tensor_tensor(
                            scores_sb[:, c0:c0 + cw], psA[:, 0:cw], scores_sb[:, c0:c0 + cw], ALU.add))
                    taps = ctx.get("l1taps")
                    t_msk = p.inc("vector", s_dve, lambda e, W=W: e.tensor_tensor(
                        scores_sb[:, W - 128:W], scores_sb[:, W - 128:W], negstep_sb[:, :], ALU.add))
                    p.wait("vector", s_dve, t_msk)
                    t_nm = p.inc("vector", s_dve, lambda e, W=W: e.reduce_max(
                        stat[:, 0:1], scores_sb[:, 0:W], axis=AX.X, negate=True))
                    bd_prev = t_nm
                    p.wait("scalar", s_dve, t_nm)
                    p.wait("scalar", s_pe, t_lastT)
                    exp_prev = t_exp = p.inc("scalar", s_act, lambda e, W=W: e.activation(
                        attn_sb[:, 0:W], scores_sb[:, 0:W], AF.Exp, bias=stat[:, 0:1],
                        accum_out=stat[:, 1:2]))
                    p.wait("vector", s_act, t_exp)
                    t_rs = p.inc("vector", s_dve, lambda e: e.reciprocal(stat[:, 2:3], stat[:, 1:2]))
                    p.wait("vector", s_dve, t_rs)
                    t_an = p.inc("vector", s_dve, lambda e, W=W: e.tensor_scalar(
                        attn_sb[:, 0:W], attn_sb[:, 0:W], stat[:, 2:3], None, ALU.mult))

                    p.wait("tensor", s_dve, t_an)
                    p.wait("tensor", s_dve, psW_free_dve)
                    t_wv = 0
                    psTb = [psT0, psT1]
                    evac_all = {}
                    for j in range(qt + 1):
                        bk = j % 2
                        sl4 = (j // 2) % 4
                        if j >= 2:
                            p.wait("tensor", s_dve, evac_all[j - 2])
                        fn = (lambda e, j=j, bk=bk, sl4=sl4: e.transpose(
                            psTb[bk][:, 128 * sl4:128 * (sl4 + 1)],
                            attn_sb[:, 128 * j:128 * (j + 1)], ident[:, :]))
                        t_lastT = p.inc("tensor", s_pe, fn)
                        p.wait("vector", s_pe, t_lastT)
                        evac_all[j] = p.inc("vector", s_dve,
                            lambda e, j=j, bk=bk, sl4=sl4: e.tensor_copy(
                                attnT_sb[:, 128 * j:128 * (j + 1)],
                                psTb[bk][:, 128 * sl4:128 * (sl4 + 1)]))
                    p.wait("tensor", s_dve, evac_all[qt])
                    for vc in range(16):
                        for j in range(qt + 1):
                            fn = (lambda e, j=j, vc=vc, qt=qt: e.matmul(
                                psW[:, 128 * vc:128 * (vc + 1)],
                                v_sb[:, VX * j + 128 * vc:128 * (vc + 1) + VX * j],
                                attnT_sb[:, 128 * j:128 * (j + 1)],
                                start=(j == 0), stop=(j == qt)))
                            if vc == 15 and j == qt:
                                t_wv = p.inc("tensor", s_pe, fn)
                            else:
                                p.run("tensor", fn)
                    p.wait("vector", s_pe, t_wv)
                    for sgrp in range(4):
                        in1 = sg_sb.ap().rearrange("p (c t) -> p c t", c=16)[:, 4 * sgrp:4 * (sgrp + 1),
                                                                            128 * qt:128 * (qt + 1)]
                        in0 = psW.ap().rearrange("p (c t) -> p c t", c=16)[:, 4 * sgrp:4 * (sgrp + 1), :]
                        out = wvg_sb.ap().rearrange("p (c t) -> p c t", c=16)[:, 4 * sgrp:4 * (sgrp + 1), :]
                        fn = (lambda e, in0=in0, in1=in1, out=out: e.tensor_tensor(out, in0, in1, ALU.mult))
                        if sgrp == 3:
                            t_wvg = p.inc("vector", s_dve, fn)
                        else:
                            p.run("vector", fn)
                    if taps is not None and sq == 0 and qt == 1:
                        p.wait("gpsimd", s_dve, t_wvg)
                        tq = p.inc("gpsimd", s_w, lambda e: e.dma_start(taps["wvg"][:, :], wvg_sb[:, 0:2048]), 16)
                        tq = p.inc("gpsimd", s_w, lambda e: e.dma_start(taps["v1"][:, :], v_sb[:, VX:VX + 2048]), 16)
                        p.wait("gpsimd", s_w, tq)
                    p.wait("tensor", s_dve, t_wvg)
                    for vc in range(16):
                        for hfp in range(2):
                            fn = (lambda e, vc=vc, hfp=hfp: e.matmul(
                                psW[:, 512 * hfp:512 * (hfp + 1)],
                                wvg_sb[:, 128 * vc:128 * (vc + 1)],
                                wbuf[:, 1024 * vc + 512 * hfp:1024 * vc + 512 * (hfp + 1)],
                                start=(vc == 0), stop=(vc == 15)))
                            if vc == 15 and hfp == 1:
                                t_pj = p.inc("tensor", s_pe, fn)
                            else:
                                p.run("tensor", fn)
                    p.wait("vector", s_pe, t_pj)
                    p.wait("vector", s_l[1], xq_ld)
                    if qt >= 1:
                        p.wait("vector", s_st[1], stored[qt - 1])
                    out_evac[qt] = psW_free_dve = p.inc("vector", s_dve,
                                                        lambda e: e.tensor_tensor(
                                                            xout[:, :], psW[:, 0:1024], xt[:, :], ALU.add))
                    p.wait("sync", s_dve, out_evac[qt])
                    if qt >= 1:
                        p.wait("sync", s_st[1], stored[qt - 1])
                    stored[qt] = p.inc("sync", s_st[1], lambda e, qt=qt: e.dma_start(
                        x_dst[128 * (tile0 + qt):128 * (tile0 + qt + 1), :], xout[:, :]), 16)
                    if qt + 1 < TPS:
                        p.wait("sync", s_l[1], xq_ld_l[qt])
                        xq_ld_l[qt + 1] = p.inc("sync", s_l[1], lambda e, qt=qt: e.dma_start(
                            xt[:, :], x_src[128 * (tile0 + qt + 1):128 * (tile0 + qt + 2), :]), 16)
                p.wait("sync", s_st[1], stored[TPS - 1])
                p.realize(nc)
            _sc3.__exit__(None, None, None)
    es_.close()


def kernel(**inputs):
    consts = host_constants()
    w = shard_weights(inputs)
    x = np.asarray(inputs["L0_attn_x"], dtype=np.float32)
    nc = build_nc()
    in_maps = []
    for core in range(NCORES):
        m = {"x": np.ascontiguousarray(x[NSEQ * core:NSEQ * (core + 1)].reshape(NTOK, D))}
        m.update(consts)
        m.update(w)
        in_maps.append(m)
    res = run_bass_kernel_spmd(nc, in_maps, core_ids=list(range(NCORES)))
    out = np.concatenate([r["y"].reshape(NSEQ, T, D) for r in res.results], axis=0)
    return out.astype(np.float32)


# version 16
# speedup vs baseline: 1.0097x; 1.0097x over previous
from contextlib import ExitStack
import numpy as np
import ml_dtypes
import concourse.bass as bass
import concourse.mybir as mybir
from concourse.bass_utils import run_bass_kernel_spmd

F32 = mybir.dt.float32
BF16 = mybir.dt.bfloat16
AF = mybir.ActivationFunctionType
ALU = mybir.AluOpType
AX = mybir.AxisListType

NCORES = 8
B, T, D = 16, 2048, 1024
NSEQ = B // NCORES
NTOK = NSEQ * T
NTT = NTOK // 128
TPS = T // 128
NH, NKV, DH, WINDOW = 16, 4, 64, 512
FF = 4 * D
KX, VX = 128, 2048
NEG = -1e30
BF = ml_dtypes.bfloat16


def host_constants():
    c = {}
    theta = 1.0 / 10000.0 ** (np.arange(0, DH, 2, dtype=np.float64) / DH)
    pos = np.arange(T, dtype=np.float64)
    freqs = np.outer(pos, theta)
    cos = np.concatenate([np.cos(freqs), np.cos(freqs)], -1).astype(np.float32)
    sin = np.concatenate([np.sin(freqs), np.sin(freqs)], -1).astype(np.float32)
    sinA = np.concatenate([-sin[:, 0:32], sin[:, 32:64]], -1).astype(np.float32)
    c["cos16"] = np.ascontiguousarray(np.tile(cos, (1, NH)))
    c["sinA16"] = np.ascontiguousarray(np.tile(sinA, (1, NH)))
    c["ident_bf"] = np.eye(128, dtype=BF)
    c["ones_bf"] = np.ones((1, 128), dtype=BF)
    c["ones_f32"] = np.ones((1, 128), dtype=np.float32)
    c["onescol_f32"] = np.ones((128, 1), dtype=np.float32)
    p = np.arange(128)[:, None]
    cq = np.arange(128)[None, :]
    negstep_gt = np.where(p > cq, np.float32(NEG), np.float32(0)).astype(BF)
    negstep_le = np.where(p <= cq, np.float32(NEG), np.float32(0)).astype(BF)
    c["negstep_gt4"] = np.ascontiguousarray(np.tile(negstep_gt, (1, 4)))
    c["negstep_le4"] = np.ascontiguousarray(np.tile(negstep_le, (1, 4)))
    c["l1_negstep"] = np.where(cq > p, np.float32(NEG), np.float32(0)).astype(np.float32)
    LAM = 10000.0
    inv_lams = 1.0 / LAM ** (np.arange(0, D, 2, dtype=np.float64) / D)
    pre = np.arange(T, dtype=np.float64)[:, None] * inv_lams[None, :]
    cat = np.concatenate([np.sin(pre), np.cos(pre)], -1)
    c["catrevT"] = np.ascontiguousarray(cat[::-1].T.astype(np.float32))
    return c


CONST_SPECS = {
    "cos16": ([T, D], F32), "sinA16": ([T, D], F32),
    "ident_bf": ([128, 128], BF16), "ones_bf": ([1, 128], BF16),
    "ones_f32": ([1, 128], F32), "onescol_f32": ([128, 1], F32),
    "negstep_gt4": ([128, 512], BF16), "negstep_le4": ([128, 512], BF16),
    "l1_negstep": ([128, 128], F32),
    "catrevT": ([D, T], F32),
}

WEIGHT_SPECS = {
    "l0_WqT": ([D, D], F32),
    "l0_WkvT": ([D, 512], F32),
    "l0_WprojTh": ([64, 16 * D], F32),
    "l0_W1T": ([D, FF], F32),
    "l0_b1c": ([128, FF // 128], F32),
    "l0_W2T": ([FF, D], F32),
    "l0_b2": ([1, D], F32),
    "l1_gc": ([128, 8], F32),
    "l1_Wq": ([D, KX], F32),
    "l1_Wk": ([D, KX], F32),
    "l1_Wv": ([D, VX], F32),
    "l1_Wg": ([D, VX], F32),
    "l1_Wr": ([D, KX], F32),
    "l1_xlu": ([KX, 1], F32),
    "l1_xlv": ([KX, 1], F32),
    "l1_Wo": ([VX, D], F32),
    "l1f_W1": ([D, FF], F32),
    "l1f_b1c": ([128, FF // 128], F32),
    "l1f_W2": ([FF, D], F32),
    "l1f_b2": ([1, D], F32),
}


def shard_weights(i):
    w = {}
    w["l0_WqT"] = i["L0_attn_Wq"].T
    w["l0_WkvT"] = i["L0_attn_Wkv"].T
    w["l0_WprojTh"] = i["L0_attn_Wproj"].T.reshape(16, 64, D).transpose(1, 0, 2).reshape(64, 16 * D)
    w["l0_W1T"] = i["L0_ffn_W1"].T
    w["l0_b1c"] = i["L0_ffn_b1"].reshape(FF // 128, 128).T
    w["l0_W2T"] = i["L0_ffn_W2"].T
    w["l0_b2"] = i["L0_ffn_b2"].reshape(1, D)
    w["l1_gc"] = i["L1_attn_g_ln"].reshape(8, 128).T
    w["l1_Wq"] = i["L1_attn_W_q"]
    w["l1_Wk"] = i["L1_attn_W_k"]
    w["l1_Wv"] = i["L1_attn_W_v"]
    w["l1_Wg"] = i["L1_attn_W_g"]
    w["l1_Wr"] = i["L1_attn_W_r"]
    w["l1_xlu"] = i["L1_attn_xl_u"].reshape(KX, 1)
    w["l1_xlv"] = i["L1_attn_xl_v"].reshape(KX, 1)
    w["l1_Wo"] = i["L1_attn_W_o"]
    w["l1f_W1"] = i["L1_ffn_W1"]
    w["l1f_b1c"] = i["L1_ffn_b1"].reshape(FF // 128, 128).T
    w["l1f_W2"] = i["L1_ffn_W2"]
    w["l1f_b2"] = i["L1_ffn_b2"].reshape(1, D)
    return {k: np.ascontiguousarray(np.asarray(v, dtype=np.float32)) for k, v in w.items()}


ENGS = ("sync", "scalar", "vector", "tensor", "gpsimd")


class Ctr:
    def __init__(self, sem):
        self.sem = sem
        self.val = 0


class Plan:

    def __init__(self, chain=None):
        self.q = {e: [] for e in ENGS}
        self.chain = chain or {}

    def on(self, eng, fn):
        self.q[eng].append(fn)

    def wait(self, eng, ctr, val=None):
        v = int(ctr.val if val is None else val)
        if v > 0:
            self.q[eng].append(lambda e, s=ctr.sem, v=v: e.wait_ge(s, v))

    def _prewait(self, eng):
        c = self.chain.get(eng)
        if c is not None and c.val > 0:
            self.q[eng].append(lambda e, s=c.sem, v=c.val: e.wait_ge(s, v))

    def inc(self, eng, ctr, fn, n=1):
        self._prewait(eng)
        ctr.val += n
        tgt = ctr.val

        def go(e, fn=fn, s=ctr.sem, n=n):
            inst = fn(e)
            inst.then_inc(s, n)
        self.q[eng].append(go)
        return tgt

    def run(self, eng, fn):
        c = self.chain.get(eng)
        if c is None:
            self.q[eng].append(fn)
            return None
        return self.inc(eng, c, fn)

    def realize(self, nc):
        with nc.Block() as blk:
            for eng in ENGS:
                ops = self.q[eng]
                if not ops:
                    continue
                getattr(blk, eng)(lambda e, ops=ops: [f(e) for f in ops])


def emit_ln_tile(p, ctx, x_ap, out_ap, scr, scr2, stat, var_scale, var_bias,
                 war_act=0, war_pe=0):
    s_dve, s_act, s_pe = ctx["s_dve"], ctx["s_act"], ctx["s_pe"]
    v = "vector"
    s = "scalar"
    t0 = p.inc(v, s_dve, lambda e: e.reduce_sum(stat[:, 0:1], x_ap, axis=AX.X))
    p.wait(v, s_dve, t0)
    t1 = p.inc(v, s_dve, lambda e: e.tensor_scalar(stat[:, 1:2], stat[:, 0:1], 1.0 / D, None, ALU.mult))
    p.wait(v, s_act, war_act)
    p.wait(v, s_dve, t1)
    t_sub = p.inc(v, s_dve, lambda e: e.tensor_scalar(scr[:, :], x_ap, stat[:, 1:2], None, ALU.subtract))
    p.wait(s, s_dve, t_sub)
    t_sq = p.inc(s, s_act, lambda e: e.activation(scr2[:, :], scr[:, :], AF.Square,
                                                 accum_out=stat[:, 2:3]))
    p.wait(v, s_act, t_sq)
    t2 = p.inc(v, s_dve, lambda e: e.tensor_scalar(stat[:, 3:4], stat[:, 2:3], var_scale, var_bias,
                                                   ALU.mult, ALU.add))
    p.wait(v, s_dve, t2)
    t_rc = p.inc(v, s_dve, lambda e: e.reciprocal(stat[:, 4:5], stat[:, 3:4]))
    p.wait(s, s_dve, t_rc)
    t_sq2 = p.inc(s, s_act, lambda e: e.activation(stat[:, 5:6], stat[:, 4:5], AF.Sqrt))
    p.wait(v, s_act, t_sq2)
    p.wait(v, s_pe, war_pe)
    t_out = p.inc(v, s_dve, lambda e: e.tensor_scalar(out_ap, scr[:, :], stat[:, 5:6], None, ALU.mult))
    return t_out, t_sq


def x_slice_dram(x_ap, sl, ntiles=2):
    r0 = 128 * ntiles * sl
    return x_ap[r0:r0 + 128 * ntiles, :].rearrange("(a p) d -> p a d", p=128)


def sb_3d(sb_ap, ntiles=2):
    return sb_ap.rearrange("p (a d) -> p a d", a=ntiles)


def emit_ffn_phase(nc, ctx, *, x_src, x_dst, W1_sb, W2_sb, b1_sb, b2row_sb,
                   act_func, name, taps=None, nsl=None):
    s_x, s_pe, s_dve, s_act = ctx["s_x"], ctx["s_pe"], ctx["s_dve"], ctx["s_act"]
    s_l = ctx["s_l"]
    s_st = ctx["s_st"]
    ones_bf = ctx["ones_bf"]
    ident = ctx["ident_bf"]
    NSL = NTT // 2 if nsl is None else nsl

    with (
        nc.sbuf_tensor(f"{name}_x0", [128, 2 * D], F32) as xb0,
        nc.sbuf_tensor(f"{name}_x1", [128, 2 * D], F32) as xb1,
        nc.sbuf_tensor(f"{name}_scr", [128, D], F32) as scr,
        nc.sbuf_tensor(f"{name}_scr2", [128, D], F32) as scr2,
        nc.sbuf_tensor(f"{name}_stat", [128, 8], F32) as stat,
        nc.sbuf_tensor(f"{name}_lnbf0", [128, D], BF16) as lnbf0,
        nc.sbuf_tensor(f"{name}_lnbf1", [128, D], BF16) as lnbf1,
        nc.sbuf_tensor(f"{name}_lnbf2", [128, D], BF16) as lnbf2,
        nc.sbuf_tensor(f"{name}_lnbf3", [128, D], BF16) as lnbf3,
        nc.sbuf_tensor(f"{name}_lnfm0", [128, 8 * 256], BF16) as lnfm0,
        nc.sbuf_tensor(f"{name}_lnfm1", [128, 8 * 256], BF16) as lnfm1,
        nc.sbuf_tensor(f"{name}_h", [128, 32 * 256], BF16) as h_sb,
        nc.sbuf_tensor(f"{name}_out", [128, 2 * 2 * D], F32) as out_f32,
        nc.psum_tensor(f"{name}_pst", [128, 1024], BF16) as ps_t,
        nc.psum_tensor(f"{name}_ph0", [128, 256], F32) as ph0,
        nc.psum_tensor(f"{name}_ph1", [128, 256], F32) as ph1,
        nc.psum_tensor(f"{name}_po0", [128, 1024], F32) as po0,
        nc.psum_tensor(f"{name}_po1", [128, 1024], F32) as po1,
    ):
        p = Plan(chain={"vector": ctx["s_dve"], "scalar": ctx["s_act"], "gpsimd": ctx["s_pool"]})
        ph = [ph0, ph1]
        po = [po0, po1]
        xb = [xb0, xb1]
        lnbf = [[lnbf0, lnbf1], [lnbf2, lnbf3]]
        lnfm = [lnfm0, lnfm1]

        x_loaded = [0] * NSL
        out_evac = [[0, 0] for _ in range(NSL)]
        stored = [0] * NSL
        h_evac = {}
        lnfm_ready = [0] * NSL
        o_used = [[0, 0] for _ in range(NSL)]
        pst_free = 0
        ln_war_act = 0
        w_tr = {}
        w_mm1 = [0, 0]
        t_ln_tile = {}
        t_mm2_sl = [0] * NSL

        def emit_load(sl):
            if sl >= NSL:
                return
            if sl >= 2:
                p.wait("sync", s_dve, out_evac[sl - 2][1])
                p.wait("sync", s_l[sl % 2], x_loaded[sl - 2])
            x_loaded[sl] = p.inc("sync", s_l[sl % 2],
                                 lambda e, sl=sl: e.dma_start(sb_3d(xb[sl % 2].ap()), x_slice_dram(x_src, sl)), 16)
        emit_load(0)
        emit_load(1)

        def s1_ln(sl):
            nonlocal ln_war_act
            b = sl % 2
            for tt in range(2):
                p.wait("vector", s_l[b], x_loaded[sl])
                t_ln, t_sq = emit_ln_tile(p, ctx, xb[b][:, D * tt:D * (tt + 1)],
                                          lnbf[b][tt][:, :],
                                          scr, scr2, stat, 1.0 / D, 1e-5,
                                          war_act=ln_war_act, war_pe=w_tr.get((b, tt), 0))
                ln_war_act = t_sq
                t_ln_tile[(sl, tt)] = t_ln

        def s1_T(sl):
            nonlocal pst_free
            b = sl % 2
            for tt in range(2):
                p.wait("tensor", s_dve, t_ln_tile[(sl, tt)])
                p.wait("tensor", s_dve, pst_free)
                for c in range(8):
                    fn = (lambda e, c=c, b=b, tt=tt: e.transpose(ps_t[:, 128 * c:128 * (c + 1)],
                                                                 lnbf[b][tt][:, 128 * c:128 * (c + 1)], ident[:, :]))
                    if c < 7:
                        p.run("tensor", fn)
                    else:
                        t_tr = p.inc("tensor", s_pe, fn)
                w_tr[(b, tt)] = t_tr
                p.wait("vector", s_pe, t_tr)
                p.wait("vector", s_pe, w_mm1[b])
                dst = lnfm[b].ap().rearrange("p (c a) -> p c a", c=8, a=256)[:, :, 128 * tt:128 * (tt + 1)]
                t_ev = p.inc("vector", s_dve,
                             lambda e, dst=dst: e.tensor_copy(dst, ps_t.ap().rearrange("p (c a) -> p c a", c=8, a=128)))
                pst_free = t_ev
            lnfm_ready[sl] = pst_free

        def s2_mm(sl):
            b = sl % 2
            p.wait("tensor", s_dve, lnfm_ready[sl])
            for fc in range(32):
                if (sl, fc - 2) in h_evac:
                    p.wait("tensor", s_act, h_evac[(sl, fc - 2)])
                elif sl > 0:
                    p.wait("tensor", s_act, h_evac[(sl - 1, 30 + fc)])
                for dc in range(8):
                    fn = (lambda e, fc=fc, dc=dc, b=b:
                          e.matmul(ph[fc % 2][:, :],
                                   W1_sb[:, FF * dc + 128 * fc:FF * dc + 128 * (fc + 1)],
                                   lnfm[b][:, 256 * dc:256 * (dc + 1)],
                                   start=(dc == 0), stop=(dc == 7)))
                    if dc < 7:
                        p.run("tensor", fn)
                    else:
                        t_mm = p.inc("tensor", s_pe, fn)
                p.wait("scalar", s_pe, t_mm)
                h_evac[(sl, fc)] = p.inc("scalar", s_act,
                                         lambda e, fc=fc: e.activation(h_sb[:, 256 * fc:256 * (fc + 1)],
                                                                       ph[fc % 2][:, :], act_func,
                                                                       bias=b1_sb[:, fc:fc + 1]))
            w_mm1[b] = t_mm
            for tt in range(2):
                p.wait("tensor", s_act, h_evac[(sl, 31)])
                if sl > 0:
                    p.wait("tensor", s_dve, o_used[sl - 1][tt])
                for half in range(2):
                    fn0 = (lambda e, tt=tt, half=half: e.matmul(po[tt][:, 512 * half:512 * (half + 1)],
                                                                ones_bf[0:1, :],
                                                                b2row_sb[0:1, 512 * half:512 * (half + 1)],
                                                                start=True, stop=False))
                    p.run("tensor", fn0)
                for fc in range(32):
                    for half in range(2):
                        fn = (lambda e, fc=fc, tt=tt, half=half:
                              e.matmul(po[tt][:, 512 * half:512 * (half + 1)],
                                       h_sb[:, 256 * fc + 128 * tt:256 * fc + 128 * (tt + 1)],
                                       W2_sb[:, D * fc + 512 * half:D * fc + 512 * (half + 1)],
                                       start=False, stop=(fc == 31)))
                        if fc == 31 and half == 1:
                            t_mm2 = p.inc("tensor", s_pe, fn)
                        else:
                            p.run("tensor", fn)
                t_mm2_sl[sl] = t_mm2 if tt == 1 else t_mm2_sl[sl]
                p.wait("vector", s_pe, t_mm2)
                if sl >= 2:
                    p.wait("vector", s_st[sl % 2], stored[sl - 2])
                half_off = 2 * D * (sl % 2)
                out_evac[sl][tt] = o_used[sl][tt] = p.inc(
                    "vector", s_dve,
                    lambda e, tt=tt, sl=sl, half_off=half_off:
                    e.tensor_tensor(out_f32[:, half_off + D * tt:half_off + D * (tt + 1)],
                                    po[tt][:, :], xb[sl % 2][:, D * tt:D * (tt + 1)], ALU.add))

        def s2_out(sl):
            p.wait("sync", s_dve, out_evac[sl][1])
            if sl >= 2:
                p.wait("sync", s_st[sl % 2], stored[sl - 2])
            stored[sl] = p.inc("sync", s_st[sl % 2],
                               lambda e, sl=sl, half_off=2 * D * (sl % 2):
                               e.dma_start(x_slice_dram(x_dst, sl),
                                           sb_3d(out_f32[:, half_off:half_off + 2 * D])), 16)
            emit_load(sl + 2)
            if taps is not None and sl == 0:
                b = sl % 2
                p.wait("gpsimd", s_dve, out_evac[sl][1])
                p.inc("gpsimd", ctx["s_w"], lambda e, b=b: e.dma_start(taps["lnfm"][:, :], lnfm[b][:, :]), 16)
                t_tap = p.inc("gpsimd", ctx["s_w"], lambda e: e.dma_start(taps["h"][:, :], h_sb[:, :]), 16)
                p.wait("gpsimd", ctx["s_w"], t_tap)

        for i in range(NSL + 1):
            if i < NSL:
                s1_ln(i)
            if i >= 1:
                s2_mm(i - 1)
            if i < NSL:
                s1_T(i)
            if i >= 1:
                s2_out(i - 1)
        p.wait("sync", s_st[(NSL - 1) % 2], stored[NSL - 1])
        p.wait("sync", s_st[NSL % 2], stored[NSL - 2])
        p.realize(nc)


def emit_ffn_weight_load_plain(nc, ctx, *, W1_dram, W2_dram, b1_dram, b2_dram,
                               W1_sb, W2_sb, b1_sb, b2row_sb, name):
    s_w = ctx["s_w"]
    p = Plan(chain={"vector": ctx["s_dve"], "scalar": ctx["s_act"], "gpsimd": ctx["s_pool"]})
    t = 0
    for dc in range(8):
        t = p.inc("gpsimd", s_w, lambda e, dc=dc: e.dma_start(
            W1_sb[:, FF * dc:FF * (dc + 1)], W1_dram[128 * dc:128 * (dc + 1), :]), 16)
    for fc in range(32):
        t = p.inc("gpsimd", s_w, lambda e, fc=fc: e.dma_start(
            W2_sb[:, D * fc:D * (fc + 1)], W2_dram[128 * fc:128 * (fc + 1), :]), 16)
    t = p.inc("gpsimd", s_w, lambda e: e.dma_start(b1_sb[:, :], b1_dram[:, :]), 16)
    t = p.inc("gpsimd", s_w, lambda e: e.dma_start(b2row_sb[:, :], b2_dram[:, :]), 16)
    p.wait("gpsimd", s_w, t)
    p.realize(nc)


def emit_ffn_weight_ternarize(nc, ctx, *, W1_dram, W2_dram, b1_dram, b2_dram,
                              W1_sb, W2_sb, b1_sb, b2row_sb, name, taps=None):
    s_x, s_w, s_pe, s_dve, s_act, s_pool = (ctx["s_x"], ctx["s_w"], ctx["s_pe"],
                                            ctx["s_dve"], ctx["s_act"], ctx["s_pool"])
    s_l = ctx["s_l"]
    ones_f32 = ctx["ones_f32"]
    onescol = ctx["onescol_f32"]

    with (
        nc.sbuf_tensor(f"{name}_m0", [128, FF], F32) as m0,
        nc.sbuf_tensor(f"{name}_m1", [128, FF], F32) as m1,
        nc.sbuf_tensor(f"{name}_pa", [128, 48], F32) as partials,
        nc.sbuf_tensor(f"{name}_sa", [128, FF], F32) as sa,
        nc.sbuf_tensor(f"{name}_sbb", [128, FF], F32) as sb_,
        nc.sbuf_tensor(f"{name}_thr", [128, 4], F32) as thr,
        nc.sbuf_tensor(f"{name}_tot", [1, 2], F32) as tot,
        nc.psum_tensor(f"{name}_ps", [128, 8], F32) as ps,
    ):
        p = Plan(chain={"vector": ctx["s_dve"], "scalar": ctx["s_act"], "gpsimd": ctx["s_pool"]})
        mb = [m0, m1]
        loads = []
        red_done = []
        nch = 8 + 32
        for ci in range(nch):
            if ci >= 2:
                p.wait("sync", s_dve, red_done[ci - 2])
                p.wait("sync", s_l[ci % 2], loads[ci - 2])
            src = (W1_dram[128 * ci:128 * (ci + 1), :] if ci < 8 else
                   W2_dram[128 * (ci - 8):128 * (ci - 7), :])
            width = FF if ci < 8 else D
            loads.append(p.inc("sync", s_l[ci % 2],
                               lambda e, src=src, ci=ci, width=width: e.dma_start(mb[ci % 2][:, 0:width], src), 16))
            p.wait("vector", s_l[ci % 2], loads[ci])
            red_done.append(p.inc("vector", s_dve,
                                  lambda e, ci=ci, width=width: e.reduce_sum(
                                      partials[:, ci:ci + 1], mb[ci % 2][:, 0:width], axis=AX.X,
                                      apply_absolute_value=True)))
        p.wait("vector", s_dve, red_done[nch - 1])
        p.run("vector", lambda e: e.reduce_sum(partials[:, 40:41], partials[:, 0:8], axis=AX.X))
        t_p = p.inc("vector", s_dve, lambda e: e.reduce_sum(partials[:, 41:42], partials[:, 8:40], axis=AX.X))
        p.wait("vector", s_dve, t_p)
        p.wait("tensor", s_dve, t_p)
        p.run("tensor", lambda e: e.matmul(ps[0:1, 0:1], partials[:, 40:41], onescol[:, 0:1],
                                           start=True, stop=True))
        t_mm = p.inc("tensor", s_pe, lambda e: e.matmul(ps[0:1, 1:2], partials[:, 41:42], onescol[:, 0:1],
                                                        start=True, stop=True))
        p.wait("scalar", s_pe, t_mm)
        p.run("scalar", lambda e: e.activation(tot[0:1, 0:1], ps[0:1, 0:1], AF.Copy,
                                               scale=float(0.5 / (D * FF))))
        t_a = p.inc("scalar", s_act, lambda e: e.activation(tot[0:1, 1:2], ps[0:1, 1:2], AF.Copy,
                                                            scale=float(0.5 / (D * FF))))
        p.wait("vector", s_act, t_a)
        t_mx = p.inc("vector", s_dve, lambda e: e.tensor_scalar(tot[0:1, 0:2], tot[0:1, 0:2], 0.5e-8, None,
                                                                ALU.max))
        p.wait("tensor", s_dve, t_mx)
        p.run("tensor", lambda e: e.matmul(ps[:, 2:3], ones_f32[0:1, :], tot[0:1, 0:1],
                                           start=True, stop=True))
        t_bc = p.inc("tensor", s_pe, lambda e: e.matmul(ps[:, 3:4], ones_f32[0:1, :], tot[0:1, 1:2],
                                                        start=True, stop=True))
        p.wait("vector", s_pe, t_bc)
        p.run("vector", lambda e: e.tensor_copy(thr[:, 0:1], ps[:, 2:3]))
        p.run("vector", lambda e: e.tensor_scalar(thr[:, 1:2], ps[:, 2:3], -1.0, None, ALU.mult))
        p.run("vector", lambda e: e.tensor_copy(thr[:, 2:3], ps[:, 3:4]))
        t_th = p.inc("vector", s_dve, lambda e: e.tensor_scalar(thr[:, 3:4], ps[:, 3:4], -1.0, None, ALU.mult))

        p.wait("sync", s_dve, t_th)
        p.wait("gpsimd", s_dve, t_th)
        tern_done_v = [0] * nch
        tern_done_p = [0] * nch
        loads2 = []
        for ci in range(nch):
            if ci >= 2:
                p.wait("sync", s_dve, tern_done_v[ci - 2])
                p.wait("sync", s_l[ci % 2], loads2[ci - 2])
            src = (W1_dram[128 * ci:128 * (ci + 1), :] if ci < 8 else
                   W2_dram[128 * (ci - 8):128 * (ci - 7), :])
            width = FF if ci < 8 else D
            dst = (W1_sb[:, FF * ci:FF * (ci + 1)] if ci < 8 else
                   W2_sb[:, D * (ci - 8):D * (ci - 7)])
            tcol = 0 if ci < 8 else 2
            loads2.append(p.inc("sync", s_l[ci % 2],
                                lambda e, src=src, ci=ci, width=width: e.dma_start(mb[ci % 2][:, 0:width], src), 16))
            p.wait("vector", s_l[ci % 2], loads2[ci])
            p.wait("gpsimd", s_l[ci % 2], loads2[ci])
            t_a2 = p.inc("vector", s_dve,
                         lambda e, ci=ci, width=width, tcol=tcol: e.tensor_scalar(
                             sa[:, 0:width], mb[ci % 2][:, 0:width], thr[:, tcol:tcol + 1], None, ALU.is_gt))
            tern_done_p[ci] = p.inc("vector", s_dve,
                                    lambda e, ci=ci, width=width, tcol=tcol: e.tensor_scalar(
                                        sb_[:, 0:width], mb[ci % 2][:, 0:width], thr[:, tcol + 1:tcol + 2],
                                        None, ALU.is_lt))
            tern_done_v[ci] = p.inc("vector", s_dve,
                                    lambda e, ci=ci, width=width, dst=dst: e.tensor_tensor(
                                        dst, sa[:, 0:width], sb_[:, 0:width], ALU.subtract))
        if taps is not None:
            p.wait("gpsimd", s_dve, t_th)
            p.inc("gpsimd", s_w, lambda e: e.dma_start(taps["thr"][:, :], thr[:, :]), 16)
            p.inc("gpsimd", s_w, lambda e: e.dma_start(taps["partials"][:, :], partials[:, :]), 16)
        t_b = p.inc("gpsimd", s_w, lambda e: e.dma_start(b1_sb[:, :], b1_dram[:, :]), 16)
        t_b = p.inc("gpsimd", s_w, lambda e: e.dma_start(b2row_sb[:, :], b2_dram[:, :]), 16)
        p.wait("gpsimd", s_w, t_b)
        p.wait("vector", s_dve, tern_done_v[nch - 1])
        p.realize(nc)


def build_nc(phases=("l0a", "l0f", "l1a", "l1f"), debug_mid=False):
    nc = bass.Bass("TRN2", target_bir_lowering=False)
    x_in = nc.dram_tensor("x", [NTOK, D], F32, kind="ExternalInput").ap()
    y_out = nc.dram_tensor("y", [NTOK, D], F32, kind="ExternalOutput").ap()
    consts = {k: nc.dram_tensor(k, sh, dt, kind="ExternalInput").ap()
              for k, (sh, dt) in CONST_SPECS.items()}
    weights = {k: nc.dram_tensor(k, sh, dt, kind="ExternalInput").ap()
               for k, (sh, dt) in WEIGHT_SPECS.items()}
    n_mid = max(len(phases) - 1, 0)
    kind_mid = "ExternalOutput" if debug_mid else "Internal"
    mids = [nc.dram_tensor(f"xmid{j}", [NTOK, D], F32, kind=kind_mid).ap()
            for j in range(n_mid)]
    chain = [x_in] + mids + [y_out]

    with (
        nc.semaphore("s_x") as _sx,
        nc.semaphore("s_w") as _sw,
        nc.semaphore("s_s2") as _ss2,
        nc.semaphore("s_pe") as _spe,
        nc.semaphore("s_dve") as _sdve,
        nc.semaphore("s_act") as _sact,
        nc.semaphore("s_pool") as _spool,
        nc.semaphore("s_l0") as _sl0,
        nc.semaphore("s_l1") as _sl1,
        nc.semaphore("s_st0") as _sst0,
        nc.semaphore("s_st1") as _sst1,
        nc.sbuf_tensor("c_ident", [128, 128], BF16) as ident_sb,
        nc.sbuf_tensor("c_ones_bf", [1, 128], BF16) as ones_bf_sb,
        nc.sbuf_tensor("c_ones_f32", [1, 128], F32) as ones_f32_sb,
        nc.sbuf_tensor("c_onescol", [128, 1], F32) as onescol_sb,
    ):
        ctx = {
            "s_x": Ctr(_sx), "s_w": Ctr(_sw), "s_s2": Ctr(_ss2), "s_pe": Ctr(_spe),
            "s_dve": Ctr(_sdve), "s_act": Ctr(_sact), "s_pool": Ctr(_spool),
            "s_l": [Ctr(_sl0), Ctr(_sl1)], "s_st": [Ctr(_sst0), Ctr(_sst1)],
            "ident_bf": ident_sb, "ones_bf": ones_bf_sb, "ones_f32": ones_f32_sb,
            "onescol_f32": onescol_sb, "consts": consts, "weights": weights,
        }
        p = Plan(chain={"vector": ctx["s_dve"], "scalar": ctx["s_act"], "gpsimd": ctx["s_pool"]})
        t = p.inc("sync", ctx["s_x"], lambda e: e.dma_start(ident_sb[:, :], consts["ident_bf"][:, :]), 16)
        p.wait("sync", ctx["s_x"], t)
        t = p.inc("sync", ctx["s_x"], lambda e: e.dma_start(ones_bf_sb[:, :], consts["ones_bf"][:, :]), 16)
        p.wait("sync", ctx["s_x"], t)
        t = p.inc("sync", ctx["s_x"], lambda e: e.dma_start(ones_f32_sb[:, :], consts["ones_f32"][:, :]), 16)
        p.wait("sync", ctx["s_x"], t)
        t = p.inc("sync", ctx["s_x"], lambda e: e.dma_start(onescol_sb[:, :], consts["onescol_f32"][:, :]), 16)
        p.wait("sync", ctx["s_x"], t)
        p.realize(nc)

        for pi, ph in enumerate(phases):
            x_src, x_dst = chain[pi], chain[pi + 1]
            _scope = nc.named_scope(f"ph_{ph}")
            _scope.__enter__()
            if ph == "l0f":
                with (
                    nc.sbuf_tensor("l0f_W1t", [128, 8 * FF], BF16) as W1_sb,
                    nc.sbuf_tensor("l0f_W2t", [128, 32 * D], BF16) as W2_sb,
                    nc.sbuf_tensor("l0f_b1sb", [128, 32], F32) as b1_sb,
                    nc.sbuf_tensor("l0f_b2sb", [1, D], BF16) as b2row_sb,
                ):
                    emit_ffn_weight_ternarize(nc, ctx, W1_dram=weights["l0_W1T"],
                                              W2_dram=weights["l0_W2T"],
                                              b1_dram=weights["l0_b1c"], b2_dram=weights["l0_b2"],
                                              W1_sb=W1_sb, W2_sb=W2_sb, b1_sb=b1_sb,
                                              b2row_sb=b2row_sb, name="l0t")
                    emit_ffn_phase(nc, ctx, x_src=x_src, x_dst=x_dst,
                                   W1_sb=W1_sb, W2_sb=W2_sb, b1_sb=b1_sb, b2row_sb=b2row_sb,
                                   act_func=AF.Gelu, name="l0f")
            elif ph == "l1f":
                with (
                    nc.sbuf_tensor("l1f_W1sb", [128, 8 * FF], BF16) as W1_sb,
                    nc.sbuf_tensor("l1f_W2sb", [128, 32 * D], BF16) as W2_sb,
                    nc.sbuf_tensor("l1f_b1sb", [128, 32], F32) as b1_sb,
                    nc.sbuf_tensor("l1f_b2sb", [1, D], BF16) as b2row_sb,
                ):
                    emit_ffn_weight_load_plain(nc, ctx, W1_dram=weights["l1f_W1"],
                                               W2_dram=weights["l1f_W2"],
                                               b1_dram=weights["l1f_b1c"], b2_dram=weights["l1f_b2"],
                                               W1_sb=W1_sb, W2_sb=W2_sb, b1_sb=b1_sb,
                                               b2row_sb=b2row_sb, name="l1f")
                    emit_ffn_phase(nc, ctx, x_src=x_src, x_dst=x_dst,
                                   W1_sb=W1_sb, W2_sb=W2_sb, b1_sb=b1_sb, b2row_sb=b2row_sb,
                                   act_func=AF.Relu, name="l1f")
            elif ph == "l0a":
                emit_l0_attn(nc, ctx, x_src=x_src, x_dst=x_dst)
            elif ph == "l1a":
                emit_l1_attn(nc, ctx, x_src=x_src, x_dst=x_dst)
            else:
                raise ValueError(ph)
            _scope.__exit__(None, None, None)
    return nc


def emit_l0_attn(nc, ctx, *, x_src, x_dst):
    s_x, s_w, s_s2, s_pe, s_dve, s_act, s_pool = (ctx["s_x"], ctx["s_w"], ctx["s_s2"],
                                                  ctx["s_pe"], ctx["s_dve"], ctx["s_act"],
                                                  ctx["s_pool"])
    s_l = ctx["s_l"]
    s_st = ctx["s_st"]
    w = ctx["weights"]
    consts = ctx["consts"]
    ident = ctx["ident_bf"]
    ones_f32 = ctx["ones_f32"]
    EPS = 1e-6

    es_ = ExitStack()

    def _sb(name, shape, dt, **kw):
        return es_.enter_context(nc.sbuf_tensor(name, shape, dt, **kw))

    WqT_sb = _sb("l0a_WqT", [128, 8 * 1024], BF16)
    WkvT_sb = _sb("l0a_WkvT", [128, 8 * 512], BF16)
    WpT_sb = _sb("l0a_WpT", [64, 16 * 1024], BF16)
    mgt_sb = _sb("l0a_mgt", [128, 512], BF16)
    mle_sb = _sb("l0a_mle", [128, 512], BF16)
    stat = _sb("l0a_stat", [128, 8], F32)
    xt = _sb("l0a_xt", [128, D], F32)
    xout = _sb("l0a_xout", [128, D], F32)
    q_hfm = _sb("l0a_qh", [128, 8 * 2048], BF16)
    k_fm = _sb("l0a_kf", [128, 2 * 2048], BF16)
    v_aug = _sb("l0a_va", [128, TPS * 260], BF16)
    if True:
        p = Plan(chain={"vector": ctx["s_dve"], "scalar": ctx["s_act"], "gpsimd": ctx["s_pool"]})
        t_w = p.inc("gpsimd", s_w, lambda e: e.dma_start(
            WqT_sb.ap().rearrange("p (c f) -> p c f", c=8),
            w["l0_WqT"].rearrange("(c p) f -> p c f", p=128)), 16)
        t_w = p.inc("gpsimd", s_w, lambda e: e.dma_start(
            WkvT_sb.ap().rearrange("p (c f) -> p c f", c=8),
            w["l0_WkvT"].rearrange("(c p) f -> p c f", p=128)), 16)
        t_w = p.inc("gpsimd", s_w, lambda e: e.dma_start(
            WpT_sb[:, :], w["l0_WprojTh"][:, :]), 16)
        t_w = p.inc("gpsimd", s_w, lambda e: e.dma_start(mgt_sb[:, :], consts["negstep_gt4"][:, :]), 16)
        t_w = p.inc("gpsimd", s_w, lambda e: e.dma_start(mle_sb[:, :], consts["negstep_le4"][:, :]), 16)
        p.wait("gpsimd", s_w, t_w)
        p.realize(nc)

        for sq in range(NSEQ):
            tile0 = TPS * sq
            _scA = nc.named_scope(f"l0a_A{sq}"); _scA.__enter__()
            esA = ExitStack()
            def _sbA(name, shape, dt):
                return esA.enter_context(nc.sbuf_tensor(name + str(sq), shape, dt, side="right"))
            scr = [_sbA("l0a_scrA", [128, D], F32), _sbA("l0a_scrB", [128, D], F32)]
            scr2 = [_sbA("l0a_scr2A", [128, D], F32), _sbA("l0a_scr2B", [128, D], F32)]
            ln_bf = [_sbA("l0a_lnbfA", [128, D], BF16), _sbA("l0a_lnbfB", [128, D], BF16)]
            lnfm = [_sbA("l0a_lnfmA", [128, D], BF16), _sbA("l0a_lnfmB", [128, D], BF16)]
            qsb = [_sbA("l0a_qsbA", [128, D], F32), _sbA("l0a_qsbB", [128, D], F32)]
            q_rope = [_sbA("l0a_qrA", [128, D], BF16), _sbA("l0a_qrB", [128, D], BF16)]
            k_rope = [_sbA("l0a_krA", [128, 256], BF16), _sbA("l0a_krB", [128, 256], BF16)]
            qn = [_sbA("l0a_qnA", [128, 64], F32), _sbA("l0a_qnB", [128, 64], F32)]
            xtb = [_sbA("l0a_xtA", [128, D], F32), _sbA("l0a_xtB", [128, D], F32)]
            tq = _sbA("l0a_tq", [128, D], F32)
            tq2 = _sbA("l0a_tq2", [128, D], F32)
            ct0 = _sbA("l0a_ct0", [128, D], F32)
            ct1 = _sbA("l0a_ct1", [128, D], F32)
            st0 = _sbA("l0a_st0", [128, D], F32)
            st1 = _sbA("l0a_st1", [128, D], F32)
            pst = esA.enter_context(nc.psum_tensor(f"l0a_pst{sq}", [128, 1024], BF16))
            psq = esA.enter_context(nc.psum_tensor(f"l0a_psq{sq}", [128, 1024], F32))
            pskv = esA.enter_context(nc.psum_tensor(f"l0a_pskv{sq}", [128, 512], F32))
            psT2a = esA.enter_context(nc.psum_tensor(f"l0a_pTa{sq}", [128, 128], BF16))
            psT2b = esA.enter_context(nc.psum_tensor(f"l0a_pTb{sq}", [128, 128], BF16))
            if True:
                p = Plan(chain={"vector": ctx["s_dve"], "scalar": ctx["s_act"], "gpsimd": ctx["s_pool"]})
                ctb = [ct0, ct1]
                stb = [st0, st1]
                psT2 = [psT2a, psT2b]
                t_ones = p.inc("vector", s_dve, lambda e: e.memset(
                    v_aug.ap().rearrange("p (j g c) -> p j g c", j=TPS, g=4)[:, :, :, 64:65], 1.0))
                w_sq = [0, 0]
                w_pool_scr2 = [0, 0]
                w_tr = [0, 0]
                w_qk = [0, 0]
                w_qrope = [0, 0]
                w_packq = [0, 0]
                w_packk = [0, 0]
                w_ln = [0, 0]
                pack_ev = {}
                bank_last = {0: 0, 1: 0}
                t_vev_prev = 0
                t_sclk_prev = 0
                pst_free = 0
                x_ld_prev = 0
                cs_ld = [0] * TPS
                t_krope_par = [0, 0]
                for tt in range(TPS):
                    b = tt % 2
                    gt = tile0 + tt
                    if tt >= 2:
                        p.wait("scalar", s_pool, t_krope_par[b])
                        p.wait("scalar", s_s2, cs_ld[tt - 1])
                    cs_ld[tt] = p.inc("scalar", s_s2, lambda e, tt=tt: e.dma_start(
                        ctb[tt % 2][:, :], consts["cos16"][128 * tt:128 * (tt + 1), :]), 16)
                    cs_ld[tt] = p.inc("scalar", s_s2, lambda e, tt=tt: e.dma_start(
                        stb[tt % 2][:, :], consts["sinA16"][128 * tt:128 * (tt + 1), :]), 16)
                    p.wait("sync", s_dve, w_ln[b])
                    p.wait("sync", s_l[1], x_ld_prev)
                    x_ld_prev = x_ld = p.inc("sync", s_l[1], lambda e, gt=gt, b=b: e.dma_start(
                        xtb[b][:, :], x_src[128 * gt:128 * (gt + 1), :]), 16)
                    p.wait("vector", s_l[1], x_ld)
                    p.wait("scalar", s_pool, w_pool_scr2[b])
                    t_ln, t_sq_ = emit_ln_tile(p, ctx, xtb[b][:, :], ln_bf[b][:, :],
                                               scr[b], scr2[b], stat, 1.0 / D, 1e-5,
                                               war_act=w_sq[b], war_pe=w_tr[b])
                    w_ln[b] = t_ln
                    w_sq[b] = t_sq_
                    p.wait("tensor", s_dve, t_ln)
                    p.wait("tensor", s_dve, pst_free)
                    for c in range(8):
                        fn = (lambda e, c=c, b=b: e.transpose(pst[:, 128 * c:128 * (c + 1)],
                                                              ln_bf[b][:, 128 * c:128 * (c + 1)], ident[:, :]))
                        if c < 7:
                            p.run("tensor", fn)
                        else:
                            t_tr = p.inc("tensor", s_pe, fn)
                    w_tr[b] = t_tr
                    p.wait("vector", s_pe, t_tr)
                    p.wait("vector", s_pe, w_qk[b])
                    t_lnfm = p.inc("vector", s_dve, lambda e, b=b: e.tensor_copy(lnfm[b][:, :], pst[:, :]))
                    pst_free = t_lnfm
                    p.wait("tensor", s_dve, t_lnfm)
                    p.wait("tensor", s_act, t_sclk_prev)
                    p.wait("tensor", s_dve, t_vev_prev)
                    for nh in range(2):
                        for dc in range(8):
                            p.run("tensor", lambda e, nh=nh, dc=dc, b=b: e.matmul(
                                psq[:, 512 * nh:512 * (nh + 1)],
                                lnfm[b][:, 128 * dc:128 * (dc + 1)],
                                WqT_sb[:, 1024 * dc + 512 * nh:1024 * dc + 512 * (nh + 1)],
                                start=(dc == 0), stop=(dc == 7)))
                    for dc in range(8):
                        fn = (lambda e, dc=dc, b=b: e.matmul(
                            pskv[:, :], lnfm[b][:, 128 * dc:128 * (dc + 1)],
                            WkvT_sb[:, 512 * dc:512 * (dc + 1)],
                            start=(dc == 0), stop=(dc == 7)))
                        if dc == 7:
                            t_qk = p.inc("tensor", s_pe, fn)
                        else:
                            p.run("tensor", fn)
                    w_qk[b] = t_qk
                    p.wait("scalar", s_pe, t_qk)
                    p.run("scalar", lambda e, b=b: e.activation(scr2[b][:, :], psq[:, :], AF.Square))
                    t_sqk = p.inc("scalar", s_act, lambda e, b=b: e.activation(
                        scr[b][:, 0:256], pskv[:, 0:256], AF.Square))
                    p.wait("vector", s_act, t_sqk)
                    p.run("vector", lambda e, b=b: e.reduce_sum(
                        qn[b][:, 0:16], scr2[b].ap().rearrange("p (h a) -> p h a", h=16), axis=AX.X))
                    t_r2 = p.inc("vector", s_dve, lambda e, b=b: e.reduce_sum(
                        qn[b][:, 16:20], scr[b][:, 0:256].rearrange("p (h a) -> p h a", h=4), axis=AX.X))
                    p.wait("scalar", s_dve, t_r2)
                    t_sqr = p.inc("scalar", s_act, lambda e, b=b: e.activation(
                        qn[b][:, 24:44], qn[b][:, 0:20], AF.Sqrt))
                    p.wait("vector", s_act, t_sqr)
                    t_eps = p.inc("vector", s_dve, lambda e, b=b: e.tensor_scalar(
                        qn[b][:, 24:44], qn[b][:, 24:44], EPS, None, ALU.add))
                    t_rcp = p.inc("vector", s_dve, lambda e, b=b: e.reciprocal(qn[b][:, 44:64], qn[b][:, 24:44]))
                    t_scq = p.inc("vector", s_dve, lambda e, b=b: e.tensor_scalar(
                        qn[b][:, 44:60], qn[b][:, 44:60], 0.125, None, ALU.mult))
                    p.wait("scalar", s_dve, t_scq)
                    p.wait("scalar", s_pool, w_qrope[b])
                    for h in range(16):
                        fn = (lambda e, h=h, b=b: e.activation(
                            qsb[b][:, 64 * h:64 * (h + 1)], psq[:, 64 * h:64 * (h + 1)],
                            AF.Identity, scale=qn[b][:, 44 + h:45 + h]))
                        if h == 15:
                            t_scl = p.inc("scalar", s_act, fn)
                        else:
                            p.run("scalar", fn)
                    for h in range(4):
                        fn = (lambda e, h=h, b=b: e.activation(
                            scr2[b][:, 64 * h:64 * (h + 1)], pskv[:, 64 * h:64 * (h + 1)],
                            AF.Identity, scale=qn[b][:, 60 + h:61 + h]))
                        if h == 3:
                            t_sclk = p.inc("scalar", s_act, fn)
                        else:
                            p.run("scalar", fn)
                    t_sclk_prev = t_sclk
                    p.wait("vector", s_act, t_sclk)
                    p.wait("vector", s_dve, t_ones)
                    t_vev = p.inc("vector", s_dve, lambda e, tt=tt: e.tensor_copy(
                        v_aug.ap().rearrange("p (j g c) -> p j g c", j=TPS, g=4)[:, tt, :, 0:64],
                        pskv[:, 256:512].rearrange("p (g c) -> p g c", g=4)))
                    t_vev_prev = t_vev
                    p.wait("gpsimd", s_act, t_scl)
                    p.wait("gpsimd", s_s2, cs_ld[tt])
                    p.run("gpsimd", lambda e, tt=tt, b=b: e.tensor_tensor(
                        tq[:, :], qsb[b][:, :], ctb[tt % 2][:, :], ALU.mult))
                    p.run("gpsimd", lambda e, tt=tt, b=b: e.tensor_tensor(
                        tq2.ap().rearrange("p (h a) -> p h a", h=16)[:, :, 0:32],
                        qsb[b].ap().rearrange("p (h a) -> p h a", h=16)[:, :, 32:64],
                        stb[tt % 2].ap().rearrange("p (h a) -> p h a", h=16)[:, :, 0:32], ALU.mult))
                    p.run("gpsimd", lambda e, tt=tt, b=b: e.tensor_tensor(
                        tq2.ap().rearrange("p (h a) -> p h a", h=16)[:, :, 32:64],
                        qsb[b].ap().rearrange("p (h a) -> p h a", h=16)[:, :, 0:32],
                        stb[tt % 2].ap().rearrange("p (h a) -> p h a", h=16)[:, :, 32:64], ALU.mult))
                    p.wait("gpsimd", s_pe, w_packq[b])
                    t_qrope = p.inc("gpsimd", s_pool, lambda e, b=b: e.tensor_tensor(
                        q_rope[b][:, :], tq[:, :], tq2[:, :], ALU.add))
                    w_qrope[b] = t_qrope
                    p.wait("gpsimd", s_act, t_sclk)
                    p.run("gpsimd", lambda e, tt=tt, b=b: e.tensor_tensor(
                        tq[:, 0:256], scr2[b][:, 0:256], ctb[tt % 2][:, 0:256], ALU.mult))
                    p.run("gpsimd", lambda e, tt=tt, b=b: e.tensor_tensor(
                        tq2.ap().rearrange("p (h a) -> p h a", h=16)[:, 0:4, 0:32],
                        scr2[b].ap().rearrange("p (h a) -> p h a", h=16)[:, 0:4, 32:64],
                        stb[tt % 2].ap().rearrange("p (h a) -> p h a", h=16)[:, 0:4, 0:32], ALU.mult))
                    p.run("gpsimd", lambda e, tt=tt, b=b: e.tensor_tensor(
                        tq2.ap().rearrange("p (h a) -> p h a", h=16)[:, 0:4, 32:64],
                        scr2[b].ap().rearrange("p (h a) -> p h a", h=16)[:, 0:4, 0:32],
                        stb[tt % 2].ap().rearrange("p (h a) -> p h a", h=16)[:, 0:4, 32:64], ALU.mult))
                    p.wait("gpsimd", s_pe, w_packk[b])
                    t_krope = p.inc("gpsimd", s_pool, lambda e, b=b: e.tensor_tensor(
                        k_rope[b][:, :], tq[:, 0:256], tq2[:, 0:256], ALU.add))
                    w_pool_scr2[b] = t_krope
                    t_krope_par[b] = t_krope
                    p.wait("tensor", s_pool, t_qrope)
                    pi = 0
                    for cpair in range(2):
                        for s5 in range(4):
                            h1 = 8 * cpair + s5
                            h2 = 8 * cpair + 4 + s5
                            g = 4 * cpair + s5
                            bk = pi % 2
                            p.wait("tensor", s_dve, bank_last[bk])
                            p.run("tensor", lambda e, h1=h1, bk=bk, b=b: e.transpose(
                                psT2[bk][0:64, :], q_rope[b][:, 64 * h1:64 * (h1 + 1)], ident[:, :]))
                            t_pT = p.inc("tensor", s_pe, lambda e, h2=h2, bk=bk, b=b: e.transpose(
                                psT2[bk][64:128, :], q_rope[b][:, 64 * h2:64 * (h2 + 1)], ident[:, :]))
                            p.wait("vector", s_pe, t_pT)
                            bank_last[bk] = pack_ev[(tt, pi)] = p.inc(
                                "vector", s_dve,
                                lambda e, g=g, tt=tt, bk=bk: e.tensor_copy(
                                    q_hfm[:, 2048 * g + 128 * tt:2048 * g + 128 * (tt + 1)],
                                    psT2[bk][:, :]))
                            pi += 1
                    w_packq[b] = t_pT
                    p.wait("tensor", s_pool, t_krope)
                    for cpair in range(2):
                        bk = pi % 2
                        p.wait("tensor", s_dve, bank_last[bk])
                        p.run("tensor", lambda e, cpair=cpair, bk=bk, b=b: e.transpose(
                            psT2[bk][0:64, :], k_rope[b][:, 128 * cpair:128 * cpair + 64], ident[:, :]))
                        t_pT = p.inc("tensor", s_pe, lambda e, cpair=cpair, bk=bk, b=b: e.transpose(
                            psT2[bk][64:128, :], k_rope[b][:, 128 * cpair + 64:128 * (cpair + 1)], ident[:, :]))
                        p.wait("vector", s_pe, t_pT)
                        bank_last[bk] = pack_ev[(tt, pi)] = p.inc(
                            "vector", s_dve,
                            lambda e, cpair=cpair, tt=tt, bk=bk: e.tensor_copy(
                                k_fm[:, 2048 * cpair + 128 * tt:2048 * cpair + 128 * (tt + 1)],
                                psT2[bk][:, :]))
                        pi += 1
                    w_packk[b] = t_pT
                p.wait("vector", s_dve, pack_ev[(TPS - 1, 9)])
                p.realize(nc)
            esA.close()
            _scA.__exit__(None, None, None)

            if globals().get("L0_SKIP_B"):
                continue
            _scB = nc.named_scope(f"l0a_B{sq}"); _scB.__enter__()
            esB = ExitStack()
            expT = esB.enter_context(nc.sbuf_tensor(f"l0b_exp{sq}", [128, 2048], BF16, side="right"))
            y_bf = esB.enter_context(nc.sbuf_tensor(f"l0b_y{sq}", [64, 2048], BF16, side="right"))
            lnrow = esB.enter_context(nc.sbuf_tensor(f"l0b_lnr{sq}", [1, 2048], F32, side="right"))
            rsrow = esB.enter_context(nc.sbuf_tensor(f"l0b_rsr{sq}", [1, 2048], F32, side="right"))
            psc = esB.enter_context(nc.psum_tensor(f"l0b_psc{sq}", [128, 2048], F32))
            psY = esB.enter_context(nc.psum_tensor(f"l0b_psY{sq}", [65, 2048], F32))
            if True:
                p = Plan(chain={"vector": ctx["s_dve"], "scalar": ctx["s_act"], "gpsimd": ctx["s_pool"]})
                out_evac = [0] * TPS
                stored = [0] * TPS
                psc_free_dve = 0
                psc_free_act = 0
                psY_free = 0
                exp_free_pe = 0
                xq_ld_l = [0] * (TPS + 1)
                xq_ld_l[0] = p.inc("sync", s_l[1], lambda e: e.dma_start(
                    xt[:, :], x_src[128 * tile0:128 * (tile0 + 1), :]), 16)
                for qt in range(TPS):
                    jt0 = max(0, qt - 4)
                    njt = qt - jt0 + 1
                    xq_ld = xq_ld_l[qt]
                    for ji, jt in enumerate(range(jt0, qt + 1)):
                        p.wait("tensor", s_dve, psc_free_dve)
                        p.wait("tensor", s_act, psc_free_act)
                        masked = (jt == qt) or (qt >= 4 and jt == jt0)
                        mask_sb = mgt_sb if jt == qt else mle_sb
                        for kvh in range(4):
                            rh = 64 * (kvh % 2)
                            qv_view = q_hfm.ap().rearrange("p (g t) -> p g t", g=8)[
                                rh:rh + 64, 4 * (kvh // 2):4 * (kvh // 2) + 4,
                                128 * qt:128 * (qt + 1)]
                            fn = (lambda e, kvh=kvh, rh=rh, jt=jt, qv_view=qv_view, masked=masked: e.matmul(
                                psc[:, 512 * kvh:512 * (kvh + 1)],
                                k_fm[rh:rh + 64, 2048 * (kvh // 2) + 128 * jt:2048 * (kvh // 2) + 128 * (jt + 1)],
                                qv_view, start=True, stop=not masked))
                            fn2 = (lambda e, kvh=kvh, mask_sb=mask_sb: e.matmul(
                                psc[:, 512 * kvh:512 * (kvh + 1)],
                                ident[:, :], mask_sb[:, :], start=False, stop=True))
                            last = (kvh == 3)
                            if masked:
                                p.run("tensor", fn)
                                if last:
                                    t_sc = p.inc("tensor", s_pe, fn2)
                                else:
                                    p.run("tensor", fn2)
                            else:
                                if last:
                                    t_sc = p.inc("tensor", s_pe, fn)
                                else:
                                    p.run("tensor", fn)
                        p.wait("scalar", s_pe, t_sc)
                        p.wait("scalar", s_pe, exp_free_pe)
                        t_exp = p.inc("scalar", s_act, lambda e: e.activation(
                            expT[:, :], psc[:, :], AF.Exp))
                        psc_free_act = t_exp
                        p.wait("tensor", s_act, t_exp)
                        if ji == 0:
                            p.wait("tensor", s_dve, psY_free)
                        for kvh in range(4):
                            fn = (lambda e, kvh=kvh, jt=jt, ji=ji, qt=qt: e.matmul(
                                psY[:, 512 * kvh:512 * (kvh + 1)],
                                v_aug[:, 260 * jt + 65 * kvh:260 * jt + 65 * (kvh + 1)],
                                expT[:, 512 * kvh:512 * (kvh + 1)],
                                start=(ji == 0), stop=(jt == qt)))
                            if kvh == 3:
                                t_av = p.inc("tensor", s_pe, fn)
                            else:
                                p.run("tensor", fn)
                        exp_free_pe = t_av
                    p.wait("scalar", s_pe, t_av)
                    t_ln2 = p.inc("scalar", s_act, lambda e: e.activation(
                        lnrow[0:1, :], psY[64:65, :], AF.Ln))
                    p.wait("scalar", s_act, t_ln2)
                    t_rs = p.inc("scalar", s_act, lambda e: e.activation(
                        rsrow[0:1, :], lnrow[0:1, :], AF.Exp, scale=-1.0))
                    p.wait("tensor", s_act, t_rs)
                    p.wait("tensor", s_dve, psc_free_dve)
                    p.wait("tensor", s_act, psc_free_act)
                    for kvh in range(4):
                        fn = (lambda e, kvh=kvh: e.matmul(
                            psc[0:64, 512 * kvh:512 * (kvh + 1)], ones_f32[0:1, 0:64],
                            rsrow[0:1, 512 * kvh:512 * (kvh + 1)], start=True, stop=True))
                        if kvh == 3:
                            t_bc = p.inc("tensor", s_pe, fn)
                        else:
                            p.run("tensor", fn)
                    p.wait("vector", s_pe, t_bc)
                    t_yc = p.inc("vector", s_dve, lambda e: e.tensor_copy(y_bf[:, :], psY[0:64, :]))
                    p.wait("vector", s_dve, t_yc)
                    t_yn = p.inc("vector", s_dve, lambda e: e.tensor_tensor(
                        y_bf[:, :], y_bf[:, :], psc[0:64, 0:2048], ALU.mult))
                    psY_free = t_yn
                    p.wait("tensor", s_dve, t_yn)
                    for h in range(16):
                        for hf in range(2):
                            kvh, s5 = h // 4, h % 4
                            fn = (lambda e, h=h, hf=hf, kvh=kvh, s5=s5: e.matmul(
                                psc[:, 1024 + 512 * hf:1024 + 512 * (hf + 1)],
                                y_bf[0:64, 512 * kvh + 128 * s5:512 * kvh + 128 * (s5 + 1)],
                                WpT_sb[0:64, 1024 * h + 512 * hf:1024 * h + 512 * (hf + 1)],
                                start=(h == 0), stop=(h == 15)))
                            if h == 15 and hf == 1:
                                t_pj = p.inc("tensor", s_pe, fn)
                            else:
                                p.run("tensor", fn)
                    p.wait("vector", s_pe, t_pj)
                    p.wait("vector", s_l[1], xq_ld)
                    if qt >= 1:
                        p.wait("vector", s_st[1], stored[qt - 1])
                    out_evac[qt] = psc_free_dve = p.inc("vector", s_dve, lambda e: e.tensor_tensor(
                        xout[:, :], psc[:, 1024:2048], xt[:, :], ALU.add))
                    p.wait("sync", s_dve, out_evac[qt])
                    if qt >= 1:
                        p.wait("sync", s_st[1], stored[qt - 1])
                    stored[qt] = p.inc("sync", s_st[1], lambda e, qt=qt: e.dma_start(
                        x_dst[128 * (tile0 + qt):128 * (tile0 + qt + 1), :], xout[:, :]), 16)
                    if qt + 1 < TPS:
                        p.wait("sync", s_l[1], xq_ld_l[qt])
                        xq_ld_l[qt + 1] = p.inc("sync", s_l[1], lambda e, qt=qt: e.dma_start(
                            xt[:, :], x_src[128 * (tile0 + qt + 1):128 * (tile0 + qt + 2), :]), 16)
                p.wait("sync", s_st[1], stored[TPS - 1])
                p.realize(nc)
            esB.close()
            _scB.__exit__(None, None, None)
    es_.close()


def emit_l1_attn(nc, ctx, *, x_src, x_dst):
    s_x, s_w, s_s2, s_pe, s_dve, s_act, s_pool = (ctx["s_x"], ctx["s_w"], ctx["s_s2"],
                                                  ctx["s_pe"], ctx["s_dve"], ctx["s_act"],
                                                  ctx["s_pool"])
    s_l = ctx["s_l"]
    s_st = ctx["s_st"]
    w = ctx["weights"]
    consts = ctx["consts"]
    ident = ctx["ident_bf"]
    TAU = float(KX) ** 0.5

    s2buf = nc.dram_tensor("l1a_s2buf", [2, 128, 2176], F32, kind="Internal").ap()
    rms_dram = nc.dram_tensor("l1a_rmsd", [8, 128, T], BF16, kind="Internal").ap()

    es_ = ExitStack()

    def _sb(name, shape, dt, **kw):
        return es_.enter_context(nc.sbuf_tensor(name, shape, dt, **kw))

    Wq_sb = _sb("l1a_Wq", [128, 8 * 128], BF16)
    Wk_sb = _sb("l1a_Wk", [128, 8 * 128], BF16)
    g_sb = _sb("l1a_gc", [128, 8], F32)
    xlu_sb = _sb("l1a_xlu", [KX, 1], F32)
    xlv_sb = _sb("l1a_xlv", [KX, 1], F32)
    rrel_sb = _sb("l1a_rrel", [128, T], BF16)
    negstep_sb = _sb("l1a_neg", [128, 128], F32)
    wbuf = _sb("l1a_wbuf", [128, 16 * 1024], BF16)
    v_sb = _sb("l1a_v", [128, TPS * VX], BF16)
    sg_sb = _sb("l1a_sg", [128, 16 * T], BF16)
    qu_sb = _sb("l1a_qu", [128, T], BF16)
    qv_sb = _sb("l1a_qv", [128, T], BF16)
    k_sb = _sb("l1a_k", [128, T], BF16)
    stat = _sb("l1a_stat", [128, 8], F32)
    xt = _sb("l1a_xt", [128, D], F32)
    xout = _sb("l1a_xout", [128, D], F32)
    if True:
        with (
            nc.sbuf_tensor("l1a_m0", [128, 256], F32) as m0,
            nc.sbuf_tensor("l1a_cat", [128, 2048], BF16) as cat0,
            nc.sbuf_tensor("l1a_catf", [128, 2048], F32) as catf,
            nc.sbuf_tensor("l1a_Wr", [128, 8 * 128], BF16) as Wr_sb,
            nc.sbuf_tensor("l1a_zero", [128, 128], F32) as zero_sb,
            nc.psum_tensor("l1a_psr", [128, 2048], F32) as psr,
        ):
            p = Plan(chain={"vector": ctx["s_dve"], "scalar": ctx["s_act"], "gpsimd": ctx["s_pool"]})
            t_g = p.inc("sync", s_x, lambda e: e.dma_start(g_sb[:, :], w["l1_gc"][:, :]), 16)
            p.wait("sync", s_x, t_g)
            t_xu = p.inc("sync", s_x, lambda e: e.dma_start(xlu_sb[:, :], w["l1_xlu"][:, :]), 16)
            p.wait("sync", s_x, t_xu)
            t_xv = p.inc("sync", s_x, lambda e: e.dma_start(xlv_sb[:, :], w["l1_xlv"][:, :]), 16)
            p.wait("sync", s_x, t_xv)
            t_ng = p.inc("sync", s_x, lambda e: e.dma_start(negstep_sb[:, :], consts["l1_negstep"][:, :]), 16)
            p.wait("sync", s_x, t_ng)
            t_wr = p.inc("gpsimd", s_w, lambda e: e.dma_start(
                Wr_sb.ap().rearrange("p (c k) -> p c k", c=8),
                w["l1_Wr"].rearrange("(c p) k -> p c k", p=128)), 16)
            t_k = 0
            t_m = 0
            t_m2 = 0
            for dc in range(8):
                p.wait("sync", s_dve, t_k)
                p.wait("sync", s_l[0], t_m)
                p.wait("sync", s_l[1], t_m2)
                t_m = p.inc("sync", s_l[0], lambda e, dc=dc: e.dma_start(
                    m0[:, 0:KX], w["l1_Wq"][128 * dc:128 * (dc + 1), :]), 16)
                t_m2 = p.inc("sync", s_l[1], lambda e, dc=dc: e.dma_start(
                    m0[:, KX:2 * KX], w["l1_Wk"][128 * dc:128 * (dc + 1), :]), 16)
                p.wait("vector", s_l[0], t_m)
                p.wait("vector", s_x, t_g)
                t_q = p.inc("vector", s_dve, lambda e, dc=dc: e.tensor_scalar(
                    Wq_sb[:, 128 * dc:128 * (dc + 1)], m0[:, 0:KX],
                    g_sb[:, dc:dc + 1], 1.0 / TAU, ALU.mult, ALU.mult))
                p.wait("vector", s_l[1], t_m2)
                t_k = p.inc("vector", s_dve, lambda e, dc=dc: e.tensor_scalar(
                    Wk_sb[:, 128 * dc:128 * (dc + 1)], m0[:, KX:2 * KX],
                    g_sb[:, dc:dc + 1], None, ALU.mult))
            rr_t = [0] * 8
            t_cat = 0
            t_cc = 0
            for dc in range(8):
                if dc == 0:
                    p.wait("gpsimd", s_w, t_wr)
                if dc >= 1:
                    p.wait("gpsimd", s_dve, t_cc)
                    p.wait("gpsimd", s_w, t_cat)
                t_cat = p.inc("gpsimd", s_w, lambda e, dc=dc: e.dma_start(
                    catf[:, :], consts["catrevT"][128 * dc:128 * (dc + 1), :]), 16)
                p.wait("vector", s_w, t_cat)
                if dc >= 1:
                    p.wait("vector", s_pe, rr_t[dc - 1])
                t_cc = p.inc("vector", s_dve, lambda e: e.tensor_copy(cat0[:, :], catf[:, :]))
                p.wait("tensor", s_dve, t_cc)
                p.wait("tensor", s_w, t_wr)
                for es in range(4):
                    fn = (lambda e, dc=dc, es=es: e.matmul(
                        psr[:, 512 * es:512 * (es + 1)],
                        Wr_sb[:, 128 * dc:128 * (dc + 1)],
                        cat0[:, 512 * es:512 * (es + 1)],
                        start=(dc == 0), stop=(dc == 7)))
                    if es == 3:
                        rr_t[dc] = p.inc("tensor", s_pe, fn)
                    else:
                        p.run("tensor", fn)
            p.wait("vector", s_pe, rr_t[7])
            t_rrev = p.inc("vector", s_dve, lambda e: e.tensor_copy(rrel_sb[:, :], psr[:, :]))
            t_z = p.inc("vector", s_dve, lambda e: e.memset(zero_sb[:, :], 0.0))
            p.wait("sync", s_dve, t_z)
            p.wait("sync", s_x, s_x.val)
            for b in range(2):
                t_zp = p.inc("sync", s_x, lambda e, b=b: e.dma_start(
                    s2buf[b, :, 2048:2176], zero_sb[:, 0:128]), 16)
                p.wait("sync", s_x, t_zp)
            p.wait("sync", s_dve, t_rrev)
            p.realize(nc)

        for sq in range(NSEQ):
            tile0 = TPS * sq
            _sc1 = nc.named_scope(f"l1a_1a{sq}"); _sc1.__enter__()
            with (
                nc.sbuf_tensor(f"l1a_rmst{sq}", [128, 8 * 128], BF16, side="right") as rmstile,
                nc.sbuf_tensor(f"l1a_wm{sq}", [128, 1024], F32, side="right") as wm,
                nc.sbuf_tensor(f"l1a_scr{sq}", [128, D], F32, side="right") as scr,
                nc.sbuf_tensor(f"l1a_scr2{sq}", [128, D], F32, side="right") as scr2,
                nc.sbuf_tensor(f"l1a_lnbf{sq}", [128, D], BF16, side="right") as ln_bf,
                nc.psum_tensor(f"l1a_pst{sq}", [128, 1024], BF16) as pst,
                nc.psum_tensor(f"l1a_psv{sq}", [128, 2048], F32) as psv,
                nc.psum_tensor(f"l1a_psq{sq}", [128, 128], F32) as psq,
                nc.psum_tensor(f"l1a_psk{sq}", [128, 128], F32) as psk,
            ):
                p = Plan(chain={"vector": ctx["s_dve"], "scalar": ctx["s_act"], "gpsimd": ctx["s_pool"]})
                wv_t = [0] * 16
                t_m = 0
                for hc in range(16):
                    dc, hh = hc // 2, hc % 2
                    if hc >= 1:
                        p.wait("sync", s_dve, wv_t[hc - 1])
                        p.wait("sync", s_l[0], t_m)
                    t_m = p.inc("sync", s_l[0], lambda e, dc=dc, hh=hh: e.dma_start(
                        wm[:, :], w["l1_Wv"][128 * dc:128 * (dc + 1), 1024 * hh:1024 * (hh + 1)]), 16)
                    p.wait("vector", s_l[0], t_m)
                    wv_t[hc] = p.inc("vector", s_dve, lambda e, dc=dc, hh=hh: e.tensor_scalar(
                        wbuf[:, 2048 * dc + 1024 * hh:2048 * dc + 1024 * (hh + 1)], wm[:, :],
                        g_sb[:, dc:dc + 1], None, ALU.mult))
                ln_war_act = 0
                ln_war_pe = 0
                pst_free = 0
                rms_st = [0] * TPS
                v_done = [0] * TPS
                quk_done = [0] * TPS
                t_ln_prev = 0
                x_ld = 0
                for tt in range(TPS):
                    p.wait("sync", s_dve, t_ln_prev)
                    p.wait("sync", s_l[1], x_ld)
                    x_ld = p.inc("sync", s_l[1], lambda e, tt=tt: e.dma_start(
                        xt[:, :], x_src[128 * (tile0 + tt):128 * (tile0 + tt + 1), :]), 16)
                    p.wait("vector", s_l[1], x_ld)
                    t_ln, t_sq_ = emit_ln_tile(p, ctx, xt[:, :], ln_bf[:, :],
                                               scr, scr2, stat,
                                               (1.0 + 1e-6) / D, 1e-11,
                                               war_act=ln_war_act, war_pe=ln_war_pe)
                    ln_war_act = t_sq_
                    t_ln_prev = t_ln
                    p.wait("tensor", s_dve, t_ln)
                    p.wait("tensor", s_dve, pst_free)
                    for c in range(8):
                        fn = (lambda e, c=c: e.transpose(pst[:, 128 * c:128 * (c + 1)],
                                                         ln_bf[:, 128 * c:128 * (c + 1)], ident[:, :]))
                        if c < 7:
                            p.run("tensor", fn)
                        else:
                            t_tr = p.inc("tensor", s_pe, fn)
                    ln_war_pe = t_tr
                    p.wait("vector", s_pe, t_tr)
                    if tt >= 1:
                        p.wait("vector", s_st[0], rms_st[tt - 1])
                    t_ev = p.inc("vector", s_dve, lambda e: e.tensor_copy(
                        rmstile.ap().rearrange("p (c t) -> p c t", c=8),
                        pst.ap().rearrange("p (c a) -> p c a", c=8)))
                    pst_free = t_ev
                    p.wait("sync", s_dve, t_ev)
                    if tt >= 1:
                        p.wait("sync", s_st[0], rms_st[tt - 1])
                    rms_st[tt] = p.inc("sync", s_st[0], lambda e, tt=tt: e.dma_start(
                        rms_dram.rearrange("c p t -> p c t")[:, :, 128 * tt:128 * (tt + 1)],
                        rmstile.ap().rearrange("p (c t) -> p c t", c=8)), 16)
                    p.wait("tensor", s_dve, t_ev)
                    if tt == 0:
                        p.wait("tensor", s_dve, wv_t[15])
                    if tt >= 1:
                        p.wait("tensor", s_act, v_done[tt - 1])
                        p.wait("tensor", s_act, quk_done[tt - 1])
                    for ns in range(4):
                        for dc in range(8):
                            fn = (lambda e, ns=ns, dc=dc: e.matmul(
                                psv[:, 512 * ns:512 * (ns + 1)],
                                rmstile[:, 128 * dc:128 * (dc + 1)],
                                wbuf[:, 2048 * dc + 512 * ns:2048 * dc + 512 * (ns + 1)],
                                start=(dc == 0), stop=(dc == 7)))
                            if ns == 3 and dc == 7:
                                t_v = p.inc("tensor", s_pe, fn)
                            else:
                                p.run("tensor", fn)
                    for dc in range(8):
                        p.run("tensor", lambda e, dc=dc: e.matmul(
                            psq[:, :], Wq_sb[:, 128 * dc:128 * (dc + 1)],
                            rmstile[:, 128 * dc:128 * (dc + 1)],
                            start=(dc == 0), stop=(dc == 7)))
                    for dc in range(8):
                        fn = (lambda e, dc=dc: e.matmul(
                            psk[:, :], Wk_sb[:, 128 * dc:128 * (dc + 1)],
                            rmstile[:, 128 * dc:128 * (dc + 1)],
                            start=(dc == 0), stop=(dc == 7)))
                        if dc == 7:
                            t_qk = p.inc("tensor", s_pe, fn)
                        else:
                            p.run("tensor", fn)
                    p.wait("scalar", s_pe, t_v)
                    v_done[tt] = p.inc("scalar", s_act, lambda e, tt=tt: e.activation(
                        v_sb[:, VX * tt:VX * (tt + 1)], psv[:, :], AF.Copy))
                    p.wait("scalar", s_pe, t_qk)
                    p.run("scalar", lambda e, tt=tt: e.activation(
                        qu_sb[:, 128 * tt:128 * (tt + 1)], psq[:, :], AF.Identity,
                        bias=xlu_sb[:, 0:1]))
                    p.run("scalar", lambda e, tt=tt: e.activation(
                        qv_sb[:, 128 * tt:128 * (tt + 1)], psq[:, :], AF.Identity,
                        bias=xlv_sb[:, 0:1]))
                    quk_done[tt] = p.inc("scalar", s_act, lambda e, tt=tt: e.activation(
                        k_sb[:, 128 * tt:128 * (tt + 1)], psk[:, :], AF.Copy))
                p.wait("scalar", s_act, quk_done[TPS - 1])
                p.realize(nc)
            rms_st_last = rms_st[TPS - 1]

            _sc1.__exit__(None, None, None)
            _sc2 = nc.named_scope(f"l1a_1b{sq}"); _sc2.__enter__()
            with (
                nc.sbuf_tensor(f"l1a_rsl0{sq}", [128, 8 * 512], BF16, side="right") as rs0,
                nc.sbuf_tensor(f"l1a_wm2{sq}", [128, 1024], F32, side="right") as wm2,
                nc.psum_tensor(f"l1a_psg{sq}", [128, 1024], F32) as psg,
            ):
                p = Plan(chain={"vector": ctx["s_dve"], "scalar": ctx["s_act"], "gpsimd": ctx["s_pool"]})
                rsl = [rs0, rs0]
                wg_t = [0] * 16
                t_m = 0
                for hc in range(16):
                    dc, hh = hc // 2, hc % 2
                    if hc >= 1:
                        p.wait("sync", s_dve, wg_t[hc - 1])
                        p.wait("sync", s_l[0], t_m)
                    t_m = p.inc("sync", s_l[0], lambda e, dc=dc, hh=hh: e.dma_start(
                        wm2[:, :], w["l1_Wg"][128 * dc:128 * (dc + 1), 1024 * hh:1024 * (hh + 1)]), 16)
                    p.wait("vector", s_l[0], t_m)
                    wg_t[hc] = p.inc("vector", s_dve, lambda e, dc=dc, hh=hh: e.tensor_scalar(
                        wbuf[:, 2048 * dc + 1024 * hh:2048 * dc + 1024 * (hh + 1)], wm2[:, :],
                        g_sb[:, dc:dc + 1], None, ALU.mult))
                sg_evac = {}
                rs_ld = [0] * 4
                p.wait("sync", s_st[0], rms_st_last)
                for ts4 in range(4):
                    if ts4 >= 1:
                        p.wait("sync", s_pe, t_sg)
                        p.wait("sync", s_l[1], rs_ld[ts4 - 1])
                    rs_ld[ts4] = p.inc("sync", s_l[1], lambda e, ts4=ts4: e.dma_start(
                        rsl[ts4 % 2].ap().rearrange("p (c t) -> p c t", c=8),
                        rms_dram.rearrange("c p t -> p c t")[:, :, 512 * ts4:512 * (ts4 + 1)]), 16)
                    for vc in range(16):
                        gi = ts4 * 16 + vc
                        bsel = gi % 2
                        if gi >= 2:
                            p.wait("tensor", s_act, sg_evac[gi - 2])
                        p.wait("tensor", s_dve, wg_t[15])
                        p.wait("tensor", s_l[1], rs_ld[ts4])
                        for dc in range(8):
                            fn = (lambda e, vc=vc, dc=dc, bsel=bsel: e.matmul(
                                psg[:, 512 * bsel:512 * (bsel + 1)],
                                wbuf[:, 2048 * dc + 128 * vc:2048 * dc + 128 * (vc + 1)],
                                rsl[0][:, 512 * dc:512 * (dc + 1)] if True else None,
                                start=(dc == 0), stop=(dc == 7)))
                            fn = (lambda e, vc=vc, dc=dc, bsel=bsel, ts4=ts4: e.matmul(
                                psg[:, 512 * bsel:512 * (bsel + 1)],
                                wbuf[:, 2048 * dc + 128 * vc:2048 * dc + 128 * (vc + 1)],
                                rsl[ts4 % 2][:, 512 * dc:512 * (dc + 1)],
                                start=(dc == 0), stop=(dc == 7)))
                            if dc == 7:
                                t_sg = p.inc("tensor", s_pe, fn)
                            else:
                                p.run("tensor", fn)
                        p.wait("scalar", s_pe, t_sg)
                        sg_evac[gi] = p.inc("scalar", s_act, lambda e, vc=vc, ts4=ts4, bsel=bsel: e.activation(
                            sg_sb[:, T * vc + 512 * ts4:T * vc + 512 * (ts4 + 1)],
                            psg[:, 512 * bsel:512 * (bsel + 1)], AF.Silu))
                p.wait("scalar", s_act, sg_evac[63])
                p.realize(nc)

            _sc2.__exit__(None, None, None)
            p = Plan(chain={"vector": ctx["s_dve"], "scalar": ctx["s_act"], "gpsimd": ctx["s_pool"]})
            wo_t = 0
            for vc in range(16):
                wo_t = p.inc("gpsimd", s_w, lambda e, vc=vc: e.dma_start(
                    wbuf[:, 1024 * vc:1024 * (vc + 1)], w["l1_Wo"][128 * vc:128 * (vc + 1), :]), 16)
            p.wait("gpsimd", s_w, wo_t)
            p.realize(nc)

            _sc3 = nc.named_scope(f"l1a_2{sq}"); _sc3.__enter__()
            with (
                nc.sbuf_tensor(f"l1a_s2w{sq}", [128, 512], F32, side="right") as s2w_sb,
                nc.sbuf_tensor(f"l1a_sc{sq}", [128, 2048], F32, side="right") as scores_sb,
                nc.sbuf_tensor(f"l1a_at{sq}", [128, 2048], BF16, side="right") as attn_sb,
                nc.sbuf_tensor(f"l1a_atT{sq}", [128, 16 * 128], BF16, side="right") as attnT_sb,
                nc.psum_tensor(f"l1a_psA{sq}", [128, 1024], F32) as psA,
                nc.psum_tensor(f"l1a_psW{sq}", [128, 2048], F32) as psW,
                nc.psum_tensor(f"l1a_psT0{sq}", [128, 512], BF16) as psT0,
                nc.psum_tensor(f"l1a_psT1{sq}", [128, 512], BF16) as psT1,
            ):
                wvg_sb = attn_sb
                p = Plan(chain={"vector": ctx["s_dve"], "scalar": ctx["s_act"], "gpsimd": ctx["s_pool"]})
                s2_read_done = [0, 0]
                out_evac = [0] * TPS
                stored = [0] * TPS
                psA_free_dve = 0
                psA_free_act = 0
                psW_free_dve = 0
                exp_prev = 0
                bd_prev = 0
                s2w_prev = 0
                t_lastT = 0
                xq_ld_l = [0] * (TPS + 1)
                xq_ld_l[0] = p.inc("sync", s_l[1], lambda e: e.dma_start(
                    xt[:, :], x_src[128 * tile0:128 * (tile0 + 1), :]), 16)
                for qt in range(TPS):
                    W = 128 * (qt + 1)
                    nhalf = (W + 1023) // 1024
                    bsel = qt % 2
                    xq_ld = xq_ld_l[qt]
                    t_s2w = 0
                    if qt >= 2:
                        p.wait("scalar", s_s2, s2_read_done[bsel])
                    for s5 in range(qt + 1):
                        pass
                    nsl5 = (W + 511) // 512
                    for s5 in range(nsl5):
                        e0 = 2048 - W + 512 * s5
                        sw = min(512, W - 512 * s5)
                        p.wait("tensor", s_dve, psA_free_dve)
                        p.wait("tensor", s_act, psA_free_act)
                        t_s2 = p.inc("tensor", s_pe, lambda e, qt=qt, e0=e0, sw=sw: e.matmul(
                            psA[:, 0:sw],
                            qv_sb[:, 128 * qt:128 * (qt + 1)],
                            rrel_sb[:, e0:e0 + sw],
                            start=True, stop=True))
                        p.wait("scalar", s_pe, t_s2)
                        p.wait("scalar", s_s2, s2w_prev)
                        t_s2e = p.inc("scalar", s_act, lambda e, sw=sw: e.activation(
                            s2w_sb[:, 0:sw], psA[:, 0:sw], AF.Copy))
                        psA_free_act = t_s2e
                        p.wait("scalar", s_act, t_s2e)
                        t_s2w = s2w_prev = p.inc("scalar", s_s2, lambda e, bsel=bsel, e0=e0, sw=sw: e.dma_start(
                            s2buf[bsel, :, e0:e0 + sw], s2w_sb[:, 0:sw]), 16)
                        
                    p.wait("scalar", s_s2, t_s2w)
                    p.wait("scalar", s_dve, bd_prev)
                    p.wait("scalar", s_act, exp_prev)
                    rd_ap = bass.AP(s2buf.tensor, bsel * 128 * 2176 + 2047 - 128 * qt,
                                    [[2175, 128], [1, W]])
                    t_s2r = p.inc("scalar", s_s2, lambda e, rd_ap=rd_ap, W=W: e.dma_start(
                        scores_sb[:, 0:W], rd_ap), 16)
                    s2_read_done[bsel] = t_s2r
                    for hf in range(nhalf):
                        c0 = 1024 * hf
                        cw = min(1024, W - c0)
                        ns = (cw + 511) // 512
                        p.wait("tensor", s_dve, psA_free_dve)
                        p.wait("tensor", s_act, psA_free_act)
                        for s5 in range(ns):
                            sw = min(512, cw - 512 * s5)
                            fn = (lambda e, qt=qt, c0=c0, s5=s5, sw=sw: e.matmul(
                                psA[:, 512 * s5:512 * s5 + sw],
                                qu_sb[:, 128 * qt:128 * (qt + 1)],
                                k_sb[:, c0 + 512 * s5:c0 + 512 * s5 + sw],
                                start=True, stop=True))
                            if s5 == ns - 1:
                                t_ac = p.inc("tensor", s_pe, fn)
                            else:
                                p.run("tensor", fn)
                        p.wait("vector", s_pe, t_ac)
                        p.wait("vector", s_s2, t_s2r)
                        psA_free_dve = p.inc("vector", s_dve, lambda e, c0=c0, cw=cw: e.tensor_tensor(
                            scores_sb[:, c0:c0 + cw], psA[:, 0:cw], scores_sb[:, c0:c0 + cw], ALU.add))
                    taps = ctx.get("l1taps")
                    t_msk = p.inc("vector", s_dve, lambda e, W=W: e.tensor_tensor(
                        scores_sb[:, W - 128:W], scores_sb[:, W - 128:W], negstep_sb[:, :], ALU.add))
                    p.wait("vector", s_dve, t_msk)
                    t_nm = p.inc("vector", s_dve, lambda e, W=W: e.reduce_max(
                        stat[:, 0:1], scores_sb[:, 0:W], axis=AX.X, negate=True))
                    bd_prev = t_nm
                    p.wait("scalar", s_dve, t_nm)
                    p.wait("scalar", s_pe, t_lastT)
                    exp_prev = t_exp = p.inc("scalar", s_act, lambda e, W=W: e.activation(
                        attn_sb[:, 0:W], scores_sb[:, 0:W], AF.Exp, bias=stat[:, 0:1],
                        accum_out=stat[:, 1:2]))
                    p.wait("vector", s_act, t_exp)
                    t_rs = p.inc("vector", s_dve, lambda e: e.reciprocal(stat[:, 2:3], stat[:, 1:2]))
                    p.wait("vector", s_dve, t_rs)
                    t_an = p.inc("vector", s_dve, lambda e, W=W: e.tensor_scalar(
                        attn_sb[:, 0:W], attn_sb[:, 0:W], stat[:, 2:3], None, ALU.mult))

                    p.wait("tensor", s_dve, t_an)
                    p.wait("tensor", s_dve, psW_free_dve)
                    t_wv = 0
                    psTb = [psT0, psT1]
                    evac_all = {}
                    for j in range(qt + 1):
                        bk = j % 2
                        sl4 = (j // 2) % 4
                        if j >= 2:
                            p.wait("tensor", s_dve, evac_all[j - 2])
                        fn = (lambda e, j=j, bk=bk, sl4=sl4: e.transpose(
                            psTb[bk][:, 128 * sl4:128 * (sl4 + 1)],
                            attn_sb[:, 128 * j:128 * (j + 1)], ident[:, :]))
                        t_lastT = p.inc("tensor", s_pe, fn)
                        p.wait("vector", s_pe, t_lastT)
                        evac_all[j] = p.inc("vector", s_dve,
                            lambda e, j=j, bk=bk, sl4=sl4: e.tensor_copy(
                                attnT_sb[:, 128 * j:128 * (j + 1)],
                                psTb[bk][:, 128 * sl4:128 * (sl4 + 1)]))
                    p.wait("tensor", s_dve, evac_all[qt])
                    for vc in range(16):
                        for j in range(qt + 1):
                            fn = (lambda e, j=j, vc=vc, qt=qt: e.matmul(
                                psW[:, 128 * vc:128 * (vc + 1)],
                                v_sb[:, VX * j + 128 * vc:128 * (vc + 1) + VX * j],
                                attnT_sb[:, 128 * j:128 * (j + 1)],
                                start=(j == 0), stop=(j == qt)))
                            if vc == 15 and j == qt:
                                t_wv = p.inc("tensor", s_pe, fn)
                            else:
                                p.run("tensor", fn)
                    p.wait("vector", s_pe, t_wv)
                    for sgrp in range(4):
                        in1 = sg_sb.ap().rearrange("p (c t) -> p c t", c=16)[:, 4 * sgrp:4 * (sgrp + 1),
                                                                            128 * qt:128 * (qt + 1)]
                        in0 = psW.ap().rearrange("p (c t) -> p c t", c=16)[:, 4 * sgrp:4 * (sgrp + 1), :]
                        out = wvg_sb.ap().rearrange("p (c t) -> p c t", c=16)[:, 4 * sgrp:4 * (sgrp + 1), :]
                        fn = (lambda e, in0=in0, in1=in1, out=out: e.tensor_tensor(out, in0, in1, ALU.mult))
                        if sgrp == 3:
                            t_wvg = p.inc("vector", s_dve, fn)
                        else:
                            p.run("vector", fn)
                    if taps is not None and sq == 0 and qt == 1:
                        p.wait("gpsimd", s_dve, t_wvg)
                        tq = p.inc("gpsimd", s_w, lambda e: e.dma_start(taps["wvg"][:, :], wvg_sb[:, 0:2048]), 16)
                        tq = p.inc("gpsimd", s_w, lambda e: e.dma_start(taps["v1"][:, :], v_sb[:, VX:VX + 2048]), 16)
                        p.wait("gpsimd", s_w, tq)
                    p.wait("tensor", s_dve, t_wvg)
                    for vc in range(16):
                        for hfp in range(2):
                            fn = (lambda e, vc=vc, hfp=hfp: e.matmul(
                                psW[:, 512 * hfp:512 * (hfp + 1)],
                                wvg_sb[:, 128 * vc:128 * (vc + 1)],
                                wbuf[:, 1024 * vc + 512 * hfp:1024 * vc + 512 * (hfp + 1)],
                                start=(vc == 0), stop=(vc == 15)))
                            if vc == 15 and hfp == 1:
                                t_pj = p.inc("tensor", s_pe, fn)
                            else:
                                p.run("tensor", fn)
                    p.wait("vector", s_pe, t_pj)
                    p.wait("vector", s_l[1], xq_ld)
                    if qt >= 1:
                        p.wait("vector", s_st[1], stored[qt - 1])
                    out_evac[qt] = psW_free_dve = p.inc("vector", s_dve,
                                                        lambda e: e.tensor_tensor(
                                                            xout[:, :], psW[:, 0:1024], xt[:, :], ALU.add))
                    p.wait("sync", s_dve, out_evac[qt])
                    if qt >= 1:
                        p.wait("sync", s_st[1], stored[qt - 1])
                    stored[qt] = p.inc("sync", s_st[1], lambda e, qt=qt: e.dma_start(
                        x_dst[128 * (tile0 + qt):128 * (tile0 + qt + 1), :], xout[:, :]), 16)
                    if qt + 1 < TPS:
                        p.wait("sync", s_l[1], xq_ld_l[qt])
                        xq_ld_l[qt + 1] = p.inc("sync", s_l[1], lambda e, qt=qt: e.dma_start(
                            xt[:, :], x_src[128 * (tile0 + qt + 1):128 * (tile0 + qt + 2), :]), 16)
                p.wait("sync", s_st[1], stored[TPS - 1])
                p.realize(nc)
            _sc3.__exit__(None, None, None)
    es_.close()


def kernel(**inputs):
    consts = host_constants()
    w = shard_weights(inputs)
    x = np.asarray(inputs["L0_attn_x"], dtype=np.float32)
    nc = build_nc()
    in_maps = []
    for core in range(NCORES):
        m = {"x": np.ascontiguousarray(x[NSEQ * core:NSEQ * (core + 1)].reshape(NTOK, D))}
        m.update(consts)
        m.update(w)
        in_maps.append(m)
    res = run_bass_kernel_spmd(nc, in_maps, core_ids=list(range(NCORES)))
    out = np.concatenate([r["y"].reshape(NSEQ, T, D) for r in res.results], axis=0)
    return out.astype(np.float32)


# version 17
# speedup vs baseline: 1.0957x; 1.0852x over previous
from contextlib import ExitStack
import numpy as np
import ml_dtypes
import concourse.bass as bass
import concourse.mybir as mybir
from concourse.bass_utils import run_bass_kernel_spmd

F32 = mybir.dt.float32
BF16 = mybir.dt.bfloat16
AF = mybir.ActivationFunctionType
ALU = mybir.AluOpType
AX = mybir.AxisListType

NCORES = 8
B, T, D = 16, 2048, 1024
NSEQ = B // NCORES
NTOK = NSEQ * T
NTT = NTOK // 128
TPS = T // 128
NH, NKV, DH, WINDOW = 16, 4, 64, 512
FF = 4 * D
KX, VX = 128, 2048
NEG = -1e30
BF = ml_dtypes.bfloat16


def host_constants():
    c = {}
    theta = 1.0 / 10000.0 ** (np.arange(0, DH, 2, dtype=np.float64) / DH)
    pos = np.arange(T, dtype=np.float64)
    freqs = np.outer(pos, theta)
    cos = np.concatenate([np.cos(freqs), np.cos(freqs)], -1).astype(np.float32)
    sin = np.concatenate([np.sin(freqs), np.sin(freqs)], -1).astype(np.float32)
    sinA = np.concatenate([-sin[:, 0:32], sin[:, 32:64]], -1).astype(np.float32)
    c["cos16"] = np.ascontiguousarray(np.tile(cos, (1, NH)))
    c["sinA16"] = np.ascontiguousarray(np.tile(sinA, (1, NH)))
    c["ident_bf"] = np.eye(128, dtype=BF)
    c["ones_bf"] = np.ones((1, 128), dtype=BF)
    c["ones_f32"] = np.ones((1, 128), dtype=np.float32)
    c["onescol_f32"] = np.ones((128, 1), dtype=np.float32)
    p = np.arange(128)[:, None]
    cq = np.arange(128)[None, :]
    negstep_gt = np.where(p > cq, np.float32(NEG), np.float32(0)).astype(BF)
    negstep_le = np.where(p <= cq, np.float32(NEG), np.float32(0)).astype(BF)
    c["negstep_gt4"] = np.ascontiguousarray(np.tile(negstep_gt, (1, 4)))
    c["negstep_le4"] = np.ascontiguousarray(np.tile(negstep_le, (1, 4)))
    c["l1_negstep"] = np.where(cq > p, np.float32(NEG), np.float32(0)).astype(np.float32)
    LAM = 10000.0
    inv_lams = 1.0 / LAM ** (np.arange(0, D, 2, dtype=np.float64) / D)
    pre = np.arange(T, dtype=np.float64)[:, None] * inv_lams[None, :]
    cat = np.concatenate([np.sin(pre), np.cos(pre)], -1)
    c["catrevT"] = np.ascontiguousarray(cat[::-1].T.astype(np.float32))
    return c


CONST_SPECS = {
    "cos16": ([T, D], F32), "sinA16": ([T, D], F32),
    "ident_bf": ([128, 128], BF16), "ones_bf": ([1, 128], BF16),
    "ones_f32": ([1, 128], F32), "onescol_f32": ([128, 1], F32),
    "negstep_gt4": ([128, 512], BF16), "negstep_le4": ([128, 512], BF16),
    "l1_negstep": ([128, 128], F32),
    "catrevT": ([D, T], F32),
}

WEIGHT_SPECS = {
    "l0_WqT": ([D, D], F32),
    "l0_WkvT": ([D, 512], F32),
    "l0_WprojTh": ([64, 16 * D], F32),
    "l0_W1T": ([D, FF], F32),
    "l0_b1c": ([128, FF // 128], F32),
    "l0_W2T": ([FF, D], F32),
    "l0_b2": ([1, D], F32),
    "l1_gc": ([128, 8], F32),
    "l1_Wq": ([D, KX], F32),
    "l1_Wk": ([D, KX], F32),
    "l1_Wv": ([D, VX], F32),
    "l1_Wg": ([D, VX], F32),
    "l1_Wr": ([D, KX], F32),
    "l1_xlu": ([KX, 1], F32),
    "l1_xlv": ([KX, 1], F32),
    "l1_Wo": ([VX, D], F32),
    "l1f_W1": ([D, FF], F32),
    "l1f_b1c": ([128, FF // 128], F32),
    "l1f_W2": ([FF, D], F32),
    "l1f_b2": ([1, D], F32),
}


def shard_weights(i):
    w = {}
    w["l0_WqT"] = i["L0_attn_Wq"].T
    w["l0_WkvT"] = i["L0_attn_Wkv"].T
    w["l0_WprojTh"] = i["L0_attn_Wproj"].T.reshape(16, 64, D).transpose(1, 0, 2).reshape(64, 16 * D)
    w["l0_W1T"] = i["L0_ffn_W1"].T
    w["l0_b1c"] = i["L0_ffn_b1"].reshape(FF // 128, 128).T
    w["l0_W2T"] = i["L0_ffn_W2"].T
    w["l0_b2"] = i["L0_ffn_b2"].reshape(1, D)
    w["l1_gc"] = i["L1_attn_g_ln"].reshape(8, 128).T
    w["l1_Wq"] = i["L1_attn_W_q"]
    w["l1_Wk"] = i["L1_attn_W_k"]
    w["l1_Wv"] = i["L1_attn_W_v"]
    w["l1_Wg"] = i["L1_attn_W_g"]
    w["l1_Wr"] = i["L1_attn_W_r"]
    w["l1_xlu"] = i["L1_attn_xl_u"].reshape(KX, 1)
    w["l1_xlv"] = i["L1_attn_xl_v"].reshape(KX, 1)
    w["l1_Wo"] = i["L1_attn_W_o"]
    w["l1f_W1"] = i["L1_ffn_W1"]
    w["l1f_b1c"] = i["L1_ffn_b1"].reshape(FF // 128, 128).T
    w["l1f_W2"] = i["L1_ffn_W2"]
    w["l1f_b2"] = i["L1_ffn_b2"].reshape(1, D)
    return {k: np.ascontiguousarray(np.asarray(v, dtype=np.float32)) for k, v in w.items()}


ENGS = ("sync", "scalar", "vector", "tensor", "gpsimd")


class Ctr:
    def __init__(self, sem):
        self.sem = sem
        self.val = 0


class Plan:

    def __init__(self, chain=None):
        self.q = {e: [] for e in ENGS}
        self.chain = chain or {}

    def on(self, eng, fn):
        self.q[eng].append(fn)

    def wait(self, eng, ctr, val=None):
        v = int(ctr.val if val is None else val)
        if v > 0:
            self.q[eng].append(lambda e, s=ctr.sem, v=v: e.wait_ge(s, v))

    def _prewait(self, eng):
        c = self.chain.get(eng)
        if c is not None and c.val > 0:
            self.q[eng].append(lambda e, s=c.sem, v=c.val: e.wait_ge(s, v))

    def inc(self, eng, ctr, fn, n=1):
        self._prewait(eng)
        ctr.val += n
        tgt = ctr.val

        def go(e, fn=fn, s=ctr.sem, n=n):
            inst = fn(e)
            inst.then_inc(s, n)
        self.q[eng].append(go)
        return tgt

    def run(self, eng, fn):
        c = self.chain.get(eng)
        if c is None:
            self.q[eng].append(fn)
            return None
        return self.inc(eng, c, fn)

    def realize(self, nc):
        with nc.Block() as blk:
            for eng in ENGS:
                ops = self.q[eng]
                if not ops:
                    continue
                getattr(blk, eng)(lambda e, ops=ops: [f(e) for f in ops])


def emit_ln_tile(p, ctx, x_ap, out_ap, scr, scr2, stat, var_scale, var_bias,
                 war_act=0, war_pe=0):
    s_dve, s_act, s_pe = ctx["s_dve"], ctx["s_act"], ctx["s_pe"]
    v = "vector"
    s = "scalar"
    t0 = p.inc(v, s_dve, lambda e: e.reduce_sum(stat[:, 0:1], x_ap, axis=AX.X))
    p.wait(v, s_dve, t0)
    t1 = p.inc(v, s_dve, lambda e: e.tensor_scalar(stat[:, 1:2], stat[:, 0:1], 1.0 / D, None, ALU.mult))
    p.wait(v, s_act, war_act)
    p.wait(v, s_dve, t1)
    t_sub = p.inc(v, s_dve, lambda e: e.tensor_scalar(scr[:, :], x_ap, stat[:, 1:2], None, ALU.subtract))
    p.wait(s, s_dve, t_sub)
    t_sq = p.inc(s, s_act, lambda e: e.activation(scr2[:, :], scr[:, :], AF.Square,
                                                 accum_out=stat[:, 2:3]))
    p.wait(v, s_act, t_sq)
    t2 = p.inc(v, s_dve, lambda e: e.tensor_scalar(stat[:, 3:4], stat[:, 2:3], var_scale, var_bias,
                                                   ALU.mult, ALU.add))
    p.wait(v, s_dve, t2)
    t_rc = p.inc(v, s_dve, lambda e: e.reciprocal(stat[:, 4:5], stat[:, 3:4]))
    p.wait(s, s_dve, t_rc)
    t_sq2 = p.inc(s, s_act, lambda e: e.activation(stat[:, 5:6], stat[:, 4:5], AF.Sqrt))
    p.wait(v, s_act, t_sq2)
    p.wait(v, s_pe, war_pe)
    t_out = p.inc(v, s_dve, lambda e: e.tensor_scalar(out_ap, scr[:, :], stat[:, 5:6], None, ALU.mult))
    return t_out, t_sq


def x_slice_dram(x_ap, sl, ntiles=2):
    r0 = 128 * ntiles * sl
    return x_ap[r0:r0 + 128 * ntiles, :].rearrange("(a p) d -> p a d", p=128)


def sb_3d(sb_ap, ntiles=2):
    return sb_ap.rearrange("p (a d) -> p a d", a=ntiles)


def emit_ffn_phase(nc, ctx, *, x_src, x_dst, W1_sb, W2_sb, b1_sb, b2row_sb,
                   act_func, name, taps=None, nsl=None, w_gate=None):
    s_x, s_w, s_pe, s_dve, s_act = ctx["s_x"], ctx["s_w"], ctx["s_pe"], ctx["s_dve"], ctx["s_act"]
    s_l = ctx["s_l"]
    s_st = ctx["s_st"]
    ones_bf = ctx["ones_bf"]
    ident = ctx["ident_bf"]
    NSL = NTT // 2 if nsl is None else nsl

    with (
        nc.sbuf_tensor(f"{name}_x0", [128, 2 * D], F32) as xb0,
        nc.sbuf_tensor(f"{name}_x1", [128, 2 * D], F32) as xb1,
        nc.sbuf_tensor(f"{name}_x2", [128, 2 * D], F32) as xb2,
        nc.sbuf_tensor(f"{name}_scr", [128, D], F32) as scr,
        nc.sbuf_tensor(f"{name}_scr2", [128, D], F32) as scr2,
        nc.sbuf_tensor(f"{name}_stat", [128, 16], F32) as stat,
        nc.sbuf_tensor(f"{name}_lnbf0", [128, D], BF16) as lnbf0,
        nc.sbuf_tensor(f"{name}_lnbf1", [128, D], BF16) as lnbf1,
        nc.sbuf_tensor(f"{name}_lnbf2", [128, D], BF16) as lnbf2,
        nc.sbuf_tensor(f"{name}_lnbf3", [128, D], BF16) as lnbf3,
        nc.sbuf_tensor(f"{name}_lnfm0", [128, 8 * 256], BF16) as lnfm0,
        nc.sbuf_tensor(f"{name}_lnfm1", [128, 8 * 256], BF16) as lnfm1,
        nc.sbuf_tensor(f"{name}_h", [128, 32 * 256], BF16) as h_sb,
        nc.sbuf_tensor(f"{name}_out", [128, 2 * D], F32) as out_f32,
        nc.psum_tensor(f"{name}_pstA", [128, 1024], BF16) as ps_tA,
        nc.psum_tensor(f"{name}_pstB", [128, 1024], BF16) as ps_tB,
        nc.psum_tensor(f"{name}_ph0", [128, 256], F32) as ph0,
        nc.psum_tensor(f"{name}_ph1", [128, 256], F32) as ph1,
        nc.psum_tensor(f"{name}_po0", [128, 1024], F32) as po0,
        nc.psum_tensor(f"{name}_po1", [128, 1024], F32) as po1,
    ):
        p = Plan(chain={"vector": ctx["s_dve"], "scalar": ctx["s_act"], "gpsimd": ctx["s_pool"]})
        ph = [ph0, ph1]
        po = [po0, po1]
        xb = [xb0, xb1, xb2]
        lnbf = [[lnbf0, lnbf1], [lnbf2, lnbf3]]
        lnfm = [lnfm0, lnfm1]
        pst = [ps_tA, ps_tB]

        x_loaded = [0] * NSL
        out_evac = [[0, 0] for _ in range(NSL)]
        stored = [0] * NSL
        h_evac = {}
        lnfm_ready = [0] * NSL
        o_used = [[0, 0] for _ in range(NSL)]
        pst_free = [0, 0]
        ln_war_act = 0
        w_tr = {}
        w_mm1 = [0, 0]
        t_ln_tile = {}
        t_mean = {}

        def emit_load(sl):
            if sl >= NSL:
                return
            if sl >= 3:
                p.wait("sync", s_dve, out_evac[sl - 3][1])
                p.wait("sync", s_l[sl % 3], x_loaded[sl - 3])
            x_loaded[sl] = p.inc("sync", s_l[sl % 3],
                                 lambda e, sl=sl: e.dma_start(sb_3d(xb[sl % 3].ap()), x_slice_dram(x_src, sl)), 16)

        def ln_a(sl):
            for tt in range(2):
                c = 8 * tt
                p.wait("vector", s_l[sl % 3], x_loaded[sl])
                x_ap = xb[sl % 3][:, D * tt:D * (tt + 1)]
                t0 = p.inc("vector", s_dve,
                           lambda e, x_ap=x_ap, c=c: e.reduce_sum(stat[:, c:c + 1], x_ap, axis=AX.X))
                p.wait("vector", s_dve, t0)
                t_mean[(sl, tt)] = p.inc("vector", s_dve,
                                         lambda e, c=c: e.tensor_scalar(stat[:, c + 1:c + 2], stat[:, c:c + 1],
                                                                        1.0 / D, None, ALU.mult))

        def ln_b(sl):
            nonlocal ln_war_act
            b = sl % 2
            for tt in range(2):
                c = 8 * tt
                x_ap = xb[sl % 3][:, D * tt:D * (tt + 1)]
                p.wait("vector", s_act, ln_war_act)
                p.wait("vector", s_dve, t_mean[(sl, tt)])
                t_sub = p.inc("vector", s_dve,
                              lambda e, x_ap=x_ap, c=c: e.tensor_scalar(scr[:, :], x_ap, stat[:, c + 1:c + 2],
                                                                        None, ALU.subtract))
                p.wait("scalar", s_dve, t_sub)
                t_sq = p.inc("scalar", s_act,
                             lambda e, c=c: e.activation(scr2[:, :], scr[:, :], AF.Square,
                                                         accum_out=stat[:, c + 2:c + 3]))
                ln_war_act = t_sq
                p.wait("vector", s_act, t_sq)
                t2 = p.inc("vector", s_dve,
                           lambda e, c=c: e.tensor_scalar(stat[:, c + 3:c + 4], stat[:, c + 2:c + 3],
                                                          1.0 / D, 1e-5, ALU.mult, ALU.add))
                p.wait("vector", s_dve, t2)
                t_rc = p.inc("vector", s_dve,
                             lambda e, c=c: e.reciprocal(stat[:, c + 4:c + 5], stat[:, c + 3:c + 4]))
                p.wait("scalar", s_dve, t_rc)
                t_sq2 = p.inc("scalar", s_act,
                              lambda e, c=c: e.activation(stat[:, c + 5:c + 6], stat[:, c + 4:c + 5], AF.Sqrt))
                p.wait("vector", s_act, t_sq2)
                p.wait("vector", s_pe, w_tr.get((b, tt), 0))
                t_ln_tile[(sl, tt)] = p.inc("vector", s_dve,
                                            lambda e, b=b, tt=tt, c=c:
                                            e.tensor_scalar(lnbf[b][tt][:, :], scr[:, :], stat[:, c + 5:c + 6],
                                                            None, ALU.mult))

        def T_stage(sl):
            b = sl % 2
            for tt in range(2):
                p.wait("tensor", s_dve, t_ln_tile[(sl, tt)])
                p.wait("tensor", s_dve, pst_free[tt])
                for c in range(8):
                    fn = (lambda e, c=c, b=b, tt=tt: e.transpose(pst[tt][:, 128 * c:128 * (c + 1)],
                                                                 lnbf[b][tt][:, 128 * c:128 * (c + 1)], ident[:, :]))
                    if c < 7:
                        p.run("tensor", fn)
                    else:
                        w_tr[(b, tt)] = p.inc("tensor", s_pe, fn)
            for tt in range(2):
                p.wait("vector", s_pe, w_tr[(b, tt)])
                p.wait("vector", s_pe, w_mm1[b])
                dst = lnfm[b].ap().rearrange("p (c a) -> p c a", c=8, a=256)[:, :, 128 * tt:128 * (tt + 1)]
                pst_free[tt] = p.inc("vector", s_dve,
                                     lambda e, dst=dst, tt=tt:
                                     e.tensor_copy(dst, pst[tt].ap().rearrange("p (c a) -> p c a", c=8, a=128)))
            lnfm_ready[sl] = pst_free[1]

        def mm1(sl):
            b = sl % 2
            p.wait("tensor", s_dve, lnfm_ready[sl])
            if sl == 0 and w_gate is not None:
                p.wait("tensor", s_w, w_gate[0])
                p.wait("scalar", s_w, w_gate[0])
            for fc in range(32):
                if (sl, fc - 2) in h_evac:
                    p.wait("tensor", s_act, h_evac[(sl, fc - 2)])
                elif sl > 0:
                    p.wait("tensor", s_act, h_evac[(sl - 1, 30 + fc)])
                for dc in range(8):
                    fn = (lambda e, fc=fc, dc=dc, b=b:
                          e.matmul(ph[fc % 2][:, :],
                                   W1_sb[:, FF * dc + 128 * fc:FF * dc + 128 * (fc + 1)],
                                   lnfm[b][:, 256 * dc:256 * (dc + 1)],
                                   start=(dc == 0), stop=(dc == 7)))
                    if dc < 7:
                        p.run("tensor", fn)
                    else:
                        t_mm = p.inc("tensor", s_pe, fn)
                p.wait("scalar", s_pe, t_mm)
                h_evac[(sl, fc)] = p.inc("scalar", s_act,
                                         lambda e, fc=fc: e.activation(h_sb[:, 256 * fc:256 * (fc + 1)],
                                                                       ph[fc % 2][:, :], act_func,
                                                                       bias=b1_sb[:, fc:fc + 1]))
            w_mm1[b] = t_mm

        def mm2_half(sl, tt):
            p.wait("tensor", s_act, h_evac[(sl, 31)])
            if sl == 0 and tt == 0 and w_gate is not None:
                p.wait("tensor", s_w, w_gate[1])
            if sl > 0:
                p.wait("tensor", s_dve, o_used[sl - 1][tt])
            for half in range(2):
                fn0 = (lambda e, tt=tt, half=half: e.matmul(po[tt][:, 512 * half:512 * (half + 1)],
                                                            ones_bf[0:1, :],
                                                            b2row_sb[0:1, 512 * half:512 * (half + 1)],
                                                            start=True, stop=False))
                p.run("tensor", fn0)
            for fc in range(32):
                for half in range(2):
                    fn = (lambda e, fc=fc, tt=tt, half=half:
                          e.matmul(po[tt][:, 512 * half:512 * (half + 1)],
                                   h_sb[:, 256 * fc + 128 * tt:256 * fc + 128 * (tt + 1)],
                                   W2_sb[:, D * fc + 512 * half:D * fc + 512 * (half + 1)],
                                   start=False, stop=(fc == 31)))
                    if fc == 31 and half == 1:
                        t_mm2 = p.inc("tensor", s_pe, fn)
                    else:
                        p.run("tensor", fn)
            p.wait("vector", s_pe, t_mm2)
            if sl >= 1:
                p.wait("vector", s_st[(sl - 1) % 2], stored[sl - 1])
            out_evac[sl][tt] = o_used[sl][tt] = p.inc(
                "vector", s_dve,
                lambda e, tt=tt, sl=sl:
                e.tensor_tensor(out_f32[:, D * tt:D * (tt + 1)],
                                po[tt][:, :], xb[sl % 3][:, D * tt:D * (tt + 1)], ALU.add))

        def store(sl):
            p.wait("sync", s_dve, out_evac[sl][1])
            if sl >= 2:
                p.wait("sync", s_st[sl % 2], stored[sl - 2])
            stored[sl] = p.inc("sync", s_st[sl % 2],
                               lambda e, sl=sl: e.dma_start(x_slice_dram(x_dst, sl),
                                                            sb_3d(out_f32[:, :])), 16)
            if taps is not None and sl == 0:
                b = sl % 2
                p.wait("gpsimd", s_dve, out_evac[sl][1])
                p.inc("gpsimd", ctx["s_w"], lambda e, b=b: e.dma_start(taps["lnfm"][:, :], lnfm[b][:, :]), 16)
                t_tap = p.inc("gpsimd", ctx["s_w"], lambda e: e.dma_start(taps["h"][:, :], h_sb[:, :]), 16)
                p.wait("gpsimd", ctx["s_w"], t_tap)

        emit_load(0)
        emit_load(1)
        emit_load(2)
        ln_a(0)
        for i in range(NSL + 1):
            if i >= 1:
                mm1(i - 1)
            if i < NSL:
                ln_b(i)
            if i >= 1:
                mm2_half(i - 1, 0)
            if i < NSL:
                T_stage(i)
            if i >= 1:
                mm2_half(i - 1, 1)
                emit_load(i + 2)
                store(i - 1)
            if i + 1 < NSL:
                ln_a(i + 1)
        p.wait("sync", s_st[(NSL - 1) % 2], stored[NSL - 1])
        p.wait("sync", s_st[(NSL - 2) % 2], stored[NSL - 2])
        p.realize(nc)


def emit_ffn_weight_load_plain(nc, ctx, *, W1_dram, W2_dram, b1_dram, b2_dram,
                               W1_sb, W2_sb, b1_sb, b2row_sb, name):
    s_w = ctx["s_w"]
    p = Plan(chain={"vector": ctx["s_dve"], "scalar": ctx["s_act"], "gpsimd": ctx["s_pool"]})
    t = 0
    for dc in range(8):
        t = p.inc("gpsimd", s_w, lambda e, dc=dc: e.dma_start(
            W1_sb[:, FF * dc:FF * (dc + 1)], W1_dram[128 * dc:128 * (dc + 1), :]), 16)
    t_w1 = p.inc("gpsimd", s_w, lambda e: e.dma_start(b1_sb[:, :], b1_dram[:, :]), 16)
    for fc in range(32):
        t = p.inc("gpsimd", s_w, lambda e, fc=fc: e.dma_start(
            W2_sb[:, D * fc:D * (fc + 1)], W2_dram[128 * fc:128 * (fc + 1), :]), 16)
    t_all = p.inc("gpsimd", s_w, lambda e: e.dma_start(b2row_sb[:, :], b2_dram[:, :]), 16)
    p.realize(nc)
    return t_w1, t_all


def emit_ffn_weight_ternarize(nc, ctx, *, W1_dram, W2_dram, b1_dram, b2_dram,
                              W1_sb, W2_sb, b1_sb, b2row_sb, name, taps=None):
    s_x, s_w, s_pe, s_dve, s_act, s_pool = (ctx["s_x"], ctx["s_w"], ctx["s_pe"],
                                            ctx["s_dve"], ctx["s_act"], ctx["s_pool"])
    s_l = ctx["s_l"]
    ones_f32 = ctx["ones_f32"]
    onescol = ctx["onescol_f32"]

    with (
        nc.sbuf_tensor(f"{name}_m0", [128, FF], F32) as m0,
        nc.sbuf_tensor(f"{name}_m1", [128, FF], F32) as m1,
        nc.sbuf_tensor(f"{name}_pa", [128, 48], F32) as partials,
        nc.sbuf_tensor(f"{name}_sa", [128, FF], F32) as sa,
        nc.sbuf_tensor(f"{name}_sbb", [128, FF], F32) as sb_,
        nc.sbuf_tensor(f"{name}_thr", [128, 4], F32) as thr,
        nc.sbuf_tensor(f"{name}_tot", [1, 2], F32) as tot,
        nc.psum_tensor(f"{name}_ps", [128, 8], F32) as ps,
    ):
        p = Plan(chain={"vector": ctx["s_dve"], "scalar": ctx["s_act"], "gpsimd": ctx["s_pool"]})
        mb = [m0, m1]
        loads = []
        red_done = []
        nch = 8 + 32
        for ci in range(nch):
            if ci >= 2:
                p.wait("sync", s_dve, red_done[ci - 2])
                p.wait("sync", s_l[ci % 2], loads[ci - 2])
            src = (W1_dram[128 * ci:128 * (ci + 1), :] if ci < 8 else
                   W2_dram[128 * (ci - 8):128 * (ci - 7), :])
            width = FF if ci < 8 else D
            loads.append(p.inc("sync", s_l[ci % 2],
                               lambda e, src=src, ci=ci, width=width: e.dma_start(mb[ci % 2][:, 0:width], src), 16))
            p.wait("vector", s_l[ci % 2], loads[ci])
            red_done.append(p.inc("vector", s_dve,
                                  lambda e, ci=ci, width=width: e.reduce_sum(
                                      partials[:, ci:ci + 1], mb[ci % 2][:, 0:width], axis=AX.X,
                                      apply_absolute_value=True)))
        p.wait("vector", s_dve, red_done[nch - 1])
        p.run("vector", lambda e: e.reduce_sum(partials[:, 40:41], partials[:, 0:8], axis=AX.X))
        t_p = p.inc("vector", s_dve, lambda e: e.reduce_sum(partials[:, 41:42], partials[:, 8:40], axis=AX.X))
        p.wait("vector", s_dve, t_p)
        p.wait("tensor", s_dve, t_p)
        p.run("tensor", lambda e: e.matmul(ps[0:1, 0:1], partials[:, 40:41], onescol[:, 0:1],
                                           start=True, stop=True))
        t_mm = p.inc("tensor", s_pe, lambda e: e.matmul(ps[0:1, 1:2], partials[:, 41:42], onescol[:, 0:1],
                                                        start=True, stop=True))
        p.wait("scalar", s_pe, t_mm)
        p.run("scalar", lambda e: e.activation(tot[0:1, 0:1], ps[0:1, 0:1], AF.Copy,
                                               scale=float(0.5 / (D * FF))))
        t_a = p.inc("scalar", s_act, lambda e: e.activation(tot[0:1, 1:2], ps[0:1, 1:2], AF.Copy,
                                                            scale=float(0.5 / (D * FF))))
        p.wait("vector", s_act, t_a)
        t_mx = p.inc("vector", s_dve, lambda e: e.tensor_scalar(tot[0:1, 0:2], tot[0:1, 0:2], 0.5e-8, None,
                                                                ALU.max))
        p.wait("tensor", s_dve, t_mx)
        p.run("tensor", lambda e: e.matmul(ps[:, 2:3], ones_f32[0:1, :], tot[0:1, 0:1],
                                           start=True, stop=True))
        t_bc = p.inc("tensor", s_pe, lambda e: e.matmul(ps[:, 3:4], ones_f32[0:1, :], tot[0:1, 1:2],
                                                        start=True, stop=True))
        p.wait("vector", s_pe, t_bc)
        p.run("vector", lambda e: e.tensor_copy(thr[:, 0:1], ps[:, 2:3]))
        p.run("vector", lambda e: e.tensor_scalar(thr[:, 1:2], ps[:, 2:3], -1.0, None, ALU.mult))
        p.run("vector", lambda e: e.tensor_copy(thr[:, 2:3], ps[:, 3:4]))
        t_th = p.inc("vector", s_dve, lambda e: e.tensor_scalar(thr[:, 3:4], ps[:, 3:4], -1.0, None, ALU.mult))

        p.wait("sync", s_dve, t_th)
        p.wait("gpsimd", s_dve, t_th)
        tern_done_v = [0] * nch
        tern_done_p = [0] * nch
        loads2 = []
        for ci in range(nch):
            if ci >= 2:
                p.wait("sync", s_dve, tern_done_v[ci - 2])
                p.wait("sync", s_l[ci % 2], loads2[ci - 2])
            src = (W1_dram[128 * ci:128 * (ci + 1), :] if ci < 8 else
                   W2_dram[128 * (ci - 8):128 * (ci - 7), :])
            width = FF if ci < 8 else D
            dst = (W1_sb[:, FF * ci:FF * (ci + 1)] if ci < 8 else
                   W2_sb[:, D * (ci - 8):D * (ci - 7)])
            tcol = 0 if ci < 8 else 2
            loads2.append(p.inc("sync", s_l[ci % 2],
                                lambda e, src=src, ci=ci, width=width: e.dma_start(mb[ci % 2][:, 0:width], src), 16))
            p.wait("vector", s_l[ci % 2], loads2[ci])
            p.wait("gpsimd", s_l[ci % 2], loads2[ci])
            t_a2 = p.inc("vector", s_dve,
                         lambda e, ci=ci, width=width, tcol=tcol: e.tensor_scalar(
                             sa[:, 0:width], mb[ci % 2][:, 0:width], thr[:, tcol:tcol + 1], None, ALU.is_gt))
            tern_done_p[ci] = p.inc("vector", s_dve,
                                    lambda e, ci=ci, width=width, tcol=tcol: e.tensor_scalar(
                                        sb_[:, 0:width], mb[ci % 2][:, 0:width], thr[:, tcol + 1:tcol + 2],
                                        None, ALU.is_lt))
            tern_done_v[ci] = p.inc("vector", s_dve,
                                    lambda e, ci=ci, width=width, dst=dst: e.tensor_tensor(
                                        dst, sa[:, 0:width], sb_[:, 0:width], ALU.subtract))
        if taps is not None:
            p.wait("gpsimd", s_dve, t_th)
            p.inc("gpsimd", s_w, lambda e: e.dma_start(taps["thr"][:, :], thr[:, :]), 16)
            p.inc("gpsimd", s_w, lambda e: e.dma_start(taps["partials"][:, :], partials[:, :]), 16)
        t_b = p.inc("gpsimd", s_w, lambda e: e.dma_start(b1_sb[:, :], b1_dram[:, :]), 16)
        t_b = p.inc("gpsimd", s_w, lambda e: e.dma_start(b2row_sb[:, :], b2_dram[:, :]), 16)
        p.wait("gpsimd", s_w, t_b)
        p.wait("vector", s_dve, tern_done_v[nch - 1])
        p.realize(nc)


def build_nc(phases=("l0a", "l0f", "l1a", "l1f"), debug_mid=False):
    nc = bass.Bass("TRN2", target_bir_lowering=False)
    x_in = nc.dram_tensor("x", [NTOK, D], F32, kind="ExternalInput").ap()
    y_out = nc.dram_tensor("y", [NTOK, D], F32, kind="ExternalOutput").ap()
    consts = {k: nc.dram_tensor(k, sh, dt, kind="ExternalInput").ap()
              for k, (sh, dt) in CONST_SPECS.items()}
    weights = {k: nc.dram_tensor(k, sh, dt, kind="ExternalInput").ap()
               for k, (sh, dt) in WEIGHT_SPECS.items()}
    n_mid = max(len(phases) - 1, 0)
    kind_mid = "ExternalOutput" if debug_mid else "Internal"
    mids = [nc.dram_tensor(f"xmid{j}", [NTOK, D], F32, kind=kind_mid).ap()
            for j in range(n_mid)]
    chain = [x_in] + mids + [y_out]

    with (
        nc.semaphore("s_x") as _sx,
        nc.semaphore("s_w") as _sw,
        nc.semaphore("s_s2") as _ss2,
        nc.semaphore("s_pe") as _spe,
        nc.semaphore("s_dve") as _sdve,
        nc.semaphore("s_act") as _sact,
        nc.semaphore("s_pool") as _spool,
        nc.semaphore("s_l0") as _sl0,
        nc.semaphore("s_l1") as _sl1,
        nc.semaphore("s_l2") as _sl2,
        nc.semaphore("s_st0") as _sst0,
        nc.semaphore("s_st1") as _sst1,
        nc.sbuf_tensor("c_ident", [128, 128], BF16) as ident_sb,
        nc.sbuf_tensor("c_ones_bf", [1, 128], BF16) as ones_bf_sb,
        nc.sbuf_tensor("c_ones_f32", [1, 128], F32) as ones_f32_sb,
        nc.sbuf_tensor("c_onescol", [128, 1], F32) as onescol_sb,
    ):
        ctx = {
            "s_x": Ctr(_sx), "s_w": Ctr(_sw), "s_s2": Ctr(_ss2), "s_pe": Ctr(_spe),
            "s_dve": Ctr(_sdve), "s_act": Ctr(_sact), "s_pool": Ctr(_spool),
            "s_l": [Ctr(_sl0), Ctr(_sl1), Ctr(_sl2)], "s_st": [Ctr(_sst0), Ctr(_sst1)],
            "ident_bf": ident_sb, "ones_bf": ones_bf_sb, "ones_f32": ones_f32_sb,
            "onescol_f32": onescol_sb, "consts": consts, "weights": weights,
        }
        p = Plan(chain={"vector": ctx["s_dve"], "scalar": ctx["s_act"], "gpsimd": ctx["s_pool"]})
        t = p.inc("sync", ctx["s_x"], lambda e: e.dma_start(ident_sb[:, :], consts["ident_bf"][:, :]), 16)
        p.wait("sync", ctx["s_x"], t)
        t = p.inc("sync", ctx["s_x"], lambda e: e.dma_start(ones_bf_sb[:, :], consts["ones_bf"][:, :]), 16)
        p.wait("sync", ctx["s_x"], t)
        t = p.inc("sync", ctx["s_x"], lambda e: e.dma_start(ones_f32_sb[:, :], consts["ones_f32"][:, :]), 16)
        p.wait("sync", ctx["s_x"], t)
        t = p.inc("sync", ctx["s_x"], lambda e: e.dma_start(onescol_sb[:, :], consts["onescol_f32"][:, :]), 16)
        p.wait("sync", ctx["s_x"], t)
        p.realize(nc)

        for pi, ph in enumerate(phases):
            x_src, x_dst = chain[pi], chain[pi + 1]
            _scope = nc.named_scope(f"ph_{ph}")
            _scope.__enter__()
            if ph == "l0f":
                with (
                    nc.sbuf_tensor("l0f_W1t", [128, 8 * FF], BF16) as W1_sb,
                    nc.sbuf_tensor("l0f_W2t", [128, 32 * D], BF16) as W2_sb,
                    nc.sbuf_tensor("l0f_b1sb", [128, 32], F32) as b1_sb,
                    nc.sbuf_tensor("l0f_b2sb", [1, D], BF16) as b2row_sb,
                ):
                    emit_ffn_weight_ternarize(nc, ctx, W1_dram=weights["l0_W1T"],
                                              W2_dram=weights["l0_W2T"],
                                              b1_dram=weights["l0_b1c"], b2_dram=weights["l0_b2"],
                                              W1_sb=W1_sb, W2_sb=W2_sb, b1_sb=b1_sb,
                                              b2row_sb=b2row_sb, name="l0t")
                    emit_ffn_phase(nc, ctx, x_src=x_src, x_dst=x_dst,
                                   W1_sb=W1_sb, W2_sb=W2_sb, b1_sb=b1_sb, b2row_sb=b2row_sb,
                                   act_func=AF.Gelu, name="l0f")
            elif ph == "l1f":
                with (
                    nc.sbuf_tensor("l1f_W1sb", [128, 8 * FF], BF16) as W1_sb,
                    nc.sbuf_tensor("l1f_W2sb", [128, 32 * D], BF16) as W2_sb,
                    nc.sbuf_tensor("l1f_b1sb", [128, 32], F32) as b1_sb,
                    nc.sbuf_tensor("l1f_b2sb", [1, D], BF16) as b2row_sb,
                ):
                    wg = emit_ffn_weight_load_plain(nc, ctx, W1_dram=weights["l1f_W1"],
                                                    W2_dram=weights["l1f_W2"],
                                                    b1_dram=weights["l1f_b1c"], b2_dram=weights["l1f_b2"],
                                                    W1_sb=W1_sb, W2_sb=W2_sb, b1_sb=b1_sb,
                                                    b2row_sb=b2row_sb, name="l1f")
                    emit_ffn_phase(nc, ctx, x_src=x_src, x_dst=x_dst,
                                   W1_sb=W1_sb, W2_sb=W2_sb, b1_sb=b1_sb, b2row_sb=b2row_sb,
                                   act_func=AF.Relu, name="l1f", w_gate=wg)
            elif ph == "l0a":
                emit_l0_attn(nc, ctx, x_src=x_src, x_dst=x_dst)
            elif ph == "l1a":
                emit_l1_attn(nc, ctx, x_src=x_src, x_dst=x_dst)
            else:
                raise ValueError(ph)
            _scope.__exit__(None, None, None)
    return nc


def emit_l0_attn(nc, ctx, *, x_src, x_dst):
    s_x, s_w, s_s2, s_pe, s_dve, s_act, s_pool = (ctx["s_x"], ctx["s_w"], ctx["s_s2"],
                                                  ctx["s_pe"], ctx["s_dve"], ctx["s_act"],
                                                  ctx["s_pool"])
    s_l = ctx["s_l"]
    s_st = ctx["s_st"]
    w = ctx["weights"]
    consts = ctx["consts"]
    ident = ctx["ident_bf"]
    ones_f32 = ctx["ones_f32"]
    EPS = 1e-6

    es_ = ExitStack()

    def _sb(name, shape, dt, **kw):
        return es_.enter_context(nc.sbuf_tensor(name, shape, dt, **kw))

    WqT_sb = _sb("l0a_WqT", [128, 8 * 1024], BF16)
    WkvT_sb = _sb("l0a_WkvT", [128, 8 * 512], BF16)
    WpT_sb = _sb("l0a_WpT", [64, 16 * 1024], BF16)
    mgt_sb = _sb("l0a_mgt", [128, 512], BF16)
    mle_sb = _sb("l0a_mle", [128, 512], BF16)
    stat = _sb("l0a_stat", [128, 8], F32)
    xt = _sb("l0a_xt", [128, D], F32)
    xout = _sb("l0a_xout", [128, D], F32)
    q_hfm = _sb("l0a_qh", [128, 8 * 2048], BF16)
    k_fm = _sb("l0a_kf", [128, 2 * 2048], BF16)
    v_aug = _sb("l0a_va", [128, TPS * 260], BF16)
    if True:
        p = Plan(chain={"vector": ctx["s_dve"], "scalar": ctx["s_act"], "gpsimd": ctx["s_pool"]})
        t_w = p.inc("gpsimd", s_w, lambda e: e.dma_start(
            WqT_sb.ap().rearrange("p (c f) -> p c f", c=8),
            w["l0_WqT"].rearrange("(c p) f -> p c f", p=128)), 16)
        t_w = p.inc("gpsimd", s_w, lambda e: e.dma_start(
            WkvT_sb.ap().rearrange("p (c f) -> p c f", c=8),
            w["l0_WkvT"].rearrange("(c p) f -> p c f", p=128)), 16)
        t_w = p.inc("gpsimd", s_w, lambda e: e.dma_start(
            WpT_sb[:, :], w["l0_WprojTh"][:, :]), 16)
        t_w = p.inc("gpsimd", s_w, lambda e: e.dma_start(mgt_sb[:, :], consts["negstep_gt4"][:, :]), 16)
        t_w = p.inc("gpsimd", s_w, lambda e: e.dma_start(mle_sb[:, :], consts["negstep_le4"][:, :]), 16)
        p.wait("gpsimd", s_w, t_w)
        p.realize(nc)

        for sq in range(NSEQ):
            tile0 = TPS * sq
            _scA = nc.named_scope(f"l0a_A{sq}"); _scA.__enter__()
            esA = ExitStack()
            def _sbA(name, shape, dt):
                return esA.enter_context(nc.sbuf_tensor(name + str(sq), shape, dt, side="right"))
            scr = [_sbA("l0a_scrA", [128, D], F32), _sbA("l0a_scrB", [128, D], F32)]
            scr2 = [_sbA("l0a_scr2A", [128, D], F32), _sbA("l0a_scr2B", [128, D], F32)]
            ln_bf = [_sbA("l0a_lnbfA", [128, D], BF16), _sbA("l0a_lnbfB", [128, D], BF16)]
            lnfm = [_sbA("l0a_lnfmA", [128, D], BF16), _sbA("l0a_lnfmB", [128, D], BF16)]
            qsb = [_sbA("l0a_qsbA", [128, D], F32), _sbA("l0a_qsbB", [128, D], F32)]
            q_rope = [_sbA("l0a_qrA", [128, D], BF16), _sbA("l0a_qrB", [128, D], BF16)]
            k_rope = [_sbA("l0a_krA", [128, 256], BF16), _sbA("l0a_krB", [128, 256], BF16)]
            qn = [_sbA("l0a_qnA", [128, 64], F32), _sbA("l0a_qnB", [128, 64], F32)]
            xtb = [_sbA("l0a_xtA", [128, D], F32), _sbA("l0a_xtB", [128, D], F32)]
            tq = _sbA("l0a_tq", [128, D], F32)
            tq2 = _sbA("l0a_tq2", [128, D], F32)
            ct0 = _sbA("l0a_ct0", [128, D], F32)
            ct1 = _sbA("l0a_ct1", [128, D], F32)
            st0 = _sbA("l0a_st0", [128, D], F32)
            st1 = _sbA("l0a_st1", [128, D], F32)
            pst = esA.enter_context(nc.psum_tensor(f"l0a_pst{sq}", [128, 1024], BF16))
            psq = esA.enter_context(nc.psum_tensor(f"l0a_psq{sq}", [128, 1024], F32))
            pskv = esA.enter_context(nc.psum_tensor(f"l0a_pskv{sq}", [128, 512], F32))
            psT2a = esA.enter_context(nc.psum_tensor(f"l0a_pTa{sq}", [128, 128], BF16))
            psT2b = esA.enter_context(nc.psum_tensor(f"l0a_pTb{sq}", [128, 128], BF16))
            if True:
                p = Plan(chain={"vector": ctx["s_dve"], "scalar": ctx["s_act"], "gpsimd": ctx["s_pool"]})
                ctb = [ct0, ct1]
                stb = [st0, st1]
                psT2 = [psT2a, psT2b]
                t_ones = p.inc("vector", s_dve, lambda e: e.memset(
                    v_aug.ap().rearrange("p (j g c) -> p j g c", j=TPS, g=4)[:, :, :, 64:65], 1.0))
                w_sq = [0, 0]
                w_pool_scr2 = [0, 0]
                w_tr = [0, 0]
                w_qk = [0, 0]
                w_qrope = [0, 0]
                w_packq = [0, 0]
                w_packk = [0, 0]
                w_ln = [0, 0]
                pack_ev = {}
                bank_last = {0: 0, 1: 0}
                t_vev_prev = 0
                t_sclk_prev = 0
                pst_free = 0
                x_ld_prev = 0
                cs_ld = [0] * TPS
                t_krope_par = [0, 0]
                for tt in range(TPS):
                    b = tt % 2
                    gt = tile0 + tt
                    if tt >= 2:
                        p.wait("scalar", s_pool, t_krope_par[b])
                        p.wait("scalar", s_s2, cs_ld[tt - 1])
                    cs_ld[tt] = p.inc("scalar", s_s2, lambda e, tt=tt: e.dma_start(
                        ctb[tt % 2][:, :], consts["cos16"][128 * tt:128 * (tt + 1), :]), 16)
                    cs_ld[tt] = p.inc("scalar", s_s2, lambda e, tt=tt: e.dma_start(
                        stb[tt % 2][:, :], consts["sinA16"][128 * tt:128 * (tt + 1), :]), 16)
                    p.wait("sync", s_dve, w_ln[b])
                    p.wait("sync", s_l[1], x_ld_prev)
                    x_ld_prev = x_ld = p.inc("sync", s_l[1], lambda e, gt=gt, b=b: e.dma_start(
                        xtb[b][:, :], x_src[128 * gt:128 * (gt + 1), :]), 16)
                    p.wait("vector", s_l[1], x_ld)
                    p.wait("scalar", s_pool, w_pool_scr2[b])
                    t_ln, t_sq_ = emit_ln_tile(p, ctx, xtb[b][:, :], ln_bf[b][:, :],
                                               scr[b], scr2[b], stat, 1.0 / D, 1e-5,
                                               war_act=w_sq[b], war_pe=w_tr[b])
                    w_ln[b] = t_ln
                    w_sq[b] = t_sq_
                    p.wait("tensor", s_dve, t_ln)
                    p.wait("tensor", s_dve, pst_free)
                    for c in range(8):
                        fn = (lambda e, c=c, b=b: e.transpose(pst[:, 128 * c:128 * (c + 1)],
                                                              ln_bf[b][:, 128 * c:128 * (c + 1)], ident[:, :]))
                        if c < 7:
                            p.run("tensor", fn)
                        else:
                            t_tr = p.inc("tensor", s_pe, fn)
                    w_tr[b] = t_tr
                    p.wait("vector", s_pe, t_tr)
                    p.wait("vector", s_pe, w_qk[b])
                    t_lnfm = p.inc("vector", s_dve, lambda e, b=b: e.tensor_copy(lnfm[b][:, :], pst[:, :]))
                    pst_free = t_lnfm
                    p.wait("tensor", s_dve, t_lnfm)
                    p.wait("tensor", s_act, t_sclk_prev)
                    p.wait("tensor", s_dve, t_vev_prev)
                    for nh in range(2):
                        for dc in range(8):
                            p.run("tensor", lambda e, nh=nh, dc=dc, b=b: e.matmul(
                                psq[:, 512 * nh:512 * (nh + 1)],
                                lnfm[b][:, 128 * dc:128 * (dc + 1)],
                                WqT_sb[:, 1024 * dc + 512 * nh:1024 * dc + 512 * (nh + 1)],
                                start=(dc == 0), stop=(dc == 7)))
                    for dc in range(8):
                        fn = (lambda e, dc=dc, b=b: e.matmul(
                            pskv[:, :], lnfm[b][:, 128 * dc:128 * (dc + 1)],
                            WkvT_sb[:, 512 * dc:512 * (dc + 1)],
                            start=(dc == 0), stop=(dc == 7)))
                        if dc == 7:
                            t_qk = p.inc("tensor", s_pe, fn)
                        else:
                            p.run("tensor", fn)
                    w_qk[b] = t_qk
                    p.wait("scalar", s_pe, t_qk)
                    p.run("scalar", lambda e, b=b: e.activation(scr2[b][:, :], psq[:, :], AF.Square))
                    t_sqk = p.inc("scalar", s_act, lambda e, b=b: e.activation(
                        scr[b][:, 0:256], pskv[:, 0:256], AF.Square))
                    p.wait("vector", s_act, t_sqk)
                    p.run("vector", lambda e, b=b: e.reduce_sum(
                        qn[b][:, 0:16], scr2[b].ap().rearrange("p (h a) -> p h a", h=16), axis=AX.X))
                    t_r2 = p.inc("vector", s_dve, lambda e, b=b: e.reduce_sum(
                        qn[b][:, 16:20], scr[b][:, 0:256].rearrange("p (h a) -> p h a", h=4), axis=AX.X))
                    p.wait("scalar", s_dve, t_r2)
                    t_sqr = p.inc("scalar", s_act, lambda e, b=b: e.activation(
                        qn[b][:, 24:44], qn[b][:, 0:20], AF.Sqrt))
                    p.wait("vector", s_act, t_sqr)
                    t_eps = p.inc("vector", s_dve, lambda e, b=b: e.tensor_scalar(
                        qn[b][:, 24:44], qn[b][:, 24:44], EPS, None, ALU.add))
                    t_rcp = p.inc("vector", s_dve, lambda e, b=b: e.reciprocal(qn[b][:, 44:64], qn[b][:, 24:44]))
                    t_scq = p.inc("vector", s_dve, lambda e, b=b: e.tensor_scalar(
                        qn[b][:, 44:60], qn[b][:, 44:60], 0.125, None, ALU.mult))
                    p.wait("scalar", s_dve, t_scq)
                    p.wait("scalar", s_pool, w_qrope[b])
                    for h in range(16):
                        fn = (lambda e, h=h, b=b: e.activation(
                            qsb[b][:, 64 * h:64 * (h + 1)], psq[:, 64 * h:64 * (h + 1)],
                            AF.Identity, scale=qn[b][:, 44 + h:45 + h]))
                        if h == 15:
                            t_scl = p.inc("scalar", s_act, fn)
                        else:
                            p.run("scalar", fn)
                    for h in range(4):
                        fn = (lambda e, h=h, b=b: e.activation(
                            scr2[b][:, 64 * h:64 * (h + 1)], pskv[:, 64 * h:64 * (h + 1)],
                            AF.Identity, scale=qn[b][:, 60 + h:61 + h]))
                        if h == 3:
                            t_sclk = p.inc("scalar", s_act, fn)
                        else:
                            p.run("scalar", fn)
                    t_sclk_prev = t_sclk
                    p.wait("vector", s_act, t_sclk)
                    p.wait("vector", s_dve, t_ones)
                    t_vev = p.inc("vector", s_dve, lambda e, tt=tt: e.tensor_copy(
                        v_aug.ap().rearrange("p (j g c) -> p j g c", j=TPS, g=4)[:, tt, :, 0:64],
                        pskv[:, 256:512].rearrange("p (g c) -> p g c", g=4)))
                    t_vev_prev = t_vev
                    p.wait("gpsimd", s_act, t_scl)
                    p.wait("gpsimd", s_s2, cs_ld[tt])
                    p.run("gpsimd", lambda e, tt=tt, b=b: e.tensor_tensor(
                        tq[:, :], qsb[b][:, :], ctb[tt % 2][:, :], ALU.mult))
                    p.run("gpsimd", lambda e, tt=tt, b=b: e.tensor_tensor(
                        tq2.ap().rearrange("p (h a) -> p h a", h=16)[:, :, 0:32],
                        qsb[b].ap().rearrange("p (h a) -> p h a", h=16)[:, :, 32:64],
                        stb[tt % 2].ap().rearrange("p (h a) -> p h a", h=16)[:, :, 0:32], ALU.mult))
                    p.run("gpsimd", lambda e, tt=tt, b=b: e.tensor_tensor(
                        tq2.ap().rearrange("p (h a) -> p h a", h=16)[:, :, 32:64],
                        qsb[b].ap().rearrange("p (h a) -> p h a", h=16)[:, :, 0:32],
                        stb[tt % 2].ap().rearrange("p (h a) -> p h a", h=16)[:, :, 32:64], ALU.mult))
                    p.wait("gpsimd", s_pe, w_packq[b])
                    t_qrope = p.inc("gpsimd", s_pool, lambda e, b=b: e.tensor_tensor(
                        q_rope[b][:, :], tq[:, :], tq2[:, :], ALU.add))
                    w_qrope[b] = t_qrope
                    p.wait("gpsimd", s_act, t_sclk)
                    p.run("gpsimd", lambda e, tt=tt, b=b: e.tensor_tensor(
                        tq[:, 0:256], scr2[b][:, 0:256], ctb[tt % 2][:, 0:256], ALU.mult))
                    p.run("gpsimd", lambda e, tt=tt, b=b: e.tensor_tensor(
                        tq2.ap().rearrange("p (h a) -> p h a", h=16)[:, 0:4, 0:32],
                        scr2[b].ap().rearrange("p (h a) -> p h a", h=16)[:, 0:4, 32:64],
                        stb[tt % 2].ap().rearrange("p (h a) -> p h a", h=16)[:, 0:4, 0:32], ALU.mult))
                    p.run("gpsimd", lambda e, tt=tt, b=b: e.tensor_tensor(
                        tq2.ap().rearrange("p (h a) -> p h a", h=16)[:, 0:4, 32:64],
                        scr2[b].ap().rearrange("p (h a) -> p h a", h=16)[:, 0:4, 0:32],
                        stb[tt % 2].ap().rearrange("p (h a) -> p h a", h=16)[:, 0:4, 32:64], ALU.mult))
                    p.wait("gpsimd", s_pe, w_packk[b])
                    t_krope = p.inc("gpsimd", s_pool, lambda e, b=b: e.tensor_tensor(
                        k_rope[b][:, :], tq[:, 0:256], tq2[:, 0:256], ALU.add))
                    w_pool_scr2[b] = t_krope
                    t_krope_par[b] = t_krope
                    p.wait("tensor", s_pool, t_qrope)
                    pi = 0
                    for cpair in range(2):
                        for s5 in range(4):
                            h1 = 8 * cpair + s5
                            h2 = 8 * cpair + 4 + s5
                            g = 4 * cpair + s5
                            bk = pi % 2
                            p.wait("tensor", s_dve, bank_last[bk])
                            p.run("tensor", lambda e, h1=h1, bk=bk, b=b: e.transpose(
                                psT2[bk][0:64, :], q_rope[b][:, 64 * h1:64 * (h1 + 1)], ident[:, :]))
                            t_pT = p.inc("tensor", s_pe, lambda e, h2=h2, bk=bk, b=b: e.transpose(
                                psT2[bk][64:128, :], q_rope[b][:, 64 * h2:64 * (h2 + 1)], ident[:, :]))
                            p.wait("vector", s_pe, t_pT)
                            bank_last[bk] = pack_ev[(tt, pi)] = p.inc(
                                "vector", s_dve,
                                lambda e, g=g, tt=tt, bk=bk: e.tensor_copy(
                                    q_hfm[:, 2048 * g + 128 * tt:2048 * g + 128 * (tt + 1)],
                                    psT2[bk][:, :]))
                            pi += 1
                    w_packq[b] = t_pT
                    p.wait("tensor", s_pool, t_krope)
                    for cpair in range(2):
                        bk = pi % 2
                        p.wait("tensor", s_dve, bank_last[bk])
                        p.run("tensor", lambda e, cpair=cpair, bk=bk, b=b: e.transpose(
                            psT2[bk][0:64, :], k_rope[b][:, 128 * cpair:128 * cpair + 64], ident[:, :]))
                        t_pT = p.inc("tensor", s_pe, lambda e, cpair=cpair, bk=bk, b=b: e.transpose(
                            psT2[bk][64:128, :], k_rope[b][:, 128 * cpair + 64:128 * (cpair + 1)], ident[:, :]))
                        p.wait("vector", s_pe, t_pT)
                        bank_last[bk] = pack_ev[(tt, pi)] = p.inc(
                            "vector", s_dve,
                            lambda e, cpair=cpair, tt=tt, bk=bk: e.tensor_copy(
                                k_fm[:, 2048 * cpair + 128 * tt:2048 * cpair + 128 * (tt + 1)],
                                psT2[bk][:, :]))
                        pi += 1
                    w_packk[b] = t_pT
                p.wait("vector", s_dve, pack_ev[(TPS - 1, 9)])
                p.realize(nc)
            esA.close()
            _scA.__exit__(None, None, None)

            if globals().get("L0_SKIP_B"):
                continue
            _scB = nc.named_scope(f"l0a_B{sq}"); _scB.__enter__()
            esB = ExitStack()
            expT = esB.enter_context(nc.sbuf_tensor(f"l0b_exp{sq}", [128, 2048], BF16, side="right"))
            y_bf = esB.enter_context(nc.sbuf_tensor(f"l0b_y{sq}", [64, 2048], BF16, side="right"))
            lnrow = esB.enter_context(nc.sbuf_tensor(f"l0b_lnr{sq}", [1, 2048], F32, side="right"))
            rsrow = esB.enter_context(nc.sbuf_tensor(f"l0b_rsr{sq}", [1, 2048], F32, side="right"))
            psc = esB.enter_context(nc.psum_tensor(f"l0b_psc{sq}", [128, 2048], F32))
            psY = esB.enter_context(nc.psum_tensor(f"l0b_psY{sq}", [65, 2048], F32))
            if True:
                p = Plan(chain={"vector": ctx["s_dve"], "scalar": ctx["s_act"], "gpsimd": ctx["s_pool"]})
                out_evac = [0] * TPS
                stored = [0] * TPS
                psc_free_dve = 0
                psc_free_act = 0
                psY_free = 0
                exp_free_pe = 0
                xq_ld_l = [0] * (TPS + 1)
                xq_ld_l[0] = p.inc("sync", s_l[1], lambda e: e.dma_start(
                    xt[:, :], x_src[128 * tile0:128 * (tile0 + 1), :]), 16)
                for qt in range(TPS):
                    jt0 = max(0, qt - 4)
                    njt = qt - jt0 + 1
                    xq_ld = xq_ld_l[qt]
                    for ji, jt in enumerate(range(jt0, qt + 1)):
                        p.wait("tensor", s_dve, psc_free_dve)
                        p.wait("tensor", s_act, psc_free_act)
                        masked = (jt == qt) or (qt >= 4 and jt == jt0)
                        mask_sb = mgt_sb if jt == qt else mle_sb
                        for kvh in range(4):
                            rh = 64 * (kvh % 2)
                            qv_view = q_hfm.ap().rearrange("p (g t) -> p g t", g=8)[
                                rh:rh + 64, 4 * (kvh // 2):4 * (kvh // 2) + 4,
                                128 * qt:128 * (qt + 1)]
                            fn = (lambda e, kvh=kvh, rh=rh, jt=jt, qv_view=qv_view, masked=masked: e.matmul(
                                psc[:, 512 * kvh:512 * (kvh + 1)],
                                k_fm[rh:rh + 64, 2048 * (kvh // 2) + 128 * jt:2048 * (kvh // 2) + 128 * (jt + 1)],
                                qv_view, start=True, stop=not masked))
                            fn2 = (lambda e, kvh=kvh, mask_sb=mask_sb: e.matmul(
                                psc[:, 512 * kvh:512 * (kvh + 1)],
                                ident[:, :], mask_sb[:, :], start=False, stop=True))
                            last = (kvh == 3)
                            if masked:
                                p.run("tensor", fn)
                                if last:
                                    t_sc = p.inc("tensor", s_pe, fn2)
                                else:
                                    p.run("tensor", fn2)
                            else:
                                if last:
                                    t_sc = p.inc("tensor", s_pe, fn)
                                else:
                                    p.run("tensor", fn)
                        p.wait("scalar", s_pe, t_sc)
                        p.wait("scalar", s_pe, exp_free_pe)
                        t_exp = p.inc("scalar", s_act, lambda e: e.activation(
                            expT[:, :], psc[:, :], AF.Exp))
                        psc_free_act = t_exp
                        p.wait("tensor", s_act, t_exp)
                        if ji == 0:
                            p.wait("tensor", s_dve, psY_free)
                        for kvh in range(4):
                            fn = (lambda e, kvh=kvh, jt=jt, ji=ji, qt=qt: e.matmul(
                                psY[:, 512 * kvh:512 * (kvh + 1)],
                                v_aug[:, 260 * jt + 65 * kvh:260 * jt + 65 * (kvh + 1)],
                                expT[:, 512 * kvh:512 * (kvh + 1)],
                                start=(ji == 0), stop=(jt == qt)))
                            if kvh == 3:
                                t_av = p.inc("tensor", s_pe, fn)
                            else:
                                p.run("tensor", fn)
                        exp_free_pe = t_av
                    p.wait("scalar", s_pe, t_av)
                    t_ln2 = p.inc("scalar", s_act, lambda e: e.activation(
                        lnrow[0:1, :], psY[64:65, :], AF.Ln))
                    p.wait("scalar", s_act, t_ln2)
                    t_rs = p.inc("scalar", s_act, lambda e: e.activation(
                        rsrow[0:1, :], lnrow[0:1, :], AF.Exp, scale=-1.0))
                    p.wait("tensor", s_act, t_rs)
                    p.wait("tensor", s_dve, psc_free_dve)
                    p.wait("tensor", s_act, psc_free_act)
                    for kvh in range(4):
                        fn = (lambda e, kvh=kvh: e.matmul(
                            psc[0:64, 512 * kvh:512 * (kvh + 1)], ones_f32[0:1, 0:64],
                            rsrow[0:1, 512 * kvh:512 * (kvh + 1)], start=True, stop=True))
                        if kvh == 3:
                            t_bc = p.inc("tensor", s_pe, fn)
                        else:
                            p.run("tensor", fn)
                    p.wait("vector", s_pe, t_bc)
                    t_yc = p.inc("vector", s_dve, lambda e: e.tensor_copy(y_bf[:, :], psY[0:64, :]))
                    p.wait("vector", s_dve, t_yc)
                    t_yn = p.inc("vector", s_dve, lambda e: e.tensor_tensor(
                        y_bf[:, :], y_bf[:, :], psc[0:64, 0:2048], ALU.mult))
                    psY_free = t_yn
                    p.wait("tensor", s_dve, t_yn)
                    for h in range(16):
                        for hf in range(2):
                            kvh, s5 = h // 4, h % 4
                            fn = (lambda e, h=h, hf=hf, kvh=kvh, s5=s5: e.matmul(
                                psc[:, 1024 + 512 * hf:1024 + 512 * (hf + 1)],
                                y_bf[0:64, 512 * kvh + 128 * s5:512 * kvh + 128 * (s5 + 1)],
                                WpT_sb[0:64, 1024 * h + 512 * hf:1024 * h + 512 * (hf + 1)],
                                start=(h == 0), stop=(h == 15)))
                            if h == 15 and hf == 1:
                                t_pj = p.inc("tensor", s_pe, fn)
                            else:
                                p.run("tensor", fn)
                    p.wait("vector", s_pe, t_pj)
                    p.wait("vector", s_l[1], xq_ld)
                    if qt >= 1:
                        p.wait("vector", s_st[1], stored[qt - 1])
                    out_evac[qt] = psc_free_dve = p.inc("vector", s_dve, lambda e: e.tensor_tensor(
                        xout[:, :], psc[:, 1024:2048], xt[:, :], ALU.add))
                    p.wait("sync", s_dve, out_evac[qt])
                    if qt >= 1:
                        p.wait("sync", s_st[1], stored[qt - 1])
                    stored[qt] = p.inc("sync", s_st[1], lambda e, qt=qt: e.dma_start(
                        x_dst[128 * (tile0 + qt):128 * (tile0 + qt + 1), :], xout[:, :]), 16)
                    if qt + 1 < TPS:
                        p.wait("sync", s_l[1], xq_ld_l[qt])
                        xq_ld_l[qt + 1] = p.inc("sync", s_l[1], lambda e, qt=qt: e.dma_start(
                            xt[:, :], x_src[128 * (tile0 + qt + 1):128 * (tile0 + qt + 2), :]), 16)
                p.wait("sync", s_st[1], stored[TPS - 1])
                p.realize(nc)
            esB.close()
            _scB.__exit__(None, None, None)
    es_.close()


def emit_l1_attn(nc, ctx, *, x_src, x_dst):
    s_x, s_w, s_s2, s_pe, s_dve, s_act, s_pool = (ctx["s_x"], ctx["s_w"], ctx["s_s2"],
                                                  ctx["s_pe"], ctx["s_dve"], ctx["s_act"],
                                                  ctx["s_pool"])
    s_l = ctx["s_l"]
    s_st = ctx["s_st"]
    w = ctx["weights"]
    consts = ctx["consts"]
    ident = ctx["ident_bf"]
    TAU = float(KX) ** 0.5

    s2buf = nc.dram_tensor("l1a_s2buf", [2, 128, 2176], F32, kind="Internal").ap()
    rms_dram = nc.dram_tensor("l1a_rmsd", [8, 128, T], BF16, kind="Internal").ap()

    es_ = ExitStack()

    def _sb(name, shape, dt, **kw):
        return es_.enter_context(nc.sbuf_tensor(name, shape, dt, **kw))

    Wq_sb = _sb("l1a_Wq", [128, 8 * 128], BF16)
    Wk_sb = _sb("l1a_Wk", [128, 8 * 128], BF16)
    g_sb = _sb("l1a_gc", [128, 8], F32)
    xlu_sb = _sb("l1a_xlu", [KX, 1], F32)
    xlv_sb = _sb("l1a_xlv", [KX, 1], F32)
    rrel_sb = _sb("l1a_rrel", [128, T], BF16)
    negstep_sb = _sb("l1a_neg", [128, 128], F32)
    wbuf = _sb("l1a_wbuf", [128, 16 * 1024], BF16)
    v_sb = _sb("l1a_v", [128, TPS * VX], BF16)
    sg_sb = _sb("l1a_sg", [128, 16 * T], BF16)
    qu_sb = _sb("l1a_qu", [128, T], BF16)
    qv_sb = _sb("l1a_qv", [128, T], BF16)
    k_sb = _sb("l1a_k", [128, T], BF16)
    stat = _sb("l1a_stat", [128, 8], F32)
    xt = _sb("l1a_xt", [128, D], F32)
    xout = _sb("l1a_xout", [128, D], F32)
    if True:
        with (
            nc.sbuf_tensor("l1a_m0", [128, 256], F32) as m0,
            nc.sbuf_tensor("l1a_cat", [128, 2048], BF16) as cat0,
            nc.sbuf_tensor("l1a_catf", [128, 2048], F32) as catf,
            nc.sbuf_tensor("l1a_Wr", [128, 8 * 128], BF16) as Wr_sb,
            nc.sbuf_tensor("l1a_zero", [128, 128], F32) as zero_sb,
            nc.psum_tensor("l1a_psr", [128, 2048], F32) as psr,
        ):
            p = Plan(chain={"vector": ctx["s_dve"], "scalar": ctx["s_act"], "gpsimd": ctx["s_pool"]})
            t_g = p.inc("sync", s_x, lambda e: e.dma_start(g_sb[:, :], w["l1_gc"][:, :]), 16)
            p.wait("sync", s_x, t_g)
            t_xu = p.inc("sync", s_x, lambda e: e.dma_start(xlu_sb[:, :], w["l1_xlu"][:, :]), 16)
            p.wait("sync", s_x, t_xu)
            t_xv = p.inc("sync", s_x, lambda e: e.dma_start(xlv_sb[:, :], w["l1_xlv"][:, :]), 16)
            p.wait("sync", s_x, t_xv)
            t_ng = p.inc("sync", s_x, lambda e: e.dma_start(negstep_sb[:, :], consts["l1_negstep"][:, :]), 16)
            p.wait("sync", s_x, t_ng)
            t_wr = p.inc("gpsimd", s_w, lambda e: e.dma_start(
                Wr_sb.ap().rearrange("p (c k) -> p c k", c=8),
                w["l1_Wr"].rearrange("(c p) k -> p c k", p=128)), 16)
            t_k = 0
            t_m = 0
            t_m2 = 0
            for dc in range(8):
                p.wait("sync", s_dve, t_k)
                p.wait("sync", s_l[0], t_m)
                p.wait("sync", s_l[1], t_m2)
                t_m = p.inc("sync", s_l[0], lambda e, dc=dc: e.dma_start(
                    m0[:, 0:KX], w["l1_Wq"][128 * dc:128 * (dc + 1), :]), 16)
                t_m2 = p.inc("sync", s_l[1], lambda e, dc=dc: e.dma_start(
                    m0[:, KX:2 * KX], w["l1_Wk"][128 * dc:128 * (dc + 1), :]), 16)
                p.wait("vector", s_l[0], t_m)
                p.wait("vector", s_x, t_g)
                t_q = p.inc("vector", s_dve, lambda e, dc=dc: e.tensor_scalar(
                    Wq_sb[:, 128 * dc:128 * (dc + 1)], m0[:, 0:KX],
                    g_sb[:, dc:dc + 1], 1.0 / TAU, ALU.mult, ALU.mult))
                p.wait("vector", s_l[1], t_m2)
                t_k = p.inc("vector", s_dve, lambda e, dc=dc: e.tensor_scalar(
                    Wk_sb[:, 128 * dc:128 * (dc + 1)], m0[:, KX:2 * KX],
                    g_sb[:, dc:dc + 1], None, ALU.mult))
            rr_t = [0] * 8
            t_cat = 0
            t_cc = 0
            for dc in range(8):
                if dc == 0:
                    p.wait("gpsimd", s_w, t_wr)
                if dc >= 1:
                    p.wait("gpsimd", s_dve, t_cc)
                    p.wait("gpsimd", s_w, t_cat)
                t_cat = p.inc("gpsimd", s_w, lambda e, dc=dc: e.dma_start(
                    catf[:, :], consts["catrevT"][128 * dc:128 * (dc + 1), :]), 16)
                p.wait("vector", s_w, t_cat)
                if dc >= 1:
                    p.wait("vector", s_pe, rr_t[dc - 1])
                t_cc = p.inc("vector", s_dve, lambda e: e.tensor_copy(cat0[:, :], catf[:, :]))
                p.wait("tensor", s_dve, t_cc)
                p.wait("tensor", s_w, t_wr)
                for es in range(4):
                    fn = (lambda e, dc=dc, es=es: e.matmul(
                        psr[:, 512 * es:512 * (es + 1)],
                        Wr_sb[:, 128 * dc:128 * (dc + 1)],
                        cat0[:, 512 * es:512 * (es + 1)],
                        start=(dc == 0), stop=(dc == 7)))
                    if es == 3:
                        rr_t[dc] = p.inc("tensor", s_pe, fn)
                    else:
                        p.run("tensor", fn)
            p.wait("vector", s_pe, rr_t[7])
            t_rrev = p.inc("vector", s_dve, lambda e: e.tensor_copy(rrel_sb[:, :], psr[:, :]))
            t_z = p.inc("vector", s_dve, lambda e: e.memset(zero_sb[:, :], 0.0))
            p.wait("sync", s_dve, t_z)
            p.wait("sync", s_x, s_x.val)
            for b in range(2):
                t_zp = p.inc("sync", s_x, lambda e, b=b: e.dma_start(
                    s2buf[b, :, 2048:2176], zero_sb[:, 0:128]), 16)
                p.wait("sync", s_x, t_zp)
            p.wait("sync", s_dve, t_rrev)
            p.realize(nc)

        for sq in range(NSEQ):
            tile0 = TPS * sq
            _sc1 = nc.named_scope(f"l1a_1a{sq}"); _sc1.__enter__()
            with (
                nc.sbuf_tensor(f"l1a_rmst{sq}", [128, 8 * 128], BF16, side="right") as rmstile,
                nc.sbuf_tensor(f"l1a_wm{sq}", [128, 1024], F32, side="right") as wm,
                nc.sbuf_tensor(f"l1a_scr{sq}", [128, D], F32, side="right") as scr,
                nc.sbuf_tensor(f"l1a_scr2{sq}", [128, D], F32, side="right") as scr2,
                nc.sbuf_tensor(f"l1a_lnbf{sq}", [128, D], BF16, side="right") as ln_bf,
                nc.psum_tensor(f"l1a_pst{sq}", [128, 1024], BF16) as pst,
                nc.psum_tensor(f"l1a_psv{sq}", [128, 2048], F32) as psv,
                nc.psum_tensor(f"l1a_psq{sq}", [128, 128], F32) as psq,
                nc.psum_tensor(f"l1a_psk{sq}", [128, 128], F32) as psk,
            ):
                p = Plan(chain={"vector": ctx["s_dve"], "scalar": ctx["s_act"], "gpsimd": ctx["s_pool"]})
                wv_t = [0] * 16
                t_m = 0
                for hc in range(16):
                    dc, hh = hc // 2, hc % 2
                    if hc >= 1:
                        p.wait("sync", s_dve, wv_t[hc - 1])
                        p.wait("sync", s_l[0], t_m)
                    t_m = p.inc("sync", s_l[0], lambda e, dc=dc, hh=hh: e.dma_start(
                        wm[:, :], w["l1_Wv"][128 * dc:128 * (dc + 1), 1024 * hh:1024 * (hh + 1)]), 16)
                    p.wait("vector", s_l[0], t_m)
                    wv_t[hc] = p.inc("vector", s_dve, lambda e, dc=dc, hh=hh: e.tensor_scalar(
                        wbuf[:, 2048 * dc + 1024 * hh:2048 * dc + 1024 * (hh + 1)], wm[:, :],
                        g_sb[:, dc:dc + 1], None, ALU.mult))
                ln_war_act = 0
                ln_war_pe = 0
                pst_free = 0
                rms_st = [0] * TPS
                v_done = [0] * TPS
                quk_done = [0] * TPS
                t_ln_prev = 0
                x_ld = 0
                for tt in range(TPS):
                    p.wait("sync", s_dve, t_ln_prev)
                    p.wait("sync", s_l[1], x_ld)
                    x_ld = p.inc("sync", s_l[1], lambda e, tt=tt: e.dma_start(
                        xt[:, :], x_src[128 * (tile0 + tt):128 * (tile0 + tt + 1), :]), 16)
                    p.wait("vector", s_l[1], x_ld)
                    t_ln, t_sq_ = emit_ln_tile(p, ctx, xt[:, :], ln_bf[:, :],
                                               scr, scr2, stat,
                                               (1.0 + 1e-6) / D, 1e-11,
                                               war_act=ln_war_act, war_pe=ln_war_pe)
                    ln_war_act = t_sq_
                    t_ln_prev = t_ln
                    p.wait("tensor", s_dve, t_ln)
                    p.wait("tensor", s_dve, pst_free)
                    for c in range(8):
                        fn = (lambda e, c=c: e.transpose(pst[:, 128 * c:128 * (c + 1)],
                                                         ln_bf[:, 128 * c:128 * (c + 1)], ident[:, :]))
                        if c < 7:
                            p.run("tensor", fn)
                        else:
                            t_tr = p.inc("tensor", s_pe, fn)
                    ln_war_pe = t_tr
                    p.wait("vector", s_pe, t_tr)
                    if tt >= 1:
                        p.wait("vector", s_st[0], rms_st[tt - 1])
                    t_ev = p.inc("vector", s_dve, lambda e: e.tensor_copy(
                        rmstile.ap().rearrange("p (c t) -> p c t", c=8),
                        pst.ap().rearrange("p (c a) -> p c a", c=8)))
                    pst_free = t_ev
                    p.wait("sync", s_dve, t_ev)
                    if tt >= 1:
                        p.wait("sync", s_st[0], rms_st[tt - 1])
                    rms_st[tt] = p.inc("sync", s_st[0], lambda e, tt=tt: e.dma_start(
                        rms_dram.rearrange("c p t -> p c t")[:, :, 128 * tt:128 * (tt + 1)],
                        rmstile.ap().rearrange("p (c t) -> p c t", c=8)), 16)
                    p.wait("tensor", s_dve, t_ev)
                    if tt == 0:
                        p.wait("tensor", s_dve, wv_t[15])
                    if tt >= 1:
                        p.wait("tensor", s_act, v_done[tt - 1])
                        p.wait("tensor", s_act, quk_done[tt - 1])
                    for ns in range(4):
                        for dc in range(8):
                            fn = (lambda e, ns=ns, dc=dc: e.matmul(
                                psv[:, 512 * ns:512 * (ns + 1)],
                                rmstile[:, 128 * dc:128 * (dc + 1)],
                                wbuf[:, 2048 * dc + 512 * ns:2048 * dc + 512 * (ns + 1)],
                                start=(dc == 0), stop=(dc == 7)))
                            if ns == 3 and dc == 7:
                                t_v = p.inc("tensor", s_pe, fn)
                            else:
                                p.run("tensor", fn)
                    for dc in range(8):
                        p.run("tensor", lambda e, dc=dc: e.matmul(
                            psq[:, :], Wq_sb[:, 128 * dc:128 * (dc + 1)],
                            rmstile[:, 128 * dc:128 * (dc + 1)],
                            start=(dc == 0), stop=(dc == 7)))
                    for dc in range(8):
                        fn = (lambda e, dc=dc: e.matmul(
                            psk[:, :], Wk_sb[:, 128 * dc:128 * (dc + 1)],
                            rmstile[:, 128 * dc:128 * (dc + 1)],
                            start=(dc == 0), stop=(dc == 7)))
                        if dc == 7:
                            t_qk = p.inc("tensor", s_pe, fn)
                        else:
                            p.run("tensor", fn)
                    p.wait("scalar", s_pe, t_v)
                    v_done[tt] = p.inc("scalar", s_act, lambda e, tt=tt: e.activation(
                        v_sb[:, VX * tt:VX * (tt + 1)], psv[:, :], AF.Copy))
                    p.wait("scalar", s_pe, t_qk)
                    p.run("scalar", lambda e, tt=tt: e.activation(
                        qu_sb[:, 128 * tt:128 * (tt + 1)], psq[:, :], AF.Identity,
                        bias=xlu_sb[:, 0:1]))
                    p.run("scalar", lambda e, tt=tt: e.activation(
                        qv_sb[:, 128 * tt:128 * (tt + 1)], psq[:, :], AF.Identity,
                        bias=xlv_sb[:, 0:1]))
                    quk_done[tt] = p.inc("scalar", s_act, lambda e, tt=tt: e.activation(
                        k_sb[:, 128 * tt:128 * (tt + 1)], psk[:, :], AF.Copy))
                p.wait("scalar", s_act, quk_done[TPS - 1])
                p.realize(nc)
            rms_st_last = rms_st[TPS - 1]

            _sc1.__exit__(None, None, None)
            _sc2 = nc.named_scope(f"l1a_1b{sq}"); _sc2.__enter__()
            with (
                nc.sbuf_tensor(f"l1a_rsl0{sq}", [128, 8 * 512], BF16, side="right") as rs0,
                nc.sbuf_tensor(f"l1a_wm2{sq}", [128, 1024], F32, side="right") as wm2,
                nc.psum_tensor(f"l1a_psg{sq}", [128, 1024], F32) as psg,
            ):
                p = Plan(chain={"vector": ctx["s_dve"], "scalar": ctx["s_act"], "gpsimd": ctx["s_pool"]})
                rsl = [rs0, rs0]
                wg_t = [0] * 16
                t_m = 0
                for hc in range(16):
                    dc, hh = hc // 2, hc % 2
                    if hc >= 1:
                        p.wait("sync", s_dve, wg_t[hc - 1])
                        p.wait("sync", s_l[0], t_m)
                    t_m = p.inc("sync", s_l[0], lambda e, dc=dc, hh=hh: e.dma_start(
                        wm2[:, :], w["l1_Wg"][128 * dc:128 * (dc + 1), 1024 * hh:1024 * (hh + 1)]), 16)
                    p.wait("vector", s_l[0], t_m)
                    wg_t[hc] = p.inc("vector", s_dve, lambda e, dc=dc, hh=hh: e.tensor_scalar(
                        wbuf[:, 2048 * dc + 1024 * hh:2048 * dc + 1024 * (hh + 1)], wm2[:, :],
                        g_sb[:, dc:dc + 1], None, ALU.mult))
                sg_evac = {}
                rs_ld = [0] * 4
                p.wait("sync", s_st[0], rms_st_last)
                for ts4 in range(4):
                    if ts4 >= 1:
                        p.wait("sync", s_pe, t_sg)
                        p.wait("sync", s_l[1], rs_ld[ts4 - 1])
                    rs_ld[ts4] = p.inc("sync", s_l[1], lambda e, ts4=ts4: e.dma_start(
                        rsl[ts4 % 2].ap().rearrange("p (c t) -> p c t", c=8),
                        rms_dram.rearrange("c p t -> p c t")[:, :, 512 * ts4:512 * (ts4 + 1)]), 16)
                    for vc in range(16):
                        gi = ts4 * 16 + vc
                        bsel = gi % 2
                        if gi >= 2:
                            p.wait("tensor", s_act, sg_evac[gi - 2])
                        p.wait("tensor", s_dve, wg_t[15])
                        p.wait("tensor", s_l[1], rs_ld[ts4])
                        for dc in range(8):
                            fn = (lambda e, vc=vc, dc=dc, bsel=bsel: e.matmul(
                                psg[:, 512 * bsel:512 * (bsel + 1)],
                                wbuf[:, 2048 * dc + 128 * vc:2048 * dc + 128 * (vc + 1)],
                                rsl[0][:, 512 * dc:512 * (dc + 1)] if True else None,
                                start=(dc == 0), stop=(dc == 7)))
                            fn = (lambda e, vc=vc, dc=dc, bsel=bsel, ts4=ts4: e.matmul(
                                psg[:, 512 * bsel:512 * (bsel + 1)],
                                wbuf[:, 2048 * dc + 128 * vc:2048 * dc + 128 * (vc + 1)],
                                rsl[ts4 % 2][:, 512 * dc:512 * (dc + 1)],
                                start=(dc == 0), stop=(dc == 7)))
                            if dc == 7:
                                t_sg = p.inc("tensor", s_pe, fn)
                            else:
                                p.run("tensor", fn)
                        p.wait("scalar", s_pe, t_sg)
                        sg_evac[gi] = p.inc("scalar", s_act, lambda e, vc=vc, ts4=ts4, bsel=bsel: e.activation(
                            sg_sb[:, T * vc + 512 * ts4:T * vc + 512 * (ts4 + 1)],
                            psg[:, 512 * bsel:512 * (bsel + 1)], AF.Silu))
                p.wait("scalar", s_act, sg_evac[63])
                p.realize(nc)

            _sc2.__exit__(None, None, None)
            p = Plan(chain={"vector": ctx["s_dve"], "scalar": ctx["s_act"], "gpsimd": ctx["s_pool"]})
            wo_t = 0
            for vc in range(16):
                wo_t = p.inc("gpsimd", s_w, lambda e, vc=vc: e.dma_start(
                    wbuf[:, 1024 * vc:1024 * (vc + 1)], w["l1_Wo"][128 * vc:128 * (vc + 1), :]), 16)
            p.wait("gpsimd", s_w, wo_t)
            p.realize(nc)

            _sc3 = nc.named_scope(f"l1a_2{sq}"); _sc3.__enter__()
            with (
                nc.sbuf_tensor(f"l1a_s2w{sq}", [128, 512], F32, side="right") as s2w_sb,
                nc.sbuf_tensor(f"l1a_sc{sq}", [128, 2048], F32, side="right") as scores_sb,
                nc.sbuf_tensor(f"l1a_at{sq}", [128, 2048], BF16, side="right") as attn_sb,
                nc.sbuf_tensor(f"l1a_atT{sq}", [128, 16 * 128], BF16, side="right") as attnT_sb,
                nc.psum_tensor(f"l1a_psA{sq}", [128, 1024], F32) as psA,
                nc.psum_tensor(f"l1a_psW{sq}", [128, 2048], F32) as psW,
                nc.psum_tensor(f"l1a_psT0{sq}", [128, 512], BF16) as psT0,
                nc.psum_tensor(f"l1a_psT1{sq}", [128, 512], BF16) as psT1,
            ):
                wvg_sb = attn_sb
                p = Plan(chain={"vector": ctx["s_dve"], "scalar": ctx["s_act"], "gpsimd": ctx["s_pool"]})
                s2_read_done = [0, 0]
                out_evac = [0] * TPS
                stored = [0] * TPS
                psA_free_dve = 0
                psA_free_act = 0
                psW_free_dve = 0
                exp_prev = 0
                bd_prev = 0
                s2w_prev = 0
                t_lastT = 0
                xq_ld_l = [0] * (TPS + 1)
                xq_ld_l[0] = p.inc("sync", s_l[1], lambda e: e.dma_start(
                    xt[:, :], x_src[128 * tile0:128 * (tile0 + 1), :]), 16)
                for qt in range(TPS):
                    W = 128 * (qt + 1)
                    nhalf = (W + 1023) // 1024
                    bsel = qt % 2
                    xq_ld = xq_ld_l[qt]
                    t_s2w = 0
                    if qt >= 2:
                        p.wait("scalar", s_s2, s2_read_done[bsel])
                    for s5 in range(qt + 1):
                        pass
                    nsl5 = (W + 511) // 512
                    for s5 in range(nsl5):
                        e0 = 2048 - W + 512 * s5
                        sw = min(512, W - 512 * s5)
                        p.wait("tensor", s_dve, psA_free_dve)
                        p.wait("tensor", s_act, psA_free_act)
                        t_s2 = p.inc("tensor", s_pe, lambda e, qt=qt, e0=e0, sw=sw: e.matmul(
                            psA[:, 0:sw],
                            qv_sb[:, 128 * qt:128 * (qt + 1)],
                            rrel_sb[:, e0:e0 + sw],
                            start=True, stop=True))
                        p.wait("scalar", s_pe, t_s2)
                        p.wait("scalar", s_s2, s2w_prev)
                        t_s2e = p.inc("scalar", s_act, lambda e, sw=sw: e.activation(
                            s2w_sb[:, 0:sw], psA[:, 0:sw], AF.Copy))
                        psA_free_act = t_s2e
                        p.wait("scalar", s_act, t_s2e)
                        t_s2w = s2w_prev = p.inc("scalar", s_s2, lambda e, bsel=bsel, e0=e0, sw=sw: e.dma_start(
                            s2buf[bsel, :, e0:e0 + sw], s2w_sb[:, 0:sw]), 16)
                        
                    p.wait("scalar", s_s2, t_s2w)
                    p.wait("scalar", s_dve, bd_prev)
                    p.wait("scalar", s_act, exp_prev)
                    rd_ap = bass.AP(s2buf.tensor, bsel * 128 * 2176 + 2047 - 128 * qt,
                                    [[2175, 128], [1, W]])
                    t_s2r = p.inc("scalar", s_s2, lambda e, rd_ap=rd_ap, W=W: e.dma_start(
                        scores_sb[:, 0:W], rd_ap), 16)
                    s2_read_done[bsel] = t_s2r
                    for hf in range(nhalf):
                        c0 = 1024 * hf
                        cw = min(1024, W - c0)
                        ns = (cw + 511) // 512
                        p.wait("tensor", s_dve, psA_free_dve)
                        p.wait("tensor", s_act, psA_free_act)
                        for s5 in range(ns):
                            sw = min(512, cw - 512 * s5)
                            fn = (lambda e, qt=qt, c0=c0, s5=s5, sw=sw: e.matmul(
                                psA[:, 512 * s5:512 * s5 + sw],
                                qu_sb[:, 128 * qt:128 * (qt + 1)],
                                k_sb[:, c0 + 512 * s5:c0 + 512 * s5 + sw],
                                start=True, stop=True))
                            if s5 == ns - 1:
                                t_ac = p.inc("tensor", s_pe, fn)
                            else:
                                p.run("tensor", fn)
                        p.wait("vector", s_pe, t_ac)
                        p.wait("vector", s_s2, t_s2r)
                        psA_free_dve = p.inc("vector", s_dve, lambda e, c0=c0, cw=cw: e.tensor_tensor(
                            scores_sb[:, c0:c0 + cw], psA[:, 0:cw], scores_sb[:, c0:c0 + cw], ALU.add))
                    taps = ctx.get("l1taps")
                    t_msk = p.inc("vector", s_dve, lambda e, W=W: e.tensor_tensor(
                        scores_sb[:, W - 128:W], scores_sb[:, W - 128:W], negstep_sb[:, :], ALU.add))
                    p.wait("vector", s_dve, t_msk)
                    t_nm = p.inc("vector", s_dve, lambda e, W=W: e.reduce_max(
                        stat[:, 0:1], scores_sb[:, 0:W], axis=AX.X, negate=True))
                    bd_prev = t_nm
                    p.wait("scalar", s_dve, t_nm)
                    p.wait("scalar", s_pe, t_lastT)
                    exp_prev = t_exp = p.inc("scalar", s_act, lambda e, W=W: e.activation(
                        attn_sb[:, 0:W], scores_sb[:, 0:W], AF.Exp, bias=stat[:, 0:1],
                        accum_out=stat[:, 1:2]))
                    p.wait("vector", s_act, t_exp)
                    t_rs = p.inc("vector", s_dve, lambda e: e.reciprocal(stat[:, 2:3], stat[:, 1:2]))
                    p.wait("vector", s_dve, t_rs)
                    t_an = p.inc("vector", s_dve, lambda e, W=W: e.tensor_scalar(
                        attn_sb[:, 0:W], attn_sb[:, 0:W], stat[:, 2:3], None, ALU.mult))

                    p.wait("tensor", s_dve, t_an)
                    p.wait("tensor", s_dve, psW_free_dve)
                    t_wv = 0
                    psTb = [psT0, psT1]
                    evac_all = {}
                    for j in range(qt + 1):
                        bk = j % 2
                        sl4 = (j // 2) % 4
                        if j >= 2:
                            p.wait("tensor", s_dve, evac_all[j - 2])
                        fn = (lambda e, j=j, bk=bk, sl4=sl4: e.transpose(
                            psTb[bk][:, 128 * sl4:128 * (sl4 + 1)],
                            attn_sb[:, 128 * j:128 * (j + 1)], ident[:, :]))
                        t_lastT = p.inc("tensor", s_pe, fn)
                        p.wait("vector", s_pe, t_lastT)
                        evac_all[j] = p.inc("vector", s_dve,
                            lambda e, j=j, bk=bk, sl4=sl4: e.tensor_copy(
                                attnT_sb[:, 128 * j:128 * (j + 1)],
                                psTb[bk][:, 128 * sl4:128 * (sl4 + 1)]))
                    p.wait("tensor", s_dve, evac_all[qt])
                    for vc in range(16):
                        for j in range(qt + 1):
                            fn = (lambda e, j=j, vc=vc, qt=qt: e.matmul(
                                psW[:, 128 * vc:128 * (vc + 1)],
                                v_sb[:, VX * j + 128 * vc:128 * (vc + 1) + VX * j],
                                attnT_sb[:, 128 * j:128 * (j + 1)],
                                start=(j == 0), stop=(j == qt)))
                            if vc == 15 and j == qt:
                                t_wv = p.inc("tensor", s_pe, fn)
                            else:
                                p.run("tensor", fn)
                    p.wait("vector", s_pe, t_wv)
                    for sgrp in range(4):
                        in1 = sg_sb.ap().rearrange("p (c t) -> p c t", c=16)[:, 4 * sgrp:4 * (sgrp + 1),
                                                                            128 * qt:128 * (qt + 1)]
                        in0 = psW.ap().rearrange("p (c t) -> p c t", c=16)[:, 4 * sgrp:4 * (sgrp + 1), :]
                        out = wvg_sb.ap().rearrange("p (c t) -> p c t", c=16)[:, 4 * sgrp:4 * (sgrp + 1), :]
                        fn = (lambda e, in0=in0, in1=in1, out=out: e.tensor_tensor(out, in0, in1, ALU.mult))
                        if sgrp == 3:
                            t_wvg = p.inc("vector", s_dve, fn)
                        else:
                            p.run("vector", fn)
                    if taps is not None and sq == 0 and qt == 1:
                        p.wait("gpsimd", s_dve, t_wvg)
                        tq = p.inc("gpsimd", s_w, lambda e: e.dma_start(taps["wvg"][:, :], wvg_sb[:, 0:2048]), 16)
                        tq = p.inc("gpsimd", s_w, lambda e: e.dma_start(taps["v1"][:, :], v_sb[:, VX:VX + 2048]), 16)
                        p.wait("gpsimd", s_w, tq)
                    p.wait("tensor", s_dve, t_wvg)
                    for vc in range(16):
                        for hfp in range(2):
                            fn = (lambda e, vc=vc, hfp=hfp: e.matmul(
                                psW[:, 512 * hfp:512 * (hfp + 1)],
                                wvg_sb[:, 128 * vc:128 * (vc + 1)],
                                wbuf[:, 1024 * vc + 512 * hfp:1024 * vc + 512 * (hfp + 1)],
                                start=(vc == 0), stop=(vc == 15)))
                            if vc == 15 and hfp == 1:
                                t_pj = p.inc("tensor", s_pe, fn)
                            else:
                                p.run("tensor", fn)
                    p.wait("vector", s_pe, t_pj)
                    p.wait("vector", s_l[1], xq_ld)
                    if qt >= 1:
                        p.wait("vector", s_st[1], stored[qt - 1])
                    out_evac[qt] = psW_free_dve = p.inc("vector", s_dve,
                                                        lambda e: e.tensor_tensor(
                                                            xout[:, :], psW[:, 0:1024], xt[:, :], ALU.add))
                    p.wait("sync", s_dve, out_evac[qt])
                    if qt >= 1:
                        p.wait("sync", s_st[1], stored[qt - 1])
                    stored[qt] = p.inc("sync", s_st[1], lambda e, qt=qt: e.dma_start(
                        x_dst[128 * (tile0 + qt):128 * (tile0 + qt + 1), :], xout[:, :]), 16)
                    if qt + 1 < TPS:
                        p.wait("sync", s_l[1], xq_ld_l[qt])
                        xq_ld_l[qt + 1] = p.inc("sync", s_l[1], lambda e, qt=qt: e.dma_start(
                            xt[:, :], x_src[128 * (tile0 + qt + 1):128 * (tile0 + qt + 2), :]), 16)
                p.wait("sync", s_st[1], stored[TPS - 1])
                p.realize(nc)
            _sc3.__exit__(None, None, None)
    es_.close()


def kernel(**inputs):
    consts = host_constants()
    w = shard_weights(inputs)
    x = np.asarray(inputs["L0_attn_x"], dtype=np.float32)
    nc = build_nc()
    in_maps = []
    for core in range(NCORES):
        m = {"x": np.ascontiguousarray(x[NSEQ * core:NSEQ * (core + 1)].reshape(NTOK, D))}
        m.update(consts)
        m.update(w)
        in_maps.append(m)
    res = run_bass_kernel_spmd(nc, in_maps, core_ids=list(range(NCORES)))
    out = np.concatenate([r["y"].reshape(NSEQ, T, D) for r in res.results], axis=0)
    return out.astype(np.float32)
